# version 04
# baseline (speedup 1.0000x reference)
import math
from contextlib import ExitStack

import numpy as np

import concourse.bass as bass
import concourse.mybir as mybir
from concourse.bass_utils import run_bass_kernel_spmd

F32 = mybir.dt.float32
BF16 = mybir.dt.bfloat16
I32 = mybir.dt.int32
AF = mybir.ActivationFunctionType
ALU = mybir.AluOpType
AX = mybir.AxisListType

B, S, D, H, HD = 4, 4096, 2048, 16, 128
NI = 5632
NIC = NI // 128
NDC = D // 128
EPS = 1e-5
TSC = 1.0 / (30.0 * math.sqrt(HD))

ENGS = ("sync", "gpsimd", "scalar", "vector", "tensor")


class Sem:
    __slots__ = ("h", "n")

    def __init__(self, h):
        self.h = h
        self.n = 0


class Ph:

    def __init__(self, nc):
        self.nc = nc
        self.st = {e: [] for e in ENGS}

    def wait(self, eng, sem, val=None):
        v = sem.n if val is None else val
        if v and v > 0:
            self.st[eng].append(lambda e, h=sem.h, vv=v: e.wait_ge(h, vv))

    def do(self, eng, build, sem=None, k=1):
        if sem is None:
            self.st[eng].append(lambda e, b=build: b(e))
            return None
        sem.n += k
        self.st[eng].append(
            lambda e, b=build, h=sem.h, kk=k: b(e).then_inc(h, kk))
        return sem.n

    def barrier(self, sB):
        for e in ENGS:
            self.do(e, lambda en: en.nop(), sB, 1)
        tgt = sB.n
        for e in ENGS:
            self.wait(e, sB, tgt)

    def run(self):
        with self.nc.Block() as blk:
            for e in ENGS:
                fns = self.st[e]
                if fns:
                    getattr(blk, e)(lambda eng, fns=fns: [f(eng) for f in fns])


def build_nc(dump=None, nphase=99):
    nc = bass.Bass("TRN2", target_bir_lowering=False)

    def kind_of(name):
        return "ExternalOutput" if dump == name else "Internal"

    x_h = nc.dram_tensor("x", [S, D], F32, kind="ExternalInput")
    wqkv_h = nc.dram_tensor("wqkvT", [NDC, 128, 3072], F32, kind="ExternalInput")
    bqkv_h = nc.dram_tensor("bqkv", [128, 24], F32, kind="ExternalInput")
    wo_h = nc.dram_tensor("woT", [NDC, 128, D], F32, kind="ExternalInput")
    bo_h = nc.dram_tensor("bo", [128, NDC], F32, kind="ExternalInput")
    gate_h = nc.dram_tensor("gateT", [NIC, NDC, 128, 128], F32, kind="ExternalInput")
    up_h = nc.dram_tensor("upT", [NIC, NDC, 128, 128], F32, kind="ExternalInput")
    down_h = nc.dram_tensor("downT", [NIC, 128, D], F32, kind="ExternalInput")
    gs_h = nc.dram_tensor("gsc", [128, NIC], F32, kind="ExternalInput")
    us_h = nc.dram_tensor("usc", [128, NIC], F32, kind="ExternalInput")
    ds_h = nc.dram_tensor("dsc", [128, NDC], F32, kind="ExternalInput")
    sl_h = nc.dram_tensor("slopes2", [2, 8], F32, kind="ExternalInput")
    out_h = nc.dram_tensor("out", [2048, D], F32, kind="ExternalOutput")

    xn_s = nc.dram_tensor("xn_s", [NDC, 128, S], BF16, kind=kind_of("xn_s"))
    q_s = nc.dram_tensor("q_s", [8, 128, S], BF16, kind=kind_of("q_s"))
    k_s = nc.dram_tensor("k_s", [8, 128, S], BF16, kind=kind_of("k_s"))
    v_s = nc.dram_tensor("v_s", [8, S, 128], BF16, kind=kind_of("v_s"))
    hdn_s = nc.dram_tensor("hdn_s", [NIC, 128, 2048], BF16, kind=kind_of("hdn_s"))
    f_s = nc.dram_tensor("f_s", [NDC, 128, 2048], F32, kind=kind_of("f_s"))
    ctxm = nc.dram_tensor("ctxm", [1024, S], BF16, kind=kind_of("ctxm"))
    relr_d = nc.dram_tensor("relr_d", [4, 512], F32, kind="Internal")
    rell_d = nc.dram_tensor("rell_d", [4, 33, 128], F32, kind="Internal")
    ctxg = nc.dram_tensor("ctxg", [2048, S], BF16, kind=kind_of("ctxg"))

    ctx = ExitStack()
    with ctx:
        def sb(name, shape, dtype):
            return ctx.enter_context(nc.sbuf_tensor(name, shape, dtype))

        def sem(name):
            return Sem(ctx.enter_context(nc.semaphore(name)))

        sQ = sem("sQ")
        sG = sem("sG")
        sC = sem("sC")
        sPE = sem("sPE")
        sSC = sem("sSC")
        sVE = sem("sVE")
        sGM = sem("sGM")
        sB = sem("sB")

        bqkv_sb = sb("bqkv_sb", [128, 24], F32)
        gs_sb = sb("gs_sb", [128, NIC], F32)
        us_sb = sb("us_sb", [128, NIC], F32)
        ds_sb = sb("ds_sb", [128, NDC], F32)
        bo_sb = sb("bo_sb", [128, NDC], F32)
        sl_sb = sb("sl_sb", [2, 8], F32)
        eps_sb = sb("eps_sb", [128, 1], F32)
        ones_bf = sb("ones_bf", [128, 128], BF16)
        id_bf = sb("id_bf", [128, 128], BF16)
        id_f32 = sb("id_f32", [128, 128], F32)
        mask_bf = sb("mask_bf", [128, 4, 512], BF16)

        own_reg = ctx.enter_context(nc.gpsimd.register("own_off_r"))

        p0 = Ph(nc)
        voff = {}

        def setup_reg(g):
            pid = g.partition_id()
            g.reg_mov(own_reg, pid)
            g.reg_mod(own_reg, own_reg, 2)
            g.reg_mul(own_reg, own_reg, 2048)
            voff["v"] = g.snap(own_reg, min_val=0, max_val=2048)
            return g.nop()
        p0.do("gpsimd", setup_reg)

        for h_, t_ in [(bqkv_h, bqkv_sb), (gs_h, gs_sb), (us_h, us_sb),
                       (ds_h, ds_sb), (bo_h, bo_sb), (sl_h, sl_sb)]:
            p0.do("sync", lambda e, i=h_, o=t_: e.dma_start(out=o[...], in_=i[...]),
                  sQ, 16)
        const_ld = sQ.n

        p0.do("vector", lambda e: e.memset(eps_sb[:, :], EPS), sVE, 1)
        eps_done = sVE.n

        p0.do("gpsimd", lambda e: e.memset(ones_bf[:, :], 1.0))
        p0.do("gpsimd", lambda e: e.memset(id_bf[:, :], 0.0))
        p0.do("gpsimd", lambda e: e.affine_select(
            out=id_bf[:, :], in_=id_bf[:, :], compare_op=ALU.not_equal,
            fill=1.0, base=0, pattern=[[-1, 128]], channel_multiplier=1))
        p0.do("gpsimd", lambda e: e.memset(id_f32[:, :], 0.0))
        p0.do("gpsimd", lambda e: e.affine_select(
            out=id_f32[:, :], in_=id_f32[:, :], compare_op=ALU.not_equal,
            fill=1.0, base=0, pattern=[[-1, 128]], channel_multiplier=1))
        p0.do("gpsimd", lambda e: e.memset(mask_bf[:, :, :], 1.0))
        p0.do("gpsimd", lambda e: e.affine_select(
            out=mask_bf[:, :, :], in_=mask_bf[:, :, :], compare_op=ALU.is_ge,
            fill=0.0, base=0, pattern=[[-128, 4], [1, 512]],
            channel_multiplier=-1), sGM, 1)
        consts_done = sGM.n
        p0.barrier(sB)
        p0.run()

        if nphase >= 1:
            with ExitStack() as pc:
                def psb(name, shape, dtype):
                    return pc.enter_context(nc.sbuf_tensor(name, shape, dtype))

                def pps(name, shape, dtype):
                    return pc.enter_context(nc.psum_tensor(name, shape, dtype))

                x_sb = [psb(f"x_sb{i}", [128, 4, D], F32) for i in range(2)]
                xn_tm = [psb(f"xn_tm{i}", [128, 4, D], BF16) for i in range(2)]
                xn_fm = [psb(f"xn_fm{i}", [128, NDC, 512], BF16) for i in range(2)]
                xsq = psb("xsq", [128, 4, D], F32)
                st_sb = psb("st_sb", [128, 8, 4], F32)
                psTR = [pps(f"psTR{i}", [128, 1024], BF16) for i in range(2)]

                sX = [sem("sX0"), sem("sX1")]
                sFM = [sem("sFM0"), sem("sFM1")]
                pa = Ph(nc)
                pa.wait("scalar", sVE, eps_done)
                pa.do("scalar", lambda e: e.preload_act_table(AF.Sqrt))
                xfree = {}
                fmfree = {}
                trfree = {}
                ld_val = {}
                for tt in range(8):
                    xb = tt % 2
                    if xb in xfree:
                        pa.wait("sync", *xfree[xb])
                    xin = x_h[tt * 512:(tt + 1) * 512, :].rearrange(
                        "(p t) d -> t p d", t=128)
                    ld_val[tt] = (sX[xb], pa.do(
                        "sync", lambda e, o=x_sb[xb], i=xin:
                        e.dma_start(out=o[:, :, :], in_=i), sX[xb], 16))
                    su = st_sb[:, 0, :]
                    ssq = st_sb[:, 1, :]
                    tm = st_sb[:, 2, :]
                    vu = st_sb[:, 3, :]
                    sd = st_sb[:, 4, :]
                    rstd = st_sb[:, 5, :]
                    nmr = st_sb[:, 6, :]
                    pa.wait("scalar", *ld_val[tt])
                    if "xsq" in trfree:
                        pa.wait("scalar", *trfree["xsq"])
                    sq_done = pa.do(
                        "scalar", lambda e, o=xsq[:, :, :], i=x_sb[xb][:, :, :]:
                        e.activation(o, i, AF.Square), sSC, 1)
                    pa.wait("vector", *ld_val[tt])
                    if tt > 0:
                        pa.wait("vector", sSC, xn_prev)
                    pa.do("vector", lambda e, o=su, i=x_sb[xb][:, :, :]:
                          e.reduce_sum(o, i, AX.X), sVE, 1)
                    pa.wait("vector", sSC, sq_done)
                    r2 = pa.do("vector", lambda e, o=ssq, i=xsq[:, :, :]:
                               e.reduce_sum(o, i, AX.X), sVE, 1)
                    trfree["xsq"] = (sVE, r2)
                    pa.wait("vector", sVE, r2)
                    t3 = pa.do("vector", lambda e, o=tm, a=su:
                               e.scalar_tensor_tensor(
                                   o, a, 1.0 / D, a, op0=ALU.mult,
                                   op1=ALU.mult), sVE, 1)
                    pa.wait("vector", sVE, t3)
                    t4 = pa.do("vector", lambda e, o=vu, a=ssq, b=tm:
                               e.scalar_tensor_tensor(
                                   o, a, 1.0, b, op0=ALU.mult,
                                   op1=ALU.subtract), sVE, 1)
                    pa.wait("scalar", sVE, t4)
                    t5 = pa.do("scalar", lambda e, o=sd, i=vu:
                               e.activation(o, i, AF.Sqrt, scale=1.0 / D,
                                            bias=eps_sb[:, 0:1]), sSC, 1)
                    pa.wait("vector", sSC, t5)
                    t6 = pa.do("vector", lambda e, o=rstd, i=sd:
                               e.reciprocal(o, i), sVE, 1)
                    pa.wait("vector", sVE, t6)
                    t7 = pa.do("vector", lambda e, o=nmr, a=su, b=rstd:
                               e.scalar_tensor_tensor(
                                   o, a, -1.0 / D, b, op0=ALU.mult,
                                   op1=ALU.mult), sVE, 1)
                    pa.wait("scalar", sVE, t7)
                    xn_done = {}
                    for p in range(4):
                        xn_done[p] = pa.do(
                            "scalar", lambda e, o=xn_tm[xb][:, p, :],
                            i=x_sb[xb][:, p, :],
                            sc=rstd[:, p:p + 1], bi=nmr[:, p:p + 1]:
                            e.activation(o, i, AF.Identity, scale=sc, bias=bi),
                            sSC, 1)
                    xn_prev = xn_done[3]
                    xfree[xb] = (sSC, xn_done[3])
                    cp_done = []
                    fb = tt % 2
                    if fb in fmfree:
                        pass
                    for gi in range(8):
                        p, dch = gi // 2, gi % 2
                        tb = gi % 2
                        pa.wait("tensor", sSC, xn_done[p])
                        if ("tr", tb) in trfree:
                            pa.wait("tensor", *trfree[("tr", tb)])
                        if gi == 0 and fb in fmfree:
                            pa.wait("tensor", *fmfree[fb])
                        last = None
                        for s_ in range(8):
                            dc = dch * 8 + s_
                            is_last = s_ == 7
                            last = pa.do(
                                "tensor",
                                lambda e, o=psTR[tb][:, s_ * 128:(s_ + 1) * 128],
                                i=xn_tm[xb][:, p, dc * 128:(dc + 1) * 128]:
                                e.transpose(o, i, id_bf[:, :]),
                                sPE if is_last else None, 1)
                        pa.wait("vector", sPE, last)
                        ov = xn_fm[fb][:, dch * 8:dch * 8 + 8,
                                       p * 128:(p + 1) * 128]
                        iv = psTR[tb].ap().rearrange("p (s d) -> p s d", s=8)
                        cp = pa.do("vector", lambda e, o=ov, i=iv:
                                   e.tensor_copy(o, i), sVE, 1)
                        trfree[("tr", tb)] = (sVE, cp)
                        cp_done.append(cp)
                    pa.wait("sync", sVE, cp_done[-1])
                    xout = xn_s.ap().rearrange("a p t -> p a t")[
                        :, :, tt * 512:(tt + 1) * 512]
                    stv = pa.do("sync", lambda e, o=xout,
                                i=xn_fm[fb][:, :, :]:
                                e.dma_start(out=o, in_=i), sFM[fb], 16)
                    fmfree[fb] = (sFM[fb], stv)
                for s_e in sX + sFM:
                    pa.wait("sync", s_e, s_e.n)
                pa.barrier(sB)
                pa.run()

        if nphase >= 2:
            with ExitStack() as pc:
                def psb(name, shape, dtype):
                    return pc.enter_context(nc.sbuf_tensor(name, shape, dtype))

                def pps(name, shape, dtype):
                    return pc.enter_context(nc.psum_tensor(name, shape, dtype))

                wqkv_sb = psb("wqkv_sb", [128, NDC, 3072], BF16)
                xn_tl = [psb(f"xn_tl{i}", [128, NDC, 512], BF16) for i in range(2)]
                qkv_bf = [psb(f"qkv_bf{i}", [128, 512], BF16) for i in range(4)]
                vtok = [psb(f"vtok{i}", [128, 512], BF16) for i in range(2)]
                ps_qkv = [pps(f"ps_qkv{i}", [128, 512], F32) for i in range(2)]
                psTR = [pps(f"psTRa{i}", [128, 512], BF16) for i in range(2)]

                sWQ = sem("sWQ")
                sXN = [sem("sXN0"), sem("sXN1")]
                sST = [sem(f"sST{i}") for i in range(4)]
                sVT = [sem("sVT0"), sem("sVT1")]
                pa = Ph(nc)
                for dc in range(NDC):
                    pa.do("gpsimd", lambda e, o=wqkv_sb[:, dc, :], i=wqkv_h[dc]:
                          e.dma_start(out=o, in_=i), sWQ, 16)
                pa.wait("tensor", sWQ, sWQ.n)
                pa.wait("scalar", sQ, const_ld)
                free = {}
                for tt in range(8):
                    xb = tt % 2
                    if ("xn", xb) in free:
                        pa.wait("sync", *free[("xn", xb)])
                    xnin = xn_s.ap().rearrange("a p t -> p a t")[
                        :, :, tt * 512:(tt + 1) * 512]
                    lv = pa.do("sync", lambda e, o=xn_tl[xb], i=xnin:
                               e.dma_start(out=o[:, :, :], in_=i), sXN[xb], 16)
                    pa.wait("tensor", sXN[xb], lv)
                    for fc in range(24):
                        qb = fc % 2
                        eb = fc % 4
                        if ("ps", qb) in free:
                            pa.wait("tensor", *free[("ps", qb)])
                        mm = None
                        for dc in range(NDC):
                            mm = pa.do(
                                "tensor",
                                lambda e, o=ps_qkv[qb][:, :],
                                l=wqkv_sb[:, dc, fc * 128:(fc + 1) * 128],
                                rr=xn_tl[xb][:, dc, :], s0=(dc == 0),
                                s1=(dc == NDC - 1):
                                e.matmul(o, l, rr, start=s0, stop=s1),
                                sPE if dc == NDC - 1 else None, 1)
                        pa.wait("scalar", sPE, mm)
                        if ("eb", eb) in free:
                            pa.wait("scalar", *free[("eb", eb)])
                        ev = pa.do("scalar", lambda e, o=qkv_bf[eb][:, :],
                                   i=ps_qkv[qb][:, :], bi=bqkv_sb[:, fc:fc + 1]:
                                   e.activation(o, i, AF.Identity, scale=1.0,
                                                bias=bi), sSC, 1)
                        free[("ps", qb)] = (sSC, ev)
                        if fc < 16:
                            dst = (q_s if fc < 8 else k_s)[
                                fc % 8, :, tt * 512:(tt + 1) * 512]
                            pa.wait("sync", sSC, ev)
                            stv = pa.do("sync", lambda e, o=dst,
                                        i=qkv_bf[eb][:, :]:
                                        e.dma_start(out=o, in_=i), sST[eb], 16)
                            free[("eb", eb)] = (sST[eb], stv)
                        else:
                            hh = fc - 16
                            tb = fc % 2
                            pa.wait("tensor", sSC, ev)
                            if ("ptr", tb) in free:
                                pa.wait("tensor", *free[("ptr", tb)])
                            last = None
                            for s_ in range(4):
                                last = pa.do(
                                    "tensor",
                                    lambda e,
                                    o=psTR[tb][:, s_ * 128:(s_ + 1) * 128],
                                    i=qkv_bf[eb][:, s_ * 128:(s_ + 1) * 128]:
                                    e.transpose(o, i, id_bf[:, :]),
                                    sPE if s_ == 3 else None, 1)
                            free[("eb", eb)] = (sPE, last)
                            vb = fc % 2
                            pa.wait("vector", sPE, last)
                            if ("vt", vb) in free:
                                pa.wait("vector", *free[("vt", vb)])
                            cp = pa.do("vector", lambda e, o=vtok[vb][:, :],
                                       i=psTR[tb][:, :]:
                                       e.tensor_copy(o, i), sVE, 1)
                            free[("ptr", tb)] = (sVE, cp)
                            pa.wait("sync", sVE, cp)
                            vdst = v_s[hh, tt * 512:(tt + 1) * 512, :].rearrange(
                                "(s t) d -> t s d", t=128)
                            ivv = vtok[vb].ap().rearrange(
                                "p (s d) -> p s d", s=4)
                            stv = pa.do("sync", lambda e, o=vdst, i=ivv:
                                        e.dma_start(out=o, in_=i), sVT[vb], 16)
                            free[("vt", vb)] = (sVT[vb], stv)
                    free[("xn", xb)] = (sPE, sPE.n)
                for s_e in sXN + sST + sVT:
                    pa.wait("sync", s_e, s_e.n)
                pa.barrier(sB)
                pa.run()

        if nphase >= 3:
            with ExitStack() as pc:
                def psb(name, shape, dtype):
                    return pc.enter_context(nc.sbuf_tensor(name, shape, dtype))

                def pps(name, shape, dtype):
                    return pc.enter_context(nc.psum_tensor(name, shape, dtype))

                xn_sb = psb("xn_sbB", [128, NDC, 2048], BF16)
                gw_sb = [psb(f"gw{i}", [128, NDC, 128], BF16) for i in range(2)]
                uw_sb = [psb(f"uw{i}", [128, NDC, 128], BF16) for i in range(2)]
                gsil = [psb(f"gsil{i}", [128, 512], F32) for i in range(2)]
                hdn_bf = [psb(f"hdnb{i}", [128, 512], BF16) for i in range(2)]
                ps_g = [pps(f"ps_g{i}", [128, 512], F32) for i in range(2)]
                ps_u = [pps(f"ps_u{i}", [128, 512], F32) for i in range(2)]

                sXB = sem("sXB")
                sW = [sem("sW0"), sem("sW1")]
                sHB = [sem("sHB0"), sem("sHB1")]
                pb = Ph(nc)
                pb.do("scalar", lambda e: e.preload_act_table(AF.Silu))
                xin = xn_s.ap().rearrange("a p t -> p a t")[
                    :, :, bass.ds(voff["v"], 2048)]
                xl = pb.do("gpsimd", lambda e, o=xn_sb, i=xin:
                           e.dma_start(out=o[:, :, :], in_=i), sXB, 16)
                pb.wait("tensor", sXB, xl)
                pb.wait("scalar", sQ, const_ld)
                free = {}
                for ic in range(NIC):
                    wb = ic % 2
                    if ("w", wb) in free:
                        pb.wait("gpsimd", *free[("w", wb)])
                    gin = gate_h[ic].rearrange("a p d -> p a d")
                    uin = up_h[ic].rearrange("a p d -> p a d")
                    pb.do("gpsimd", lambda e, o=gw_sb[wb], i=gin:
                          e.dma_start(out=o[:, :, :], in_=i), sW[wb], 16)
                    wl = pb.do("gpsimd", lambda e, o=uw_sb[wb], i=uin:
                               e.dma_start(out=o[:, :, :], in_=i), sW[wb], 16)
                    pb.wait("tensor", sW[wb], wl)
                    last_mm = None
                    for tt in range(4):
                        gb = tt % 2
                        if ("pg", gb) in free:
                            pb.wait("tensor", *free[("pg", gb)])
                        mmg = None
                        for dc in range(NDC):
                            mmg = pa_mm = pb.do(
                                "tensor", lambda e, o=ps_g[gb][:, :],
                                l=gw_sb[wb][:, dc, :],
                                rr=xn_sb[:, dc, tt * 512:(tt + 1) * 512],
                                s0=(dc == 0), s1=(dc == NDC - 1):
                                e.matmul(o, l, rr, start=s0, stop=s1),
                                sPE if dc == NDC - 1 else None, 1)
                        if ("pu", gb) in free:
                            pb.wait("tensor", *free[("pu", gb)])
                        mmu = None
                        for dc in range(NDC):
                            mmu = pb.do(
                                "tensor", lambda e, o=ps_u[gb][:, :],
                                l=uw_sb[wb][:, dc, :],
                                rr=xn_sb[:, dc, tt * 512:(tt + 1) * 512],
                                s0=(dc == 0), s1=(dc == NDC - 1):
                                e.matmul(o, l, rr, start=s0, stop=s1),
                                sPE if dc == NDC - 1 else None, 1)
                        last_mm = mmu
                        sb2 = tt % 2
                        pb.wait("scalar", sPE, mmg)
                        if ("gs", sb2) in free:
                            pb.wait("scalar", *free[("gs", sb2)])
                        sl = pb.do("scalar", lambda e, o=gsil[sb2][:, :],
                                   i=ps_g[gb][:, :], sc=gs_sb[:, ic:ic + 1]:
                                   e.activation(o, i, AF.Silu, scale=sc),
                                   sSC, 1)
                        free[("pg", gb)] = (sSC, sl)
                        hb2 = tt % 2
                        pb.wait("vector", sSC, sl)
                        pb.wait("vector", sPE, mmu)
                        if ("hb", hb2) in free:
                            pb.wait("vector", *free[("hb", hb2)])
                        hv = pb.do("vector", lambda e, o=hdn_bf[hb2][:, :],
                                   i=ps_u[gb][:, :], sc=us_sb[:, ic:ic + 1],
                                   g2=gsil[sb2][:, :]:
                                   e.scalar_tensor_tensor(
                                       o, i, sc, g2, op0=ALU.mult,
                                       op1=ALU.mult), sVE, 1)
                        free[("pu", gb)] = (sVE, hv)
                        free[("gs", sb2)] = (sVE, hv)
                        pb.wait("sync", sVE, hv)
                        stv = pb.do("sync", lambda e,
                                    o=hdn_s[ic, :, tt * 512:(tt + 1) * 512],
                                    i=hdn_bf[hb2][:, :]:
                                    e.dma_start(out=o, in_=i), sHB[hb2], 16)
                        free[("hb", hb2)] = (sHB[hb2], stv)
                    free[("w", wb)] = (sPE, last_mm)
                for s_e in sHB:
                    pb.wait("sync", s_e, s_e.n)
                pb.wait("gpsimd", sXB, sXB.n)
                for s_e in sW:
                    pb.wait("gpsimd", s_e, s_e.n)
                pb.barrier(sB)
                pb.run()

        if nphase >= 4:
            with ExitStack() as pc:
                def psb(name, shape, dtype):
                    return pc.enter_context(nc.sbuf_tensor(name, shape, dtype))

                def pps(name, shape, dtype):
                    return pc.enter_context(nc.psum_tensor(name, shape, dtype))

                k_sb = [psb(f"k_sb{i}", [128, S], BF16) for i in range(2)]
                q_sb = [psb(f"q_sb{i}", [128, S], BF16) for i in range(2)]
                v_sb = [psb(f"v_sb{i}", [128, 32, 128], BF16) for i in range(2)]
                rhsRel = psb("rhsRel", [4, 512], BF16)
                lhsRel = psb("lhsRel", [4, 33, 128], BF16)
                slb = psb("slb", [128, 8], F32)
                rsl = psb("rsl", [128, 8], F32)
                slTSC = psb("slTSC", [128, 8], F32)
                ones1 = psb("ones1", [1, 128], F32)
                wspA = psb("wspA", [1, 512], F32)
                wspH = psb("wspH", [1, 512], F32)
                wspB = psb("wspB", [1, 512], F32)
                wspL = psb("wspL", [1, 33, 128], F32)
                tn = [psb(f"tn{i}", [128, 1024], F32) for i in range(2)]
                pr = [psb(f"pr{i}", [128, 1024], BF16) for i in range(2)]
                rc = [psb(f"rc{i}", [128, 512], F32) for i in range(2)]
                cbf = [psb(f"cbf{i}", [128, 512], BF16) for i in range(2)]
                dw_q = psb("dw_q", [128, 2, 22, 512], BF16)
                hdn_fl = psb("hdn_fl", [128, 2, 22, 512], BF16)
                fD = [psb(f"fD{i}", [128, 512], F32) for i in range(2)]
                psS = [pps(f"psS{i}", [128, 1024], F32) for i in range(2)]
                psCx = pps("psCx", [128, 512], F32)
                psDn = pps("psDn", [128, 512], F32)
                psD = [pps(f"psD{i}", [128, 512], F32) for i in range(2)]

                sHL = [sem("sHL0"), sem("sHL1")]
                sCS = [sem("sCS0"), sem("sCS1")]
                sRELs = sem("sRELs")
                sRELl = sem("sRELl")
                pcx = Ph(nc)
                pcx.do("scalar", lambda e: e.preload_act_table(AF.Tanh))
                pcx.wait("gpsimd", sGM, consts_done)
                pcx.wait("vector", sQ, const_ld)
                g1 = pcx.do("gpsimd", lambda e: e.memset(ones1[:, :], 1.0),
                            sGM, 1)
                pcx.wait("tensor", sGM, g1)
                pcx.wait("tensor", sQ, const_ld)
                bmm = pcx.do("tensor", lambda e: e.matmul(
                    psCx[:, 0:8], ones1[0:1, :], sl_sb[0:1, 0:8],
                    start=True, stop=True, skip_group_check=True), sPE, 1)
                pcx.wait("vector", sPE, bmm)
                v1 = pcx.do("vector", lambda e: e.tensor_copy(
                    slb[:, :], psCx[:, 0:8]), sVE, 1)
                pcx.wait("vector", sVE, v1)
                v2 = pcx.do("vector", lambda e: e.reciprocal(
                    rsl[:, :], slb[:, :]), sVE, 1)
                pcx.wait("vector", sVE, v2)
                v3 = pcx.do("vector", lambda e: e.tensor_scalar_mul(
                    slTSC[:, :], slb[:, :], TSC), sVE, 1)
                setup_norm = v1
                gi1 = pcx.do("gpsimd", lambda e: e.iota(
                    wspA[:, :], [[1, 512]], channel_multiplier=0,
                    allow_small_or_imprecise_dtypes=True))
                pcx.do("gpsimd", lambda e: e.memset(wspH[:, :], 1.0))
                gi2 = pcx.do("gpsimd", lambda e: e.affine_select(
                    out=wspH[:, :], in_=wspH[:, :], compare_op=ALU.is_ge,
                    fill=0.0, base=-256, pattern=[[1, 512]],
                    channel_multiplier=0), sGM, 1)
                pcx.wait("vector", sGM, gi2)
                v4 = pcx.do("vector", lambda e: e.scalar_tensor_tensor(
                    wspB[:, :], wspH[:, :], 256.0, wspA[:, :],
                    op0=ALU.mult, op1=ALU.subtract), sVE, 1)
                pcx.wait("vector", sVE, v4)
                v5 = pcx.do("vector", lambda e: e.tensor_scalar_mul(
                    wspA[:, :], wspH[:, :], -256.0), sVE, 1)
                pcx.wait("vector", sVE, v5)
                v6 = pcx.do("vector", lambda e: e.memset(wspH[:, :], 1.0),
                            sVE, 1)
                pcx.wait("sync", sVE, v6)
                pcx.do("sync", lambda e: e.dma_start(
                    out=relr_d[0:1, :], in_=wspH[0:1, :]), sRELs, 16)
                pcx.do("sync", lambda e: e.dma_start(
                    out=relr_d[1:2, :], in_=wspB[0:1, :]), sRELs, 16)
                pcx.do("sync", lambda e: e.dma_start(
                    out=relr_d[2:3, :], in_=wspA[0:1, :]), sRELs, 16)
                pcx.do("sync", lambda e: e.dma_start(
                    out=relr_d[3:4, :], in_=wspH[0:1, :]), sRELs, 16)
                gi3 = pcx.do("gpsimd", lambda e: e.iota(
                    wspL[:, :, :], [[0, 33], [1, 128]], channel_multiplier=0,
                    allow_small_or_imprecise_dtypes=True), sGM, 1)
                pcx.wait("sync", sGM, gi3)
                pcx.do("sync", lambda e: e.dma_start(
                    out=rell_d[0:1, :, :], in_=wspL[0:1, :, :]), sRELs, 16)
                pcx.wait("gpsimd", sRELs, sRELs.n)
                gi4 = pcx.do("gpsimd", lambda e: e.iota(
                    wspL[:, :, :], [[128, 33], [0, 128]],
                    channel_multiplier=0,
                    allow_small_or_imprecise_dtypes=True), sGM, 1)
                pcx.wait("vector", sGM, gi4)
                v7 = pcx.do("vector", lambda e: e.tensor_scalar_add(
                    wspL[:, :, :], wspL[:, :, :], -3584.0), sVE, 1)
                pcx.wait("sync", sVE, v7)
                pcx.do("sync", lambda e: e.dma_start(
                    out=rell_d[3:4, :, :], in_=wspL[0:1, :, :]), sRELs, 16)
                pcx.wait("vector", sRELs, sRELs.n)
                v8 = pcx.do("vector", lambda e: e.memset(wspL[:, :, :], 1.0),
                            sVE, 1)
                pcx.wait("sync", sVE, v8)
                pcx.do("sync", lambda e: e.dma_start(
                    out=rell_d[1:2, :, :], in_=wspL[0:1, :, :]), sRELs, 16)
                pcx.do("sync", lambda e: e.dma_start(
                    out=rell_d[2:3, :, :], in_=wspL[0:1, :, :]), sRELs, 16)
                pcx.wait("gpsimd", sRELs, sRELs.n)
                pcx.do("gpsimd", lambda e: e.dma_start(
                    out=rhsRel[:, :], in_=relr_d[:, :]), sRELl, 16)
                rel_ld = pcx.do("gpsimd", lambda e: e.dma_start(
                    out=lhsRel[:, :, :], in_=rell_d[:, :, :]), sRELl, 16)
                pcx.wait("tensor", sRELl, rel_ld)
                grp_free = {}
                pr_free = {}
                head_free = {}
                norm_done_val = setup_norm
                gi_ctr = 0
                sDWh = [sem("sDWh0"), sem("sDWh1")]
                sHFh = [sem("sHFh0"), sem("sHFh1")]
                sFS = [sem("sFS0"), sem("sFS1")]
                dstate = {"dw_war": {}, "hdn_war": {}, "dw_ld": {},
                          "hdn_ld": {}, "evac": {}, "fwar": {},
                          "last_mm": None}

                def mk_dw_load(qr, hf):
                    def go():
                        if hf in dstate["dw_war"]:
                            pcx.wait("gpsimd", sPE, dstate["dw_war"][hf])
                        lv = None
                        for icl in range(22):
                            ic = hf * 22 + icl
                            din = down_h[ic, :, qr * 512:(qr + 1) * 512]
                            lv = pcx.do("gpsimd",
                                        lambda e, o=dw_q[:, hf, icl, :],
                                        i=din: e.dma_start(out=o, in_=i),
                                        sDWh[hf], 16)
                        dstate["dw_ld"][hf] = lv
                    return go

                def mk_hdn_load(qr, tt, hf):
                    def go():
                        if hf in dstate["hdn_war"]:
                            pcx.wait("sync", sPE, dstate["hdn_war"][hf])
                        hin = hdn_s[hf * 22:hf * 22 + 22, :,
                                    tt * 512:(tt + 1) * 512].rearrange(
                                        "a p d -> p a d")
                        lv = pcx.do("sync", lambda e,
                                    o=hdn_fl[:, hf, :, :], i=hin:
                                    e.dma_start(out=o, in_=i), sHFh[hf], 16)
                        dstate["hdn_ld"][hf] = lv
                    return go

                def mk_mms(qr, tt, pas, hf, ic0, icn):
                    def go():
                        if ic0 == 0:
                            pcx.wait("tensor", sDWh[hf], dstate["dw_ld"][hf])
                            pcx.wait("tensor", sHFh[hf], dstate["hdn_ld"][hf])
                            if hf == 0 and pas == 0:
                                pass
                            if hf == 0:
                                for m in range(2):
                                    if m in dstate["evac"]:
                                        pcx.wait("tensor", sVE,
                                                 dstate["evac"][m])
                        lm = None
                        for icl in range(ic0, ic0 + icn):
                            for m in range(2):
                                mc_loc = 2 * pas + m
                                st0 = (hf == 0 and icl == 0)
                                st1 = (hf == 1 and icl == 21)
                                lm = pcx.do(
                                    "tensor", lambda e, o=psD[m][:, :],
                                    l=dw_q[:, hf, icl,
                                           mc_loc * 128:(mc_loc + 1) * 128],
                                    rr=hdn_fl[:, hf, icl, :],
                                    s0=st0, s1=st1:
                                    e.matmul(o, l, rr, start=s0, stop=s1,
                                             skip_group_check=True),
                                    sPE, 1)
                        dstate["last_mm"] = lm
                        dstate["dw_war"][hf] = lm
                        dstate["hdn_war"][hf] = lm
                    return go

                def mk_evac(qr, tt, pas):
                    def go():
                        pcx.wait("vector", sPE, dstate["last_mm"])
                        for m in range(2):
                            mc_g = qr * 4 + 2 * pas + m
                            if m in dstate["fwar"]:
                                pcx.wait("vector", sFS[m], dstate["fwar"][m])
                            ev = pcx.do(
                                "vector", lambda e, o=fD[m][:, :],
                                i=psD[m][:, :],
                                sc=ds_sb[:, mc_g:mc_g + 1]:
                                e.tensor_single_scalar(o, i, sc, ALU.mult),
                                sVE, 1)
                            dstate["evac"][m] = ev
                            pcx.wait("sync", sVE, ev)
                            stv = pcx.do(
                                "sync", lambda e,
                                o=f_s[mc_g, :, tt * 512:(tt + 1) * 512],
                                i=fD[m][:, :]:
                                e.dma_start(out=o, in_=i), sFS[m], 16)
                            dstate["fwar"][m] = stv
                    return go

                down_steps = []
                for qr in range(4):
                    down_steps.append(mk_dw_load(qr, 0))
                    down_steps.append(mk_dw_load(qr, 1))
                    for tt in range(4):
                        down_steps.append(mk_hdn_load(qr, tt, 0))
                        down_steps.append(mk_hdn_load(qr, tt, 1))
                        for pas in range(2):
                            for hf in range(2):
                                down_steps.append(mk_mms(qr, tt, pas, hf, 0, 11))
                                down_steps.append(mk_mms(qr, tt, pas, hf, 11, 11))
                            down_steps.append(mk_evac(qr, tt, pas))
                dtot = len(down_steps)
                ddone = [0]
                batch_ctr = [0]

                def pace_down(final=False):
                    if final:
                        tgt = dtot
                    else:
                        tgt = min(dtot, int(dtot * batch_ctr[0] / 576.0))
                    while ddone[0] < tgt:
                        down_steps[ddone[0]]()
                        ddone[0] += 1
                for h in range(8):
                    hb = h % 2
                    if hb in head_free:
                        pcx.wait("sync", *head_free[hb])
                    pcx.do("sync", lambda e, o=k_sb[hb][:, :], i=k_s[h]:
                           e.dma_start(out=o, in_=i), sHL[hb], 16)
                    pcx.do("sync", lambda e, o=q_sb[hb][:, :], i=q_s[h]:
                           e.dma_start(out=o, in_=i), sHL[hb], 16)
                    vin = v_s[h].rearrange("(s t) d -> t s d", t=128)
                    hl = pcx.do("sync", lambda e, o=v_sb[hb][:, :, :], i=vin:
                                e.dma_start(out=o, in_=i), sHL[hb], 16)
                    pcx.wait("vector", sHL[hb], hl)
                    pcx.wait("vector", sVE, v3)
                    qpv = pcx.do("vector", lambda e, o=q_sb[hb][:, :],
                                 sc=rsl[:, h:h + 1]:
                                 e.tensor_single_scalar(o, o, sc, ALU.mult),
                                 sVE, 1)
                    pcx.wait("tensor", sHL[hb], hl)
                    pcx.wait("tensor", sVE, qpv)
                    h_last_pe = None
                    for qt in range(8):
                        nb = 2 * (qt + 1)
                        pend = []
                        for j in range(nb):
                            g = gi_ctr % 2
                            pb_ = gi_ctr % 2
                            gi_ctr += 1
                            if g in grp_free:
                                pcx.wait("tensor", sSC, grp_free[g])
                            for u in range(2):
                                kt = 2 * j + u
                                pcx.do("tensor", lambda e,
                                       o=psS[g][:, u * 512:(u + 1) * 512],
                                       l=k_sb[hb][:, kt * 128:(kt + 1) * 128],
                                       rr=q_sb[hb][:, qt * 512:(qt + 1) * 512]:
                                       e.matmul(o, l, rr, start=True,
                                                stop=False,
                                                skip_group_check=True))
                                didx = kt - 4 * qt + 28
                                sc_done = pcx.do(
                                    "tensor", lambda e,
                                    o=psS[g][:, u * 512:(u + 1) * 512],
                                    l=lhsRel[0:4, didx, :],
                                    rr=rhsRel[0:4, :]:
                                    e.matmul(o, l, rr, start=False, stop=True,
                                             skip_group_check=True),
                                    sPE if u == 1 else None, 1)
                            pcx.wait("scalar", sPE, sc_done)
                            tv = pcx.do("scalar", lambda e, o=tn[g][:, :],
                                        i=psS[g][:, :],
                                        sc=slTSC[:, h:h + 1]:
                                        e.activation(o, i, AF.Tanh, scale=sc),
                                        sSC, 1)
                            grp_free[g] = tv
                            if pb_ in pr_free:
                                pcx.wait("scalar", sPE, pr_free[pb_])
                            pv = pcx.do("scalar", lambda e, o=pr[pb_][:, :],
                                        i=tn[g][:, :]:
                                        e.activation(o, i, AF.Exp, scale=30.0),
                                        sSC, 1)
                            diag = j >= nb - 2
                            mv = None
                            if diag:
                                off = 0 if j == nb - 2 else 2
                                pcx.wait("vector", sSC, pv)
                                mv = pcx.do(
                                    "vector", lambda e, o=pr[pb_][:, :],
                                    m=mask_bf[:, off:off + 2, :]:
                                    e.tensor_tensor(
                                        o, o,
                                        m.rearrange("p a d -> p (a d)"),
                                        ALU.mult), sVE, 1)
                            pend.append((j, g, pb_, pv, mv))
                            while pend and (len(pend) > 1 or j == nb - 1):
                                jj, gg, pp2, pvv, mvv = pend.pop(0)
                                if mvv is not None:
                                    pcx.wait("tensor", sVE, mvv)
                                else:
                                    pcx.wait("tensor", sSC, pvv)
                                if jj == 0 and norm_done_val is not None:
                                    pcx.wait("tensor", sVE, norm_done_val)
                                for u in range(2):
                                    kt = 2 * jj + u
                                    st0 = (jj == 0 and u == 0)
                                    st1 = (jj == nb - 1 and u == 1)
                                    pcx.do("tensor", lambda e, o=psCx[:, :],
                                           l=v_sb[hb][:, kt, :],
                                           rr=pr[pp2][:, u * 512:(u + 1) * 512],
                                           s0=st0, s1=st1:
                                           e.matmul(o, l, rr, start=s0,
                                                    stop=s1,
                                                    skip_group_check=True))
                                    cd = pcx.do(
                                        "tensor", lambda e, o=psDn[:, :],
                                        l=ones_bf[:, :],
                                        rr=pr[pp2][:, u * 512:(u + 1) * 512],
                                        s0=st0, s1=st1:
                                        e.matmul(o, l, rr, start=s0, stop=s1,
                                                 skip_group_check=True),
                                        sPE if u == 1 else None, 1)
                                pr_free[pp2] = cd
                                h_last_pe = cd
                                batch_ctr[0] += 1
                                pace_down()
                        qb2 = (h * 8 + qt) % 2
                        pcx.wait("vector", sPE, h_last_pe)
                        if ("cbf", qb2) in pr_free:
                            pcx.wait("vector", sCS[qb2], pr_free[("cbf", qb2)])
                        pcx.do("vector", lambda e, o=rc[qb2][:, :],
                               i=psDn[:, :]: e.reciprocal(o, i))
                        nv = pcx.do("vector", lambda e, o=cbf[qb2][:, :],
                                    a=psCx[:, :], b=rc[qb2][:, :]:
                                    e.tensor_mul(o, a, b), sVE, 1)
                        norm_done_val = nv
                        pcx.wait("sync", sVE, nv)
                        stc = pcx.do("sync", lambda e,
                               o=ctxm[h * 128:(h + 1) * 128,
                                      qt * 512:(qt + 1) * 512],
                               i=cbf[qb2][:, :]:
                               e.dma_start(out=o, in_=i), sCS[qb2], 16)
                        pr_free[("cbf", qb2)] = stc
                    head_free[hb] = (sPE, h_last_pe)
                pace_down(final=True)
                pcx.wait("gpsimd", sCS[0], sCS[0].n)
                pcx.wait("gpsimd", sCS[1], sCS[1].n)
                if dump != "ctxm":
                    pcx.do("gpsimd", lambda e: e.collective_compute(
                    "AllGather", ALU.bypass,
                    replica_groups=[[0, 1], [2, 3], [4, 5], [6, 7]],
                        ins=[ctxm.ap().opt()],
                        outs=[ctxg.ap().opt()]), sC, 1)
                for s_e in sHL + sCS + [sHFh[0], sHFh[1], sFS[0], sFS[1]]:
                    pcx.wait("sync", s_e, s_e.n)
                for s_e in sDWh:
                    pcx.wait("gpsimd", s_e, s_e.n)
                pcx.barrier(sB)
                pcx.run()

        if False:
            with ExitStack() as pc:
                def psb(name, shape, dtype):
                    return pc.enter_context(nc.sbuf_tensor(name, shape, dtype))

                def pps(name, shape, dtype):
                    return pc.enter_context(nc.psum_tensor(name, shape, dtype))

                dw_sb = psb("dw_sb", [128, NIC, 1024], BF16)
                hdn_fl = psb("hdn_fl", [128, NIC, 512], BF16)
                f_buf = [psb(f"f_buf{i}", [128, 512], F32) for i in range(2)]
                ps_dn = [pps(f"ps_dn{i}", [128, 512], F32) for i in range(8)]

                sDW = sem("sDW")
                sHF = sem("sHF")
                sFS = [sem("sFS0"), sem("sFS1")]
                pc2 = Ph(nc)
                free = {}
                for half in range(2):
                    if "dw" in free:
                        pc2.wait("gpsimd", *free["dw"])
                    dl = None
                    for ic in range(NIC):
                        din = down_h[ic, :, half * 1024:(half + 1) * 1024]
                        dl = pc2.do("gpsimd", lambda e, o=dw_sb[:, ic, :],
                                    i=din: e.dma_start(out=o, in_=i), sDW, 16)
                    pc2.wait("tensor", sDW, dl)
                    last_mm = None
                    for tt in range(4):
                        if "hdn" in free:
                            pc2.wait("sync", *free["hdn"])
                        hin = hdn_s[:, :, tt * 512:(tt + 1) * 512].rearrange(
                            "a p d -> p a d")
                        hl = pc2.do("sync", lambda e, o=hdn_fl[:, :, :], i=hin:
                                    e.dma_start(out=o, in_=i), sHF, 16)
                        pc2.wait("tensor", sHF, hl)
                        if "ev" in free:
                            pc2.wait("tensor", *free["ev"])
                        for ic in range(NIC):
                            for mc in range(8):
                                is_l = ic == NIC - 1 and mc == 7
                                last_mm = pc2.do(
                                    "tensor", lambda e, o=ps_dn[mc][:, :],
                                    l=dw_sb[:, ic, mc * 128:(mc + 1) * 128],
                                    rr=hdn_fl[:, ic, :], s0=(ic == 0),
                                    s1=(ic == NIC - 1):
                                    e.matmul(o, l, rr, start=s0, stop=s1,
                                             skip_group_check=True),
                                    sPE if is_l else None, 1)
                        free["hdn"] = (sPE, last_mm)
                        pc2.wait("vector", sPE, last_mm)
                        evs = []
                        for mc in range(8):
                            fb2 = mc % 2
                            if ("f", fb2) in free:
                                pc2.wait("vector", *free[("f", fb2)])
                            ev = pc2.do(
                                "vector", lambda e, o=f_buf[fb2][:, :],
                                i=ps_dn[mc][:, :],
                                sc=ds_sb[:, half * 8 + mc:half * 8 + mc + 1]:
                                e.tensor_single_scalar(o, i, sc, ALU.mult),
                                sVE, 1)
                            evs.append(ev)
                            pc2.wait("sync", sVE, ev)
                            stv = pc2.do(
                                "sync", lambda e,
                                o=f_s[half * 8 + mc, :,
                                      tt * 512:(tt + 1) * 512],
                                i=f_buf[fb2][:, :]:
                                e.dma_start(out=o, in_=i), sFS[fb2], 16)
                            free[("f", fb2)] = (sFS[fb2], stv)
                        free["ev"] = (sVE, evs[-1])
                    free["dw"] = (sPE, last_mm)
                for s_e in [sHF] + sFS:
                    pc2.wait("sync", s_e, s_e.n)
                pc2.wait("gpsimd", sDW, sDW.n)
                pc2.barrier(sB)
                pc2.run()

        if nphase >= 5:
            with ExitStack() as pc:
                def psb(name, shape, dtype):
                    return pc.enter_context(nc.sbuf_tensor(name, shape, dtype))

                def pps(name, shape, dtype):
                    return pc.enter_context(nc.psum_tensor(name, shape, dtype))

                wo_sb = psb("wo_sb", [128, NDC, 2048], BF16)
                ctx_tl = [psb(f"ctx_tl{i}", [128, NDC, 512], BF16)
                          for i in range(2)]
                x_tl = psb("x_tl", [128, 4, D], F32)
                out_sb = psb("out_sb", [128, 4, D], F32)
                f_tl = [psb(f"f_tl{i}", [128, 512], F32) for i in range(2)]
                gsil = [psb(f"gsilD{i}", [128, 512], F32) for i in range(2)]
                sg = [psb(f"sg{i}", [128, 512], F32) for i in range(2)]
                ps_s = [pps(f"ps_s{i}", [128, 512], F32) for i in range(2)]
                psTRf = [pps(f"psTRf{i}", [128, 512], F32) for i in range(2)]

                sWO = sem("sWO")
                sCT = sem("sCT")
                sXD = sem("sXD")
                sFL = [sem("sFL0"), sem("sFL1")]
                sOS = sem("sOS")
                pd = Ph(nc)
                pd.do("scalar", lambda e: e.preload_act_table(AF.Silu))
                for dc in range(NDC):
                    pd.do("gpsimd", lambda e, o=wo_sb[:, dc, :], i=wo_h[dc]:
                          e.dma_start(out=o, in_=i), sWO, 16)
                pd.wait("gpsimd", sC, 1)
                pd.wait("tensor", sWO, sWO.n)
                free = {}
                for tt in range(4):
                    cb_ = tt % 2
                    if ("ctx", cb_) in free:
                        pd.wait("gpsimd", *free[("ctx", cb_)])
                    cin = ctxg.ap().rearrange("(a p) t -> p a t", p=128)[
                        :, :, bass.ds(voff["v"], 2048)][
                        :, :, tt * 512:(tt + 1) * 512]
                    cl = pd.do("gpsimd", lambda e, o=ctx_tl[cb_][:, :, :],
                               i=cin: e.dma_start(out=o, in_=i), sCT, 16)
                    if "x" in free:
                        pd.wait("gpsimd", *free["x"])
                    xin = x_h.ap()[bass.ds(voff["v"], 2048), :].rearrange(
                        "(a p t) d -> t a p d", p=4, t=128)
                    xl = pd.do("gpsimd", lambda e, o=x_tl[:, :, :],
                               i=xin[:, tt, :, :]:
                               e.dma_start(out=o, in_=i), sXD, 16)
                    pd.wait("tensor", sCT, cl)
                    ocp = []
                    for mc in range(NDC):
                        pb2 = mc % 2
                        if ("f", pb2) in free:
                            pd.wait("sync", *free[("f", pb2)])
                        fl = pd.do("sync", lambda e, o=f_tl[pb2][:, :],
                                   i=f_s[mc, :, tt * 512:(tt + 1) * 512]:
                                   e.dma_start(out=o, in_=i), sFL[pb2], 16)
                        if ("ps", pb2) in free:
                            pd.wait("tensor", *free[("ps", pb2)])
                        mm = None
                        for kc in range(NDC):
                            mm = pd.do(
                                "tensor", lambda e, o=ps_s[pb2][:, :],
                                l=wo_sb[:, kc, mc * 128:(mc + 1) * 128],
                                rr=ctx_tl[cb_][:, kc, :], s0=(kc == 0),
                                s1=(kc == NDC - 1):
                                e.matmul(o, l, rr, start=s0, stop=s1),
                                sPE if kc == NDC - 1 else None, 1)
                        pd.wait("scalar", sFL[pb2], fl)
                        if ("gs", pb2) in free:
                            pd.wait("scalar", *free[("gs", pb2)])
                        sl = pd.do("scalar", lambda e, o=gsil[pb2][:, :],
                                   i=f_tl[pb2][:, :]:
                                   e.activation(o, i, AF.Silu), sSC, 1)
                        free[("f", pb2)] = (sSC, sl)
                        pd.wait("vector", sPE, mm)
                        pd.wait("vector", sSC, sl)
                        if ("sg", pb2) in free:
                            pd.wait("vector", *free[("sg", pb2)])
                        sv = pd.do("vector", lambda e, o=sg[pb2][:, :],
                                   i=ps_s[pb2][:, :],
                                   bi=bo_sb[:, mc:mc + 1],
                                   g2=gsil[pb2][:, :]:
                                   e.scalar_tensor_tensor(
                                       o, i, bi, g2, op0=ALU.add,
                                       op1=ALU.mult), sVE, 1)
                        free[("ps", pb2)] = (sVE, sv)
                        free[("gs", pb2)] = (sVE, sv)
                        pd.wait("tensor", sVE, sv)
                        if ("tr", pb2) in free:
                            pd.wait("tensor", *free[("tr", pb2)])
                        last = None
                        for s_ in range(4):
                            last = pd.do(
                                "tensor", lambda e,
                                o=psTRf[pb2][:, s_ * 128:(s_ + 1) * 128],
                                i=sg[pb2][:, s_ * 128:(s_ + 1) * 128]:
                                e.transpose(o, i, id_f32[:, :]),
                                sPE if s_ == 3 else None, 1)
                        free[("sg", pb2)] = (sPE, last)
                        pd.wait("vector", sPE, last)
                        if mc < 2 and "out" in free:
                            pd.wait("vector", *free["out"])
                        ov = out_sb[:, :, mc * 128:(mc + 1) * 128]
                        iv = psTRf[pb2].ap().rearrange("p (s d) -> p s d", s=4)
                        cp = pd.do("vector", lambda e, o=ov, i=iv:
                                   e.tensor_copy(o, i), sVE, 1)
                        free[("tr", pb2)] = (sVE, cp)
                        ocp.append(cp)
                    pd.wait("vector", sXD, xl)
                    av = pd.do("vector", lambda e, o=out_sb[:, :, :],
                               a=out_sb[:, :, :], b=x_tl[:, :, :]:
                               e.tensor_add(o, a, b), sVE, 1)
                    free["x"] = (sVE, av)
                    pd.wait("sync", sVE, av)
                    odst = out_h[tt * 512:(tt + 1) * 512, :].rearrange(
                        "(p t) d -> t p d", t=128)
                    ost = pd.do("sync", lambda e, o=odst, i=out_sb[:, :, :]:
                                e.dma_start(out=o, in_=i), sOS, 16)
                    free["out"] = (sOS, ost)
                    free[("ctx", cb_)] = (sPE, sPE.n)
                pd.wait("sync", sOS, sOS.n)
                for s_e in sFL:
                    pd.wait("sync", s_e, s_e.n)
                for s_e in (sWO, sCT, sXD):
                    pd.wait("gpsimd", s_e, s_e.n)
                pd.barrier(sB)
                pd.run()
        else:
            pz = Ph(nc)
            pz.wait("sync", sQ, sQ.n)
            pz.barrier(sB)
            pz.run()

    return nc


def prep_inputs(c, inputs):
    b, r = c // 2, c % 2
    hs = slice(1024 * r, 1024 * r + 1024)
    Wq = inputs["L0_attn_Wq"]
    Wk = inputs["L0_attn_Wk"]
    Wv = inputs["L0_attn_Wv"]
    w_cat = np.concatenate([Wq[hs], Wk[hs], Wv[hs]], 0)
    b_cat = np.concatenate([inputs["L0_attn_bq"][hs], inputs["L0_attn_bk"][hs],
                            inputs["L0_attn_bv"][hs]])
    gate = inputs["L0_ffn_gate_w"]
    up = inputs["L0_ffn_up_w"]
    down = inputs["L0_ffn_down_w"]
    slopes = inputs["L0_attn_alibi_slopes"][8 * r:8 * r + 8]
    return {
        "x": np.ascontiguousarray(inputs["L0_attn_hidden_states"][b]),
        "wqkvT": np.ascontiguousarray(w_cat.T.reshape(NDC, 128, 3072)),
        "bqkv": np.ascontiguousarray(b_cat.reshape(24, 128).T),
        "woT": np.ascontiguousarray(
            inputs["L0_attn_Wo"].T.reshape(NDC, 128, D)),
        "bo": np.ascontiguousarray(inputs["L0_attn_bo"].reshape(NDC, 128).T),
        "gateT": np.ascontiguousarray(
            gate.T.reshape(NDC, 128, NIC, 128).transpose(2, 0, 1, 3)),
        "upT": np.ascontiguousarray(
            up.T.reshape(NDC, 128, NIC, 128).transpose(2, 0, 1, 3)),
        "downT": np.ascontiguousarray(down.T.reshape(NIC, 128, D)),
        "gsc": np.ascontiguousarray(
            inputs["L0_ffn_gate_s"].reshape(NIC, 128).T),
        "usc": np.ascontiguousarray(
            inputs["L0_ffn_up_s"].reshape(NIC, 128).T),
        "dsc": np.ascontiguousarray(
            inputs["L0_ffn_down_s"].reshape(NDC, 128).T),
        "slopes2": np.ascontiguousarray(
            np.stack([slopes, slopes], 0)),
    }


def kernel(**inputs):
    nc = build_nc()
    in_maps = [prep_inputs(c, inputs) for c in range(8)]
    res = run_bass_kernel_spmd(nc, in_maps, core_ids=list(range(8)))
    out = np.empty((B, S, D), dtype=np.float32)
    for c in range(8):
        b, r = c // 2, c % 2
        out[b, 2048 * r:2048 * r + 2048, :] = res.results[c]["out"]
    return out


# version 05
# speedup vs baseline: 1.0344x; 1.0344x over previous
import math
from contextlib import ExitStack

import numpy as np

import concourse.bass as bass
import concourse.mybir as mybir
from concourse.bass_utils import run_bass_kernel_spmd

F32 = mybir.dt.float32
BF16 = mybir.dt.bfloat16
I32 = mybir.dt.int32
AF = mybir.ActivationFunctionType
ALU = mybir.AluOpType
AX = mybir.AxisListType

B, S, D, H, HD = 4, 4096, 2048, 16, 128
NI = 5632
NIC = NI // 128
NDC = D // 128
EPS = 1e-5
TSC = 1.0 / (30.0 * math.sqrt(HD))

ENGS = ("sync", "gpsimd", "scalar", "vector", "tensor")


class Sem:
    __slots__ = ("h", "n")

    def __init__(self, h):
        self.h = h
        self.n = 0


class Ph:

    def __init__(self, nc):
        self.nc = nc
        self.st = {e: [] for e in ENGS}

    def wait(self, eng, sem, val=None):
        v = sem.n if val is None else val
        if v and v > 0:
            self.st[eng].append(lambda e, h=sem.h, vv=v: e.wait_ge(h, vv))

    def do(self, eng, build, sem=None, k=1):
        if sem is None:
            self.st[eng].append(lambda e, b=build: b(e))
            return None
        sem.n += k
        self.st[eng].append(
            lambda e, b=build, h=sem.h, kk=k: b(e).then_inc(h, kk))
        return sem.n

    def barrier(self, sB):
        for e in ENGS:
            self.do(e, lambda en: en.nop(), sB, 1)
        tgt = sB.n
        for e in ENGS:
            self.wait(e, sB, tgt)

    def run(self):
        with self.nc.Block() as blk:
            for e in ENGS:
                fns = self.st[e]
                if fns:
                    getattr(blk, e)(lambda eng, fns=fns: [f(eng) for f in fns])


def build_nc(dump=None, nphase=99):
    nc = bass.Bass("TRN2", target_bir_lowering=False)

    def kind_of(name):
        return "ExternalOutput" if dump == name else "Internal"

    x_h = nc.dram_tensor("x", [S, D], F32, kind="ExternalInput")
    wqkv_h = nc.dram_tensor("wqkvT", [NDC, 128, 3072], F32, kind="ExternalInput")
    bqkv_h = nc.dram_tensor("bqkv", [128, 24], F32, kind="ExternalInput")
    wo_h = nc.dram_tensor("woT", [NDC, 128, D], F32, kind="ExternalInput")
    bo_h = nc.dram_tensor("bo", [128, NDC], F32, kind="ExternalInput")
    gate_h = nc.dram_tensor("gateT", [NIC, NDC, 128, 128], F32, kind="ExternalInput")
    up_h = nc.dram_tensor("upT", [NIC, NDC, 128, 128], F32, kind="ExternalInput")
    down_h = nc.dram_tensor("downT", [NIC, 128, D], F32, kind="ExternalInput")
    gs_h = nc.dram_tensor("gsc", [128, NIC], F32, kind="ExternalInput")
    us_h = nc.dram_tensor("usc", [128, NIC], F32, kind="ExternalInput")
    ds_h = nc.dram_tensor("dsc", [128, NDC], F32, kind="ExternalInput")
    sl_h = nc.dram_tensor("slopes2", [2, 8], F32, kind="ExternalInput")
    out_h = nc.dram_tensor("out", [2048, D], F32, kind="ExternalOutput")

    xn_s = nc.dram_tensor("xn_s", [NDC, 128, S], BF16, kind=kind_of("xn_s"))
    q_s = nc.dram_tensor("q_s", [8, 128, S], BF16, kind=kind_of("q_s"))
    k_s = nc.dram_tensor("k_s", [8, 128, S], BF16, kind=kind_of("k_s"))
    v_s = nc.dram_tensor("v_s", [8, S, 128], BF16, kind=kind_of("v_s"))
    hdn_s = nc.dram_tensor("hdn_s", [NIC, 128, 2048], BF16, kind=kind_of("hdn_s"))
    f_s = nc.dram_tensor("f_s", [NDC, 128, 2048], F32, kind=kind_of("f_s"))
    ctxm = nc.dram_tensor("ctxm", [1024, S], BF16, kind=kind_of("ctxm"))
    relr_d = nc.dram_tensor("relr_d", [4, 512], F32, kind="Internal")
    rell_d = nc.dram_tensor("rell_d", [4, 33, 128], F32, kind="Internal")
    ctxg = nc.dram_tensor("ctxg", [2048, S], BF16, kind=kind_of("ctxg"))

    ctx = ExitStack()
    with ctx:
        def sb(name, shape, dtype):
            return ctx.enter_context(nc.sbuf_tensor(name, shape, dtype))

        def sem(name):
            return Sem(ctx.enter_context(nc.semaphore(name)))

        sQ = sem("sQ")
        sG = sem("sG")
        sC = sem("sC")
        sPE = sem("sPE")
        sSC = sem("sSC")
        sVE = sem("sVE")
        sGM = sem("sGM")
        sB = sem("sB")

        bqkv_sb = sb("bqkv_sb", [128, 24], F32)
        gs_sb = sb("gs_sb", [128, NIC], F32)
        us_sb = sb("us_sb", [128, NIC], F32)
        ds_sb = sb("ds_sb", [128, NDC], F32)
        bo_sb = sb("bo_sb", [128, NDC], F32)
        sl_sb = sb("sl_sb", [2, 8], F32)
        eps_sb = sb("eps_sb", [128, 1], F32)
        ones_bf = sb("ones_bf", [128, 128], BF16)
        id_bf = sb("id_bf", [128, 128], BF16)
        id_f32 = sb("id_f32", [128, 128], F32)
        mask_bf = sb("mask_bf", [128, 4, 512], BF16)

        own_reg = ctx.enter_context(nc.gpsimd.register("own_off_r"))

        p0 = Ph(nc)
        voff = {}

        def setup_reg(g):
            pid = g.partition_id()
            g.reg_mov(own_reg, pid)
            g.reg_mod(own_reg, own_reg, 2)
            g.reg_mul(own_reg, own_reg, 2048)
            voff["v"] = g.snap(own_reg, min_val=0, max_val=2048)
            return g.nop()
        p0.do("gpsimd", setup_reg)

        for h_, t_ in [(bqkv_h, bqkv_sb), (gs_h, gs_sb), (us_h, us_sb),
                       (ds_h, ds_sb), (bo_h, bo_sb), (sl_h, sl_sb)]:
            p0.do("sync", lambda e, i=h_, o=t_: e.dma_start(out=o[...], in_=i[...]),
                  sQ, 16)
        const_ld = sQ.n

        p0.do("vector", lambda e: e.memset(eps_sb[:, :], EPS), sVE, 1)
        eps_done = sVE.n

        p0.do("gpsimd", lambda e: e.memset(ones_bf[:, :], 1.0))
        p0.do("gpsimd", lambda e: e.memset(id_bf[:, :], 0.0))
        p0.do("gpsimd", lambda e: e.affine_select(
            out=id_bf[:, :], in_=id_bf[:, :], compare_op=ALU.not_equal,
            fill=1.0, base=0, pattern=[[-1, 128]], channel_multiplier=1))
        p0.do("gpsimd", lambda e: e.memset(id_f32[:, :], 0.0))
        p0.do("gpsimd", lambda e: e.affine_select(
            out=id_f32[:, :], in_=id_f32[:, :], compare_op=ALU.not_equal,
            fill=1.0, base=0, pattern=[[-1, 128]], channel_multiplier=1))
        p0.do("gpsimd", lambda e: e.memset(mask_bf[:, :, :], 1.0))
        p0.do("gpsimd", lambda e: e.affine_select(
            out=mask_bf[:, :, :], in_=mask_bf[:, :, :], compare_op=ALU.is_ge,
            fill=0.0, base=0, pattern=[[-128, 4], [1, 512]],
            channel_multiplier=-1), sGM, 1)
        consts_done = sGM.n
        p0.barrier(sB)
        p0.run()

        if nphase >= 1:
            with ExitStack() as pc:
                def psb(name, shape, dtype):
                    return pc.enter_context(nc.sbuf_tensor(name, shape, dtype))

                def pps(name, shape, dtype):
                    return pc.enter_context(nc.psum_tensor(name, shape, dtype))

                x_sb = [psb(f"x_sb{i}", [128, 4, D], F32) for i in range(2)]
                xn_tm = [psb(f"xn_tm{i}", [128, 4, D], BF16) for i in range(2)]
                xn_fm = [psb(f"xn_fm{i}", [128, NDC, 512], BF16) for i in range(2)]
                xsq = psb("xsq", [128, 4, D], F32)
                st_sb = psb("st_sb", [128, 8, 4], F32)
                psTR = [pps(f"psTR{i}", [128, 1024], BF16) for i in range(2)]

                sX = [sem("sX0"), sem("sX1")]
                sFM = [sem("sFM0"), sem("sFM1")]
                pa = Ph(nc)
                pa.wait("scalar", sVE, eps_done)
                pa.do("scalar", lambda e: e.preload_act_table(AF.Sqrt))
                xfree = {}
                fmfree = {}
                trfree = {}
                ld_val = {}
                for tt in range(8):
                    xb = tt % 2
                    if xb in xfree:
                        pa.wait("sync", *xfree[xb])
                    xin = x_h[tt * 512:(tt + 1) * 512, :].rearrange(
                        "(p t) d -> t p d", t=128)
                    ld_val[tt] = (sX[xb], pa.do(
                        "sync", lambda e, o=x_sb[xb], i=xin:
                        e.dma_start(out=o[:, :, :], in_=i), sX[xb], 16))
                    su = st_sb[:, 0, :]
                    ssq = st_sb[:, 1, :]
                    tm = st_sb[:, 2, :]
                    vu = st_sb[:, 3, :]
                    sd = st_sb[:, 4, :]
                    rstd = st_sb[:, 5, :]
                    nmr = st_sb[:, 6, :]
                    pa.wait("scalar", *ld_val[tt])
                    if "xsq" in trfree:
                        pa.wait("scalar", *trfree["xsq"])
                    sq_done = pa.do(
                        "scalar", lambda e, o=xsq[:, :, :], i=x_sb[xb][:, :, :]:
                        e.activation(o, i, AF.Square), sSC, 1)
                    pa.wait("vector", *ld_val[tt])
                    if tt > 0:
                        pa.wait("vector", sSC, xn_prev)
                    pa.do("vector", lambda e, o=su, i=x_sb[xb][:, :, :]:
                          e.reduce_sum(o, i, AX.X), sVE, 1)
                    pa.wait("vector", sSC, sq_done)
                    r2 = pa.do("vector", lambda e, o=ssq, i=xsq[:, :, :]:
                               e.reduce_sum(o, i, AX.X), sVE, 1)
                    trfree["xsq"] = (sVE, r2)
                    pa.wait("vector", sVE, r2)
                    t3 = pa.do("vector", lambda e, o=tm, a=su:
                               e.scalar_tensor_tensor(
                                   o, a, 1.0 / D, a, op0=ALU.mult,
                                   op1=ALU.mult), sVE, 1)
                    pa.wait("vector", sVE, t3)
                    t4 = pa.do("vector", lambda e, o=vu, a=ssq, b=tm:
                               e.scalar_tensor_tensor(
                                   o, a, 1.0, b, op0=ALU.mult,
                                   op1=ALU.subtract), sVE, 1)
                    pa.wait("scalar", sVE, t4)
                    t5 = pa.do("scalar", lambda e, o=sd, i=vu:
                               e.activation(o, i, AF.Sqrt, scale=1.0 / D,
                                            bias=eps_sb[:, 0:1]), sSC, 1)
                    pa.wait("vector", sSC, t5)
                    t6 = pa.do("vector", lambda e, o=rstd, i=sd:
                               e.reciprocal(o, i), sVE, 1)
                    pa.wait("vector", sVE, t6)
                    t7 = pa.do("vector", lambda e, o=nmr, a=su, b=rstd:
                               e.scalar_tensor_tensor(
                                   o, a, -1.0 / D, b, op0=ALU.mult,
                                   op1=ALU.mult), sVE, 1)
                    pa.wait("scalar", sVE, t7)
                    xn_done = {}
                    for p in range(4):
                        xn_done[p] = pa.do(
                            "scalar", lambda e, o=xn_tm[xb][:, p, :],
                            i=x_sb[xb][:, p, :],
                            sc=rstd[:, p:p + 1], bi=nmr[:, p:p + 1]:
                            e.activation(o, i, AF.Identity, scale=sc, bias=bi),
                            sSC, 1)
                    xn_prev = xn_done[3]
                    xfree[xb] = (sSC, xn_done[3])
                    cp_done = []
                    fb = tt % 2
                    if fb in fmfree:
                        pass
                    for gi in range(8):
                        p, dch = gi // 2, gi % 2
                        tb = gi % 2
                        pa.wait("tensor", sSC, xn_done[p])
                        if ("tr", tb) in trfree:
                            pa.wait("tensor", *trfree[("tr", tb)])
                        if gi == 0 and fb in fmfree:
                            pa.wait("tensor", *fmfree[fb])
                        last = None
                        for s_ in range(8):
                            dc = dch * 8 + s_
                            is_last = s_ == 7
                            last = pa.do(
                                "tensor",
                                lambda e, o=psTR[tb][:, s_ * 128:(s_ + 1) * 128],
                                i=xn_tm[xb][:, p, dc * 128:(dc + 1) * 128]:
                                e.transpose(o, i, id_bf[:, :]),
                                sPE if is_last else None, 1)
                        pa.wait("vector", sPE, last)
                        ov = xn_fm[fb][:, dch * 8:dch * 8 + 8,
                                       p * 128:(p + 1) * 128]
                        iv = psTR[tb].ap().rearrange("p (s d) -> p s d", s=8)
                        cp = pa.do("vector", lambda e, o=ov, i=iv:
                                   e.tensor_copy(o, i), sVE, 1)
                        trfree[("tr", tb)] = (sVE, cp)
                        cp_done.append(cp)
                    pa.wait("sync", sVE, cp_done[-1])
                    xout = xn_s.ap().rearrange("a p t -> p a t")[
                        :, :, tt * 512:(tt + 1) * 512]
                    stv = pa.do("sync", lambda e, o=xout,
                                i=xn_fm[fb][:, :, :]:
                                e.dma_start(out=o, in_=i), sFM[fb], 16)
                    fmfree[fb] = (sFM[fb], stv)
                for s_e in sX + sFM:
                    pa.wait("sync", s_e, s_e.n)
                pa.barrier(sB)
                pa.run()

        if nphase >= 2:
            with ExitStack() as pc:
                def psb(name, shape, dtype):
                    return pc.enter_context(nc.sbuf_tensor(name, shape, dtype))

                def pps(name, shape, dtype):
                    return pc.enter_context(nc.psum_tensor(name, shape, dtype))

                wqkv_sb = psb("wqkv_sb", [128, NDC, 3072], BF16)
                xn_tl = [psb(f"xn_tl{i}", [128, NDC, 512], BF16) for i in range(2)]
                qkv_bf = [psb(f"qkv_bf{i}", [128, 512], BF16) for i in range(4)]
                vtok = [psb(f"vtok{i}", [128, 512], BF16) for i in range(2)]
                ps_qkv = [pps(f"ps_qkv{i}", [128, 512], F32) for i in range(2)]
                psTR = [pps(f"psTRa{i}", [128, 512], BF16) for i in range(2)]

                sWQ = sem("sWQ")
                sXN = [sem("sXN0"), sem("sXN1")]
                sST = [sem(f"sST{i}") for i in range(4)]
                sVT = [sem("sVT0"), sem("sVT1")]
                pa = Ph(nc)
                for dc in range(NDC):
                    pa.do("gpsimd", lambda e, o=wqkv_sb[:, dc, :], i=wqkv_h[dc]:
                          e.dma_start(out=o, in_=i), sWQ, 16)
                pa.wait("tensor", sWQ, sWQ.n)
                pa.wait("scalar", sQ, const_ld)
                free = {}
                for tt in range(8):
                    xb = tt % 2
                    if ("xn", xb) in free:
                        pa.wait("sync", *free[("xn", xb)])
                    xnin = xn_s.ap().rearrange("a p t -> p a t")[
                        :, :, tt * 512:(tt + 1) * 512]
                    lv = pa.do("sync", lambda e, o=xn_tl[xb], i=xnin:
                               e.dma_start(out=o[:, :, :], in_=i), sXN[xb], 16)
                    pa.wait("tensor", sXN[xb], lv)
                    for fc in range(24):
                        qb = fc % 2
                        eb = fc % 4
                        if ("ps", qb) in free:
                            pa.wait("tensor", *free[("ps", qb)])
                        mm = None
                        for dc in range(NDC):
                            mm = pa.do(
                                "tensor",
                                lambda e, o=ps_qkv[qb][:, :],
                                l=wqkv_sb[:, dc, fc * 128:(fc + 1) * 128],
                                rr=xn_tl[xb][:, dc, :], s0=(dc == 0),
                                s1=(dc == NDC - 1):
                                e.matmul(o, l, rr, start=s0, stop=s1),
                                sPE if dc == NDC - 1 else None, 1)
                        pa.wait("scalar", sPE, mm)
                        if ("eb", eb) in free:
                            pa.wait("scalar", *free[("eb", eb)])
                        ev = pa.do("scalar", lambda e, o=qkv_bf[eb][:, :],
                                   i=ps_qkv[qb][:, :], bi=bqkv_sb[:, fc:fc + 1]:
                                   e.activation(o, i, AF.Identity, scale=1.0,
                                                bias=bi), sSC, 1)
                        free[("ps", qb)] = (sSC, ev)
                        if fc < 16:
                            dst = (q_s if fc < 8 else k_s)[
                                fc % 8, :, tt * 512:(tt + 1) * 512]
                            pa.wait("sync", sSC, ev)
                            stv = pa.do("sync", lambda e, o=dst,
                                        i=qkv_bf[eb][:, :]:
                                        e.dma_start(out=o, in_=i), sST[eb], 16)
                            free[("eb", eb)] = (sST[eb], stv)
                        else:
                            hh = fc - 16
                            tb = fc % 2
                            pa.wait("tensor", sSC, ev)
                            if ("ptr", tb) in free:
                                pa.wait("tensor", *free[("ptr", tb)])
                            last = None
                            for s_ in range(4):
                                last = pa.do(
                                    "tensor",
                                    lambda e,
                                    o=psTR[tb][:, s_ * 128:(s_ + 1) * 128],
                                    i=qkv_bf[eb][:, s_ * 128:(s_ + 1) * 128]:
                                    e.transpose(o, i, id_bf[:, :]),
                                    sPE if s_ == 3 else None, 1)
                            free[("eb", eb)] = (sPE, last)
                            vb = fc % 2
                            pa.wait("vector", sPE, last)
                            if ("vt", vb) in free:
                                pa.wait("vector", *free[("vt", vb)])
                            cp = pa.do("vector", lambda e, o=vtok[vb][:, :],
                                       i=psTR[tb][:, :]:
                                       e.tensor_copy(o, i), sVE, 1)
                            free[("ptr", tb)] = (sVE, cp)
                            pa.wait("sync", sVE, cp)
                            vdst = v_s[hh, tt * 512:(tt + 1) * 512, :].rearrange(
                                "(s t) d -> t s d", t=128)
                            ivv = vtok[vb].ap().rearrange(
                                "p (s d) -> p s d", s=4)
                            stv = pa.do("sync", lambda e, o=vdst, i=ivv:
                                        e.dma_start(out=o, in_=i), sVT[vb], 16)
                            free[("vt", vb)] = (sVT[vb], stv)
                    free[("xn", xb)] = (sPE, sPE.n)
                for s_e in sXN + sST + sVT:
                    pa.wait("sync", s_e, s_e.n)
                pa.barrier(sB)
                pa.run()

        if nphase >= 3:
            with ExitStack() as pc:
                def psb(name, shape, dtype):
                    return pc.enter_context(nc.sbuf_tensor(name, shape, dtype))

                def pps(name, shape, dtype):
                    return pc.enter_context(nc.psum_tensor(name, shape, dtype))

                xn_sb = psb("xn_sbB", [128, NDC, 2048], BF16)
                gw_sb = [psb(f"gw{i}", [128, NDC, 128], BF16) for i in range(2)]
                uw_sb = [psb(f"uw{i}", [128, NDC, 128], BF16) for i in range(2)]
                gsil = [psb(f"gsil{i}", [128, 512], F32) for i in range(2)]
                hdn_bf = [psb(f"hdnb{i}", [128, 512], BF16) for i in range(2)]
                ps_g = [pps(f"ps_g{i}", [128, 512], F32) for i in range(2)]
                ps_u = [pps(f"ps_u{i}", [128, 512], F32) for i in range(2)]

                sXB = sem("sXB")
                sW = [sem("sW0"), sem("sW1")]
                sHB = [sem("sHB0"), sem("sHB1")]
                pb = Ph(nc)
                pb.do("scalar", lambda e: e.preload_act_table(AF.Silu))
                xin = xn_s.ap().rearrange("a p t -> p a t")[
                    :, :, bass.ds(voff["v"], 2048)]
                xl = pb.do("gpsimd", lambda e, o=xn_sb, i=xin:
                           e.dma_start(out=o[:, :, :], in_=i), sXB, 16)
                pb.wait("tensor", sXB, xl)
                pb.wait("scalar", sQ, const_ld)
                free = {}
                for ic in range(NIC):
                    wb = ic % 2
                    if ("w", wb) in free:
                        pb.wait("gpsimd", *free[("w", wb)])
                    gin = gate_h[ic].rearrange("a p d -> p a d")
                    uin = up_h[ic].rearrange("a p d -> p a d")
                    pb.do("gpsimd", lambda e, o=gw_sb[wb], i=gin:
                          e.dma_start(out=o[:, :, :], in_=i), sW[wb], 16)
                    wl = pb.do("gpsimd", lambda e, o=uw_sb[wb], i=uin:
                               e.dma_start(out=o[:, :, :], in_=i), sW[wb], 16)
                    pb.wait("tensor", sW[wb], wl)
                    last_mm = None
                    for tt in range(4):
                        gb = tt % 2
                        if ("pg", gb) in free:
                            pb.wait("tensor", *free[("pg", gb)])
                        mmg = None
                        for dc in range(NDC):
                            mmg = pa_mm = pb.do(
                                "tensor", lambda e, o=ps_g[gb][:, :],
                                l=gw_sb[wb][:, dc, :],
                                rr=xn_sb[:, dc, tt * 512:(tt + 1) * 512],
                                s0=(dc == 0), s1=(dc == NDC - 1):
                                e.matmul(o, l, rr, start=s0, stop=s1),
                                sPE if dc == NDC - 1 else None, 1)
                        if ("pu", gb) in free:
                            pb.wait("tensor", *free[("pu", gb)])
                        mmu = None
                        for dc in range(NDC):
                            mmu = pb.do(
                                "tensor", lambda e, o=ps_u[gb][:, :],
                                l=uw_sb[wb][:, dc, :],
                                rr=xn_sb[:, dc, tt * 512:(tt + 1) * 512],
                                s0=(dc == 0), s1=(dc == NDC - 1):
                                e.matmul(o, l, rr, start=s0, stop=s1),
                                sPE if dc == NDC - 1 else None, 1)
                        last_mm = mmu
                        sb2 = tt % 2
                        pb.wait("scalar", sPE, mmg)
                        if ("gs", sb2) in free:
                            pb.wait("scalar", *free[("gs", sb2)])
                        sl = pb.do("scalar", lambda e, o=gsil[sb2][:, :],
                                   i=ps_g[gb][:, :], sc=gs_sb[:, ic:ic + 1]:
                                   e.activation(o, i, AF.Silu, scale=sc),
                                   sSC, 1)
                        free[("pg", gb)] = (sSC, sl)
                        hb2 = tt % 2
                        pb.wait("vector", sSC, sl)
                        pb.wait("vector", sPE, mmu)
                        if ("hb", hb2) in free:
                            pb.wait("vector", *free[("hb", hb2)])
                        hv = pb.do("vector", lambda e, o=hdn_bf[hb2][:, :],
                                   i=ps_u[gb][:, :], sc=us_sb[:, ic:ic + 1],
                                   g2=gsil[sb2][:, :]:
                                   e.scalar_tensor_tensor(
                                       o, i, sc, g2, op0=ALU.mult,
                                       op1=ALU.mult), sVE, 1)
                        free[("pu", gb)] = (sVE, hv)
                        free[("gs", sb2)] = (sVE, hv)
                        pb.wait("sync", sVE, hv)
                        stv = pb.do("sync", lambda e,
                                    o=hdn_s[ic, :, tt * 512:(tt + 1) * 512],
                                    i=hdn_bf[hb2][:, :]:
                                    e.dma_start(out=o, in_=i), sHB[hb2], 16)
                        free[("hb", hb2)] = (sHB[hb2], stv)
                    free[("w", wb)] = (sPE, last_mm)
                for s_e in sHB:
                    pb.wait("sync", s_e, s_e.n)
                pb.wait("gpsimd", sXB, sXB.n)
                for s_e in sW:
                    pb.wait("gpsimd", s_e, s_e.n)
                pb.barrier(sB)
                pb.run()

        if nphase >= 4:
            with ExitStack() as pc:
                def psb(name, shape, dtype):
                    return pc.enter_context(nc.sbuf_tensor(name, shape, dtype))

                def pps(name, shape, dtype):
                    return pc.enter_context(nc.psum_tensor(name, shape, dtype))

                k_sb = [psb(f"k_sb{i}", [128, S], BF16) for i in range(2)]
                q_sb = [psb(f"q_sb{i}", [128, S], BF16) for i in range(2)]
                v_sb = [psb(f"v_sb{i}", [128, 32, 128], BF16) for i in range(2)]
                rhsRel = psb("rhsRel", [4, 512], BF16)
                lhsRel = psb("lhsRel", [4, 33, 128], BF16)
                slb = psb("slb", [128, 8], F32)
                rsl = psb("rsl", [128, 8], F32)
                slTSC = psb("slTSC", [128, 8], F32)
                ones1 = psb("ones1", [1, 128], F32)
                wspA = psb("wspA", [1, 512], F32)
                wspH = psb("wspH", [1, 512], F32)
                wspB = psb("wspB", [1, 512], F32)
                wspL = psb("wspL", [1, 33, 128], F32)
                tn = [psb(f"tn{i}", [128, 1024], F32) for i in range(2)]
                pr = [psb(f"pr{i}", [128, 1024], BF16) for i in range(4)]
                rc = [psb(f"rc{i}", [128, 512], F32) for i in range(2)]
                cbf = [psb(f"cbf{i}", [128, 512], BF16) for i in range(2)]
                dw_q = psb("dw_q", [128, 2, 22, 512], BF16)
                hdn_fl = psb("hdn_fl", [128, 2, 22, 512], BF16)
                fD = [psb(f"fD{i}", [128, 512], F32) for i in range(2)]
                psS = [pps(f"psS{i}", [128, 1024], F32) for i in range(2)]
                psCx = pps("psCx", [128, 512], F32)
                psDn = pps("psDn", [128, 512], F32)
                psD = [pps(f"psD{i}", [128, 512], F32) for i in range(2)]

                sHL = [sem("sHL0"), sem("sHL1")]
                sCS = [sem("sCS0"), sem("sCS1")]
                sRELs = sem("sRELs")
                sRELl = sem("sRELl")
                pcx = Ph(nc)
                pcx.do("scalar", lambda e: e.preload_act_table(AF.Tanh))
                pcx.wait("gpsimd", sGM, consts_done)
                pcx.wait("vector", sQ, const_ld)
                g1 = pcx.do("gpsimd", lambda e: e.memset(ones1[:, :], 1.0),
                            sGM, 1)
                pcx.wait("tensor", sGM, g1)
                pcx.wait("tensor", sQ, const_ld)
                bmm = pcx.do("tensor", lambda e: e.matmul(
                    psCx[:, 0:8], ones1[0:1, :], sl_sb[0:1, 0:8],
                    start=True, stop=True, skip_group_check=True), sPE, 1)
                pcx.wait("vector", sPE, bmm)
                v1 = pcx.do("vector", lambda e: e.tensor_copy(
                    slb[:, :], psCx[:, 0:8]), sVE, 1)
                pcx.wait("vector", sVE, v1)
                v2 = pcx.do("vector", lambda e: e.reciprocal(
                    rsl[:, :], slb[:, :]), sVE, 1)
                pcx.wait("vector", sVE, v2)
                v3 = pcx.do("vector", lambda e: e.tensor_scalar_mul(
                    slTSC[:, :], slb[:, :], TSC), sVE, 1)
                setup_norm = v1
                gi1 = pcx.do("gpsimd", lambda e: e.iota(
                    wspA[:, :], [[1, 512]], channel_multiplier=0,
                    allow_small_or_imprecise_dtypes=True))
                pcx.do("gpsimd", lambda e: e.memset(wspH[:, :], 1.0))
                gi2 = pcx.do("gpsimd", lambda e: e.affine_select(
                    out=wspH[:, :], in_=wspH[:, :], compare_op=ALU.is_ge,
                    fill=0.0, base=-256, pattern=[[1, 512]],
                    channel_multiplier=0), sGM, 1)
                pcx.wait("vector", sGM, gi2)
                v4 = pcx.do("vector", lambda e: e.scalar_tensor_tensor(
                    wspB[:, :], wspH[:, :], 256.0, wspA[:, :],
                    op0=ALU.mult, op1=ALU.subtract), sVE, 1)
                pcx.wait("vector", sVE, v4)
                v5 = pcx.do("vector", lambda e: e.tensor_scalar_mul(
                    wspA[:, :], wspH[:, :], -256.0), sVE, 1)
                pcx.wait("vector", sVE, v5)
                v6 = pcx.do("vector", lambda e: e.memset(wspH[:, :], 1.0),
                            sVE, 1)
                pcx.wait("sync", sVE, v6)
                pcx.do("sync", lambda e: e.dma_start(
                    out=relr_d[0:1, :], in_=wspH[0:1, :]), sRELs, 16)
                pcx.do("sync", lambda e: e.dma_start(
                    out=relr_d[1:2, :], in_=wspB[0:1, :]), sRELs, 16)
                pcx.do("sync", lambda e: e.dma_start(
                    out=relr_d[2:3, :], in_=wspA[0:1, :]), sRELs, 16)
                pcx.do("sync", lambda e: e.dma_start(
                    out=relr_d[3:4, :], in_=wspH[0:1, :]), sRELs, 16)
                gi3 = pcx.do("gpsimd", lambda e: e.iota(
                    wspL[:, :, :], [[0, 33], [1, 128]], channel_multiplier=0,
                    allow_small_or_imprecise_dtypes=True), sGM, 1)
                pcx.wait("sync", sGM, gi3)
                pcx.do("sync", lambda e: e.dma_start(
                    out=rell_d[0:1, :, :], in_=wspL[0:1, :, :]), sRELs, 16)
                pcx.wait("gpsimd", sRELs, sRELs.n)
                gi4 = pcx.do("gpsimd", lambda e: e.iota(
                    wspL[:, :, :], [[128, 33], [0, 128]],
                    channel_multiplier=0,
                    allow_small_or_imprecise_dtypes=True), sGM, 1)
                pcx.wait("vector", sGM, gi4)
                v7 = pcx.do("vector", lambda e: e.tensor_scalar_add(
                    wspL[:, :, :], wspL[:, :, :], -3584.0), sVE, 1)
                pcx.wait("sync", sVE, v7)
                pcx.do("sync", lambda e: e.dma_start(
                    out=rell_d[3:4, :, :], in_=wspL[0:1, :, :]), sRELs, 16)
                pcx.wait("vector", sRELs, sRELs.n)
                v8 = pcx.do("vector", lambda e: e.memset(wspL[:, :, :], 1.0),
                            sVE, 1)
                pcx.wait("sync", sVE, v8)
                pcx.do("sync", lambda e: e.dma_start(
                    out=rell_d[1:2, :, :], in_=wspL[0:1, :, :]), sRELs, 16)
                pcx.do("sync", lambda e: e.dma_start(
                    out=rell_d[2:3, :, :], in_=wspL[0:1, :, :]), sRELs, 16)
                pcx.wait("gpsimd", sRELs, sRELs.n)
                pcx.do("gpsimd", lambda e: e.dma_start(
                    out=rhsRel[:, :], in_=relr_d[:, :]), sRELl, 16)
                rel_ld = pcx.do("gpsimd", lambda e: e.dma_start(
                    out=lhsRel[:, :, :], in_=rell_d[:, :, :]), sRELl, 16)
                pcx.wait("tensor", sRELl, rel_ld)
                grp_free = {}
                pr_free = {}
                head_free = {}
                norm_done_val = setup_norm
                gi_ctr = 0
                sDWh = [sem("sDWh0"), sem("sDWh1")]
                sHFh = [sem("sHFh0"), sem("sHFh1")]
                sFS = [sem("sFS0"), sem("sFS1")]
                dstate = {"dw_war": {}, "hdn_war": {}, "dw_ld": {},
                          "hdn_ld": {}, "evac": {}, "fwar": {},
                          "last_mm": None}

                def mk_dw_load(qr, hf):
                    def go():
                        if hf in dstate["dw_war"]:
                            pcx.wait("gpsimd", sPE, dstate["dw_war"][hf])
                        lv = None
                        for icl in range(22):
                            ic = hf * 22 + icl
                            din = down_h[ic, :, qr * 512:(qr + 1) * 512]
                            lv = pcx.do("gpsimd",
                                        lambda e, o=dw_q[:, hf, icl, :],
                                        i=din: e.dma_start(out=o, in_=i),
                                        sDWh[hf], 16)
                        dstate["dw_ld"][hf] = lv
                    return go

                def mk_hdn_load(qr, tt, hf):
                    def go():
                        if hf in dstate["hdn_war"]:
                            pcx.wait("sync", sPE, dstate["hdn_war"][hf])
                        hin = hdn_s[hf * 22:hf * 22 + 22, :,
                                    tt * 512:(tt + 1) * 512].rearrange(
                                        "a p d -> p a d")
                        lv = pcx.do("sync", lambda e,
                                    o=hdn_fl[:, hf, :, :], i=hin:
                                    e.dma_start(out=o, in_=i), sHFh[hf], 16)
                        dstate["hdn_ld"][hf] = lv
                    return go

                def mk_mms(qr, tt, pas, hf, ic0, icn):
                    def go():
                        if ic0 == 0:
                            pcx.wait("tensor", sDWh[hf], dstate["dw_ld"][hf])
                            pcx.wait("tensor", sHFh[hf], dstate["hdn_ld"][hf])
                            if hf == 0 and pas == 0:
                                pass
                            if hf == 0:
                                for m in range(2):
                                    if m in dstate["evac"]:
                                        pcx.wait("tensor", sVE,
                                                 dstate["evac"][m])
                        lm = None
                        for icl in range(ic0, ic0 + icn):
                            for m in range(2):
                                mc_loc = 2 * pas + m
                                st0 = (hf == 0 and icl == 0)
                                st1 = (hf == 1 and icl == 21)
                                lm = pcx.do(
                                    "tensor", lambda e, o=psD[m][:, :],
                                    l=dw_q[:, hf, icl,
                                           mc_loc * 128:(mc_loc + 1) * 128],
                                    rr=hdn_fl[:, hf, icl, :],
                                    s0=st0, s1=st1:
                                    e.matmul(o, l, rr, start=s0, stop=s1,
                                             skip_group_check=True),
                                    sPE, 1)
                        dstate["last_mm"] = lm
                        dstate["dw_war"][hf] = lm
                        dstate["hdn_war"][hf] = lm
                    return go

                def mk_evac(qr, tt, pas):
                    def go():
                        pcx.wait("vector", sPE, dstate["last_mm"])
                        for m in range(2):
                            mc_g = qr * 4 + 2 * pas + m
                            if m in dstate["fwar"]:
                                pcx.wait("vector", sFS[m], dstate["fwar"][m])
                            ev = pcx.do(
                                "vector", lambda e, o=fD[m][:, :],
                                i=psD[m][:, :],
                                sc=ds_sb[:, mc_g:mc_g + 1]:
                                e.tensor_single_scalar(o, i, sc, ALU.mult),
                                sVE, 1)
                            dstate["evac"][m] = ev
                            pcx.wait("sync", sVE, ev)
                            stv = pcx.do(
                                "sync", lambda e,
                                o=f_s[mc_g, :, tt * 512:(tt + 1) * 512],
                                i=fD[m][:, :]:
                                e.dma_start(out=o, in_=i), sFS[m], 16)
                            dstate["fwar"][m] = stv
                    return go

                down_steps = []
                for qr in range(4):
                    down_steps.append(mk_dw_load(qr, 0))
                    down_steps.append(mk_dw_load(qr, 1))
                    for tt in range(4):
                        down_steps.append(mk_hdn_load(qr, tt, 0))
                        down_steps.append(mk_hdn_load(qr, tt, 1))
                        for pas in range(2):
                            for hf in range(2):
                                down_steps.append(mk_mms(qr, tt, pas, hf, 0, 11))
                                down_steps.append(mk_mms(qr, tt, pas, hf, 11, 11))
                            down_steps.append(mk_evac(qr, tt, pas))
                dtot = len(down_steps)
                ddone = [0]
                batch_ctr = [0]

                def pace_down(final=False):
                    if final:
                        tgt = dtot
                    else:
                        tgt = min(dtot, int(dtot * batch_ctr[0] / 576.0))
                    while ddone[0] < tgt:
                        down_steps[ddone[0]]()
                        ddone[0] += 1
                for h in range(8):
                    hb = h % 2
                    if hb in head_free:
                        pcx.wait("sync", *head_free[hb])
                    pcx.do("sync", lambda e, o=k_sb[hb][:, :], i=k_s[h]:
                           e.dma_start(out=o, in_=i), sHL[hb], 16)
                    pcx.do("sync", lambda e, o=q_sb[hb][:, :], i=q_s[h]:
                           e.dma_start(out=o, in_=i), sHL[hb], 16)
                    vin = v_s[h].rearrange("(s t) d -> t s d", t=128)
                    hl = pcx.do("sync", lambda e, o=v_sb[hb][:, :, :], i=vin:
                                e.dma_start(out=o, in_=i), sHL[hb], 16)
                    pcx.wait("vector", sHL[hb], hl)
                    pcx.wait("vector", sVE, v3)
                    qpv = pcx.do("vector", lambda e, o=q_sb[hb][:, :],
                                 sc=rsl[:, h:h + 1]:
                                 e.tensor_single_scalar(o, o, sc, ALU.mult),
                                 sVE, 1)
                    pcx.wait("tensor", sHL[hb], hl)
                    pcx.wait("tensor", sVE, qpv)
                    h_last_pe = None
                    for qt in range(8):
                        nb = 2 * (qt + 1)
                        pend = []
                        for j in range(nb):
                            g = gi_ctr % 2
                            pb_ = gi_ctr % 4
                            gi_ctr += 1
                            if g in grp_free:
                                pcx.wait("tensor", sSC, grp_free[g])
                            sc_done = [None, None]
                            for u in range(2):
                                kt = 2 * j + u
                                pcx.do("tensor", lambda e,
                                       o=psS[g][:, u * 512:(u + 1) * 512],
                                       l=k_sb[hb][:, kt * 128:(kt + 1) * 128],
                                       rr=q_sb[hb][:, qt * 512:(qt + 1) * 512]:
                                       e.matmul(o, l, rr, start=True,
                                                stop=False,
                                                skip_group_check=True))
                                didx = kt - 4 * qt + 28
                                sc_done[u] = pcx.do(
                                    "tensor", lambda e,
                                    o=psS[g][:, u * 512:(u + 1) * 512],
                                    l=lhsRel[0:4, didx, :],
                                    rr=rhsRel[0:4, :]:
                                    e.matmul(o, l, rr, start=False, stop=True,
                                             skip_group_check=True),
                                    sPE, 1)
                            diag = j >= nb - 2
                            off = 0 if j == nb - 2 else 2
                            pv = [None, None]
                            mv = [None, None]
                            for u in range(2):
                                pcx.wait("scalar", sPE, sc_done[u])
                                tv = pcx.do(
                                    "scalar", lambda e,
                                    o=tn[g][:, u * 512:(u + 1) * 512],
                                    i=psS[g][:, u * 512:(u + 1) * 512],
                                    sc=slTSC[:, h:h + 1]:
                                    e.activation(o, i, AF.Tanh, scale=sc),
                                    sSC, 1)
                                if u == 1:
                                    grp_free[g] = tv
                                if u == 0 and pb_ in pr_free:
                                    pcx.wait("scalar", sPE, pr_free[pb_])
                                pv[u] = pcx.do(
                                    "scalar", lambda e,
                                    o=pr[pb_][:, u * 512:(u + 1) * 512],
                                    i=tn[g][:, u * 512:(u + 1) * 512]:
                                    e.activation(o, i, AF.Exp, scale=30.0),
                                    sSC, 1)
                                if diag:
                                    pcx.wait("vector", sSC, pv[u])
                                    mv[u] = pcx.do(
                                        "vector", lambda e,
                                        o=pr[pb_][:, u * 512:(u + 1) * 512],
                                        m=mask_bf[:, off + u, :]:
                                        e.tensor_tensor(o, o, m, ALU.mult),
                                        sVE, 1)
                            pend.append((j, g, pb_, pv, mv))
                            while pend and (len(pend) > 1 or j == nb - 1):
                                jj, gg, pp2, pvv, mvv = pend.pop(0)
                                if jj == 0 and norm_done_val is not None:
                                    pcx.wait("tensor", sVE, norm_done_val)
                                for u in range(2):
                                    if mvv[u] is not None:
                                        pcx.wait("tensor", sVE, mvv[u])
                                    else:
                                        pcx.wait("tensor", sSC, pvv[u])
                                    kt = 2 * jj + u
                                    st0 = (jj == 0 and u == 0)
                                    st1 = (jj == nb - 1 and u == 1)
                                    pcx.do("tensor", lambda e, o=psCx[:, :],
                                           l=v_sb[hb][:, kt, :],
                                           rr=pr[pp2][:, u * 512:(u + 1) * 512],
                                           s0=st0, s1=st1:
                                           e.matmul(o, l, rr, start=s0,
                                                    stop=s1,
                                                    skip_group_check=True))
                                cd = None
                                for u in range(2):
                                    st0 = (jj == 0 and u == 0)
                                    st1 = (jj == nb - 1 and u == 1)
                                    cd = pcx.do(
                                        "tensor", lambda e, o=psDn[:, :],
                                        l=ones_bf[:, :],
                                        rr=pr[pp2][:, u * 512:(u + 1) * 512],
                                        s0=st0, s1=st1:
                                        e.matmul(o, l, rr, start=s0, stop=s1,
                                                 skip_group_check=True),
                                        sPE if u == 1 else None, 1)
                                pr_free[pp2] = cd
                                h_last_pe = cd
                                batch_ctr[0] += 1
                                pace_down()
                        qb2 = (h * 8 + qt) % 2
                        pcx.wait("vector", sPE, h_last_pe)
                        if ("cbf", qb2) in pr_free:
                            pcx.wait("vector", sCS[qb2], pr_free[("cbf", qb2)])
                        pcx.do("vector", lambda e, o=rc[qb2][:, :],
                               i=psDn[:, :]: e.reciprocal(o, i))
                        nv = pcx.do("vector", lambda e, o=cbf[qb2][:, :],
                                    a=psCx[:, :], b=rc[qb2][:, :]:
                                    e.tensor_mul(o, a, b), sVE, 1)
                        norm_done_val = nv
                        pcx.wait("sync", sVE, nv)
                        stc = pcx.do("sync", lambda e,
                               o=ctxm[h * 128:(h + 1) * 128,
                                      qt * 512:(qt + 1) * 512],
                               i=cbf[qb2][:, :]:
                               e.dma_start(out=o, in_=i), sCS[qb2], 16)
                        pr_free[("cbf", qb2)] = stc
                    head_free[hb] = (sPE, h_last_pe)
                pace_down(final=True)
                pcx.wait("gpsimd", sCS[0], sCS[0].n)
                pcx.wait("gpsimd", sCS[1], sCS[1].n)
                if dump != "ctxm":
                    pcx.do("gpsimd", lambda e: e.collective_compute(
                    "AllGather", ALU.bypass,
                    replica_groups=[[0, 1], [2, 3], [4, 5], [6, 7]],
                        ins=[ctxm.ap().opt()],
                        outs=[ctxg.ap().opt()]), sC, 1)
                for s_e in sHL + sCS + [sHFh[0], sHFh[1], sFS[0], sFS[1]]:
                    pcx.wait("sync", s_e, s_e.n)
                for s_e in sDWh:
                    pcx.wait("gpsimd", s_e, s_e.n)
                pcx.barrier(sB)
                pcx.run()

        if False:
            with ExitStack() as pc:
                def psb(name, shape, dtype):
                    return pc.enter_context(nc.sbuf_tensor(name, shape, dtype))

                def pps(name, shape, dtype):
                    return pc.enter_context(nc.psum_tensor(name, shape, dtype))

                dw_sb = psb("dw_sb", [128, NIC, 1024], BF16)
                hdn_fl = psb("hdn_fl", [128, NIC, 512], BF16)
                f_buf = [psb(f"f_buf{i}", [128, 512], F32) for i in range(2)]
                ps_dn = [pps(f"ps_dn{i}", [128, 512], F32) for i in range(8)]

                sDW = sem("sDW")
                sHF = sem("sHF")
                sFS = [sem("sFS0"), sem("sFS1")]
                pc2 = Ph(nc)
                free = {}
                for half in range(2):
                    if "dw" in free:
                        pc2.wait("gpsimd", *free["dw"])
                    dl = None
                    for ic in range(NIC):
                        din = down_h[ic, :, half * 1024:(half + 1) * 1024]
                        dl = pc2.do("gpsimd", lambda e, o=dw_sb[:, ic, :],
                                    i=din: e.dma_start(out=o, in_=i), sDW, 16)
                    pc2.wait("tensor", sDW, dl)
                    last_mm = None
                    for tt in range(4):
                        if "hdn" in free:
                            pc2.wait("sync", *free["hdn"])
                        hin = hdn_s[:, :, tt * 512:(tt + 1) * 512].rearrange(
                            "a p d -> p a d")
                        hl = pc2.do("sync", lambda e, o=hdn_fl[:, :, :], i=hin:
                                    e.dma_start(out=o, in_=i), sHF, 16)
                        pc2.wait("tensor", sHF, hl)
                        if "ev" in free:
                            pc2.wait("tensor", *free["ev"])
                        for ic in range(NIC):
                            for mc in range(8):
                                is_l = ic == NIC - 1 and mc == 7
                                last_mm = pc2.do(
                                    "tensor", lambda e, o=ps_dn[mc][:, :],
                                    l=dw_sb[:, ic, mc * 128:(mc + 1) * 128],
                                    rr=hdn_fl[:, ic, :], s0=(ic == 0),
                                    s1=(ic == NIC - 1):
                                    e.matmul(o, l, rr, start=s0, stop=s1,
                                             skip_group_check=True),
                                    sPE if is_l else None, 1)
                        free["hdn"] = (sPE, last_mm)
                        pc2.wait("vector", sPE, last_mm)
                        evs = []
                        for mc in range(8):
                            fb2 = mc % 2
                            if ("f", fb2) in free:
                                pc2.wait("vector", *free[("f", fb2)])
                            ev = pc2.do(
                                "vector", lambda e, o=f_buf[fb2][:, :],
                                i=ps_dn[mc][:, :],
                                sc=ds_sb[:, half * 8 + mc:half * 8 + mc + 1]:
                                e.tensor_single_scalar(o, i, sc, ALU.mult),
                                sVE, 1)
                            evs.append(ev)
                            pc2.wait("sync", sVE, ev)
                            stv = pc2.do(
                                "sync", lambda e,
                                o=f_s[half * 8 + mc, :,
                                      tt * 512:(tt + 1) * 512],
                                i=f_buf[fb2][:, :]:
                                e.dma_start(out=o, in_=i), sFS[fb2], 16)
                            free[("f", fb2)] = (sFS[fb2], stv)
                        free["ev"] = (sVE, evs[-1])
                    free["dw"] = (sPE, last_mm)
                for s_e in [sHF] + sFS:
                    pc2.wait("sync", s_e, s_e.n)
                pc2.wait("gpsimd", sDW, sDW.n)
                pc2.barrier(sB)
                pc2.run()

        if nphase >= 5:
            with ExitStack() as pc:
                def psb(name, shape, dtype):
                    return pc.enter_context(nc.sbuf_tensor(name, shape, dtype))

                def pps(name, shape, dtype):
                    return pc.enter_context(nc.psum_tensor(name, shape, dtype))

                wo_sb = psb("wo_sb", [128, NDC, 2048], BF16)
                ctx_tl = [psb(f"ctx_tl{i}", [128, NDC, 512], BF16)
                          for i in range(2)]
                x_tl = psb("x_tl", [128, 4, D], F32)
                out_sb = psb("out_sb", [128, 4, D], F32)
                f_tl = [psb(f"f_tl{i}", [128, 512], F32) for i in range(4)]
                gsil = [psb(f"gsilD{i}", [128, 512], F32) for i in range(2)]
                sg = [psb(f"sg{i}", [128, 512], F32) for i in range(2)]
                ps_s = [pps(f"ps_s{i}", [128, 512], F32) for i in range(2)]
                psTRf = [pps(f"psTRf{i}", [128, 512], F32) for i in range(2)]

                sWO = sem("sWO")
                sCT = sem("sCT")
                sXD = sem("sXD")
                sFL = [sem(f"sFL{i}") for i in range(4)]
                sOS = sem("sOS")
                pd = Ph(nc)
                pd.do("scalar", lambda e: e.preload_act_table(AF.Silu))
                for dc in range(NDC):
                    pd.do("gpsimd", lambda e, o=wo_sb[:, dc, :], i=wo_h[dc]:
                          e.dma_start(out=o, in_=i), sWO, 16)
                pd.wait("gpsimd", sC, 1)
                pd.wait("tensor", sWO, sWO.n)
                free = {}
                for tt in range(4):
                    cb_ = tt % 2
                    if ("ctx", cb_) in free:
                        pd.wait("gpsimd", *free[("ctx", cb_)])
                    cin = ctxg.ap().rearrange("(a p) t -> p a t", p=128)[
                        :, :, bass.ds(voff["v"], 2048)][
                        :, :, tt * 512:(tt + 1) * 512]
                    cl = pd.do("gpsimd", lambda e, o=ctx_tl[cb_][:, :, :],
                               i=cin: e.dma_start(out=o, in_=i), sCT, 16)
                    if "x" in free:
                        pd.wait("gpsimd", *free["x"])
                    xin = x_h.ap()[bass.ds(voff["v"], 2048), :].rearrange(
                        "(a p t) d -> t a p d", p=4, t=128)
                    xl = pd.do("gpsimd", lambda e, o=x_tl[:, :, :],
                               i=xin[:, tt, :, :]:
                               e.dma_start(out=o, in_=i), sXD, 16)
                    pd.wait("tensor", sCT, cl)
                    ocp = []
                    tail = []
                    for mc in range(NDC + 1):
                        if mc < NDC:
                            pb2 = mc % 2
                            fb4 = mc % 4
                            if ("f", fb4) in free:
                                pd.wait("sync", *free[("f", fb4)])
                            fl = pd.do("sync", lambda e, o=f_tl[fb4][:, :],
                                       i=f_s[mc, :, tt * 512:(tt + 1) * 512]:
                                       e.dma_start(out=o, in_=i), sFL[fb4], 16)
                            if ("ps", pb2) in free:
                                pd.wait("tensor", *free[("ps", pb2)])
                            mm = None
                            for kc in range(NDC):
                                mm = pd.do(
                                    "tensor", lambda e, o=ps_s[pb2][:, :],
                                    l=wo_sb[:, kc, mc * 128:(mc + 1) * 128],
                                    rr=ctx_tl[cb_][:, kc, :], s0=(kc == 0),
                                    s1=(kc == NDC - 1):
                                    e.matmul(o, l, rr, start=s0, stop=s1),
                                    sPE if kc == NDC - 1 else None, 1)
                            pd.wait("scalar", sFL[fb4], fl)
                            if ("gs", pb2) in free:
                                pd.wait("scalar", *free[("gs", pb2)])
                            sl = pd.do("scalar", lambda e, o=gsil[pb2][:, :],
                                       i=f_tl[fb4][:, :]:
                                       e.activation(o, i, AF.Silu), sSC, 1)
                            free[("f", fb4)] = (sSC, sl)
                            pd.wait("vector", sPE, mm)
                            pd.wait("vector", sSC, sl)
                            if ("sg", pb2) in free:
                                pd.wait("vector", *free[("sg", pb2)])
                            sv = pd.do("vector", lambda e, o=sg[pb2][:, :],
                                       i=ps_s[pb2][:, :],
                                       bi=bo_sb[:, mc:mc + 1],
                                       g2=gsil[pb2][:, :]:
                                       e.scalar_tensor_tensor(
                                           o, i, bi, g2, op0=ALU.add,
                                           op1=ALU.mult), sVE, 1)
                            free[("ps", pb2)] = (sVE, sv)
                            free[("gs", pb2)] = (sVE, sv)
                            tail.append((mc, pb2, sv))
                        if tail and (len(tail) > 1 or mc == NDC):
                            mcp, pbp, svp = tail.pop(0)
                            pd.wait("tensor", sVE, svp)
                            if ("tr", pbp) in free:
                                pd.wait("tensor", *free[("tr", pbp)])
                            last = None
                            for s_ in range(4):
                                last = pd.do(
                                    "tensor", lambda e,
                                    o=psTRf[pbp][:, s_ * 128:(s_ + 1) * 128],
                                    i=sg[pbp][:, s_ * 128:(s_ + 1) * 128]:
                                    e.transpose(o, i, id_f32[:, :]),
                                    sPE if s_ == 3 else None, 1)
                            free[("sg", pbp)] = (sPE, last)
                            pd.wait("vector", sPE, last)
                            if mcp < 2 and "out" in free:
                                pd.wait("vector", *free["out"])
                            ov = out_sb[:, :, mcp * 128:(mcp + 1) * 128]
                            iv = psTRf[pbp].ap().rearrange(
                                "p (s d) -> p s d", s=4)
                            cp = pd.do("vector", lambda e, o=ov, i=iv:
                                       e.tensor_copy(o, i), sVE, 1)
                            free[("tr", pbp)] = (sVE, cp)
                            ocp.append(cp)
                    pd.wait("vector", sXD, xl)
                    av = pd.do("vector", lambda e, o=out_sb[:, :, :],
                               a=out_sb[:, :, :], b=x_tl[:, :, :]:
                               e.tensor_add(o, a, b), sVE, 1)
                    free["x"] = (sVE, av)
                    pd.wait("sync", sVE, av)
                    odst = out_h[tt * 512:(tt + 1) * 512, :].rearrange(
                        "(p t) d -> t p d", t=128)
                    ost = pd.do("sync", lambda e, o=odst, i=out_sb[:, :, :]:
                                e.dma_start(out=o, in_=i), sOS, 16)
                    free["out"] = (sOS, ost)
                    free[("ctx", cb_)] = (sPE, sPE.n)
                pd.wait("sync", sOS, sOS.n)
                for s_e in sFL:
                    pd.wait("sync", s_e, s_e.n)
                for s_e in (sWO, sCT, sXD):
                    pd.wait("gpsimd", s_e, s_e.n)
                pd.barrier(sB)
                pd.run()
        else:
            pz = Ph(nc)
            pz.wait("sync", sQ, sQ.n)
            pz.barrier(sB)
            pz.run()

    return nc


def prep_inputs(c, inputs):
    b, r = c // 2, c % 2
    hs = slice(1024 * r, 1024 * r + 1024)
    Wq = inputs["L0_attn_Wq"]
    Wk = inputs["L0_attn_Wk"]
    Wv = inputs["L0_attn_Wv"]
    w_cat = np.concatenate([Wq[hs], Wk[hs], Wv[hs]], 0)
    b_cat = np.concatenate([inputs["L0_attn_bq"][hs], inputs["L0_attn_bk"][hs],
                            inputs["L0_attn_bv"][hs]])
    gate = inputs["L0_ffn_gate_w"]
    up = inputs["L0_ffn_up_w"]
    down = inputs["L0_ffn_down_w"]
    slopes = inputs["L0_attn_alibi_slopes"][8 * r:8 * r + 8]
    return {
        "x": np.ascontiguousarray(inputs["L0_attn_hidden_states"][b]),
        "wqkvT": np.ascontiguousarray(w_cat.T.reshape(NDC, 128, 3072)),
        "bqkv": np.ascontiguousarray(b_cat.reshape(24, 128).T),
        "woT": np.ascontiguousarray(
            inputs["L0_attn_Wo"].T.reshape(NDC, 128, D)),
        "bo": np.ascontiguousarray(inputs["L0_attn_bo"].reshape(NDC, 128).T),
        "gateT": np.ascontiguousarray(
            gate.T.reshape(NDC, 128, NIC, 128).transpose(2, 0, 1, 3)),
        "upT": np.ascontiguousarray(
            up.T.reshape(NDC, 128, NIC, 128).transpose(2, 0, 1, 3)),
        "downT": np.ascontiguousarray(down.T.reshape(NIC, 128, D)),
        "gsc": np.ascontiguousarray(
            inputs["L0_ffn_gate_s"].reshape(NIC, 128).T),
        "usc": np.ascontiguousarray(
            inputs["L0_ffn_up_s"].reshape(NIC, 128).T),
        "dsc": np.ascontiguousarray(
            inputs["L0_ffn_down_s"].reshape(NDC, 128).T),
        "slopes2": np.ascontiguousarray(
            np.stack([slopes, slopes], 0)),
    }


def kernel(**inputs):
    nc = build_nc()
    in_maps = [prep_inputs(c, inputs) for c in range(8)]
    res = run_bass_kernel_spmd(nc, in_maps, core_ids=list(range(8)))
    out = np.empty((B, S, D), dtype=np.float32)
    for c in range(8):
        b, r = c // 2, c % 2
        out[b, 2048 * r:2048 * r + 2048, :] = res.results[c]["out"]
    return out


# version 07
# speedup vs baseline: 1.1863x; 1.1469x over previous
import math
from contextlib import ExitStack

import numpy as np

import concourse.bass as bass
import concourse.mybir as mybir
from concourse.bass_utils import run_bass_kernel_spmd

F32 = mybir.dt.float32
BF16 = mybir.dt.bfloat16
I32 = mybir.dt.int32
AF = mybir.ActivationFunctionType
ALU = mybir.AluOpType
AX = mybir.AxisListType

B, S, D, H, HD = 4, 4096, 2048, 16, 128
NI = 5632
NIC = NI // 128
NDC = D // 128
EPS = 1e-5
TSC = 1.0 / (30.0 * math.sqrt(HD))

ENGS = ("sync", "gpsimd", "scalar", "vector", "tensor")


class Sem:
    __slots__ = ("h", "n")

    def __init__(self, h):
        self.h = h
        self.n = 0


class Ph:

    def __init__(self, nc):
        self.nc = nc
        self.st = {e: [] for e in ENGS}

    def wait(self, eng, sem, val=None):
        v = sem.n if val is None else val
        if v and v > 0:
            self.st[eng].append(lambda e, h=sem.h, vv=v: e.wait_ge(h, vv))

    def do(self, eng, build, sem=None, k=1):
        if sem is None:
            self.st[eng].append(lambda e, b=build: b(e))
            return None
        sem.n += k
        self.st[eng].append(
            lambda e, b=build, h=sem.h, kk=k: b(e).then_inc(h, kk))
        return sem.n

    def barrier(self, sB):
        for e in ENGS:
            self.do(e, lambda en: en.nop(), sB, 1)
        tgt = sB.n
        for e in ENGS:
            self.wait(e, sB, tgt)

    def run(self):
        with self.nc.Block() as blk:
            for e in ENGS:
                fns = self.st[e]
                if fns:
                    getattr(blk, e)(lambda eng, fns=fns: [f(eng) for f in fns])


def build_nc(dump=None, nphase=99):
    nc = bass.Bass("TRN2", target_bir_lowering=False)

    def kind_of(name):
        return "ExternalOutput" if dump == name else "Internal"

    x_h = nc.dram_tensor("x", [S, D], F32, kind="ExternalInput")
    wqkv_h = nc.dram_tensor("wqkvT", [NDC, 128, 3072], F32, kind="ExternalInput")
    bqkv_h = nc.dram_tensor("bqkv", [128, 24], F32, kind="ExternalInput")
    wo_h = nc.dram_tensor("woT", [NDC, 128, D], F32, kind="ExternalInput")
    bo_h = nc.dram_tensor("bo", [128, NDC], F32, kind="ExternalInput")
    gate_h = nc.dram_tensor("gateT", [NIC, NDC, 128, 128], F32, kind="ExternalInput")
    up_h = nc.dram_tensor("upT", [NIC, NDC, 128, 128], F32, kind="ExternalInput")
    down_h = nc.dram_tensor("downT", [NIC, 128, D], F32, kind="ExternalInput")
    gs_h = nc.dram_tensor("gsc", [128, NIC], F32, kind="ExternalInput")
    us_h = nc.dram_tensor("usc", [128, NIC], F32, kind="ExternalInput")
    ds_h = nc.dram_tensor("dsc", [128, NDC], F32, kind="ExternalInput")
    sl_h = nc.dram_tensor("slopes2", [2, 8], F32, kind="ExternalInput")
    out_h = nc.dram_tensor("out", [2048, D], F32, kind="ExternalOutput")

    xn_s = nc.dram_tensor("xn_s", [NDC, 128, S], BF16, kind=kind_of("xn_s"))
    q_s = nc.dram_tensor("q_s", [8, 128, S], BF16, kind=kind_of("q_s"))
    k_s = nc.dram_tensor("k_s", [8, 128, S], BF16, kind=kind_of("k_s"))
    v_s = nc.dram_tensor("v_s", [8, S, 128], BF16, kind=kind_of("v_s"))
    hdn_s = nc.dram_tensor("hdn_s", [NIC, 128, 2048], BF16, kind=kind_of("hdn_s"))
    f_s = nc.dram_tensor("f_s", [NDC, 128, 2048], F32, kind=kind_of("f_s"))
    ctxm = nc.dram_tensor("ctxm", [1024, S], BF16, kind=kind_of("ctxm"))
    relr_d = nc.dram_tensor("relr_d", [4, 512], F32, kind="Internal")
    rell_d = nc.dram_tensor("rell_d", [4, 33, 128], F32, kind="Internal")
    ctxg = nc.dram_tensor("ctxg", [2048, S], BF16, kind=kind_of("ctxg"))

    ctx = ExitStack()
    with ctx:
        def sb(name, shape, dtype):
            return ctx.enter_context(nc.sbuf_tensor(name, shape, dtype))

        def sem(name):
            return Sem(ctx.enter_context(nc.semaphore(name)))

        sQ = sem("sQ")
        sG = sem("sG")
        sC = sem("sC")
        sPE = sem("sPE")
        sSC = sem("sSC")
        sVE = sem("sVE")
        sGM = sem("sGM")
        sB = sem("sB")

        bqkv_sb = sb("bqkv_sb", [128, 24], F32)
        gs_sb = sb("gs_sb", [128, NIC], F32)
        us_sb = sb("us_sb", [128, NIC], F32)
        ds_sb = sb("ds_sb", [128, NDC], F32)
        bo_sb = sb("bo_sb", [128, NDC], F32)
        sl_sb = sb("sl_sb", [2, 8], F32)
        eps_sb = sb("eps_sb", [128, 1], F32)
        ones_bf = sb("ones_bf", [128, 128], BF16)
        id_bf = sb("id_bf", [128, 128], BF16)
        id_f32 = sb("id_f32", [128, 128], F32)
        mask_bf = sb("mask_bf", [128, 4, 512], BF16)

        own_reg = ctx.enter_context(nc.gpsimd.register("own_off_r"))

        p0 = Ph(nc)
        voff = {}

        def setup_reg(g):
            pid = g.partition_id()
            g.reg_mov(own_reg, pid)
            g.reg_mod(own_reg, own_reg, 2)
            g.reg_mul(own_reg, own_reg, 2048)
            voff["v"] = g.snap(own_reg, min_val=0, max_val=2048)
            return g.nop()
        p0.do("gpsimd", setup_reg)

        for h_, t_ in [(bqkv_h, bqkv_sb), (gs_h, gs_sb), (us_h, us_sb),
                       (ds_h, ds_sb), (bo_h, bo_sb), (sl_h, sl_sb)]:
            p0.do("sync", lambda e, i=h_, o=t_: e.dma_start(out=o[...], in_=i[...]),
                  sQ, 16)
        const_ld = sQ.n

        p0.do("vector", lambda e: e.memset(eps_sb[:, :], EPS), sVE, 1)
        eps_done = sVE.n

        p0.do("gpsimd", lambda e: e.memset(ones_bf[:, :], 1.0))
        p0.do("gpsimd", lambda e: e.memset(id_bf[:, :], 0.0))
        p0.do("gpsimd", lambda e: e.affine_select(
            out=id_bf[:, :], in_=id_bf[:, :], compare_op=ALU.not_equal,
            fill=1.0, base=0, pattern=[[-1, 128]], channel_multiplier=1))
        p0.do("gpsimd", lambda e: e.memset(id_f32[:, :], 0.0))
        p0.do("gpsimd", lambda e: e.affine_select(
            out=id_f32[:, :], in_=id_f32[:, :], compare_op=ALU.not_equal,
            fill=1.0, base=0, pattern=[[-1, 128]], channel_multiplier=1))
        p0.do("gpsimd", lambda e: e.memset(mask_bf[:, :, :], 1.0))
        p0.do("gpsimd", lambda e: e.affine_select(
            out=mask_bf[:, :, :], in_=mask_bf[:, :, :], compare_op=ALU.is_ge,
            fill=0.0, base=0, pattern=[[-128, 4], [1, 512]],
            channel_multiplier=-1), sGM, 1)
        consts_done = sGM.n
        p0.barrier(sB)
        p0.run()

        if nphase >= 1:
            with ExitStack() as pc:
                def psb(name, shape, dtype):
                    return pc.enter_context(nc.sbuf_tensor(name, shape, dtype))

                def pps(name, shape, dtype):
                    return pc.enter_context(nc.psum_tensor(name, shape, dtype))

                wqkv_sb = psb("wqkv_sb", [128, NDC, 3072], BF16)
                x_sb = [psb(f"x_sb{i}", [128, 4, D], BF16) for i in range(2)]
                xn_tm = psb("xn_tm", [128, 4, D], BF16)
                xn_fm = [psb(f"xn_fm{i}", [128, NDC, 512], BF16) for i in range(2)]
                st_sb = psb("st_sb", [128, 8, 4], F32)
                qkv_bf = [psb(f"qkv_bf{i}", [128, 512], BF16) for i in range(4)]
                vtok = [psb(f"vtok{i}", [128, 512], BF16) for i in range(2)]
                ps_qkv = [pps(f"ps_qkv{i}", [128, 512], F32) for i in range(4)]
                psTR = [pps(f"psTRa{i}", [128, 1024], BF16) for i in range(2)]

                sWQ = sem("sWQ")
                sX = [sem("sX0"), sem("sX1")]
                sFM = [sem("sFM0"), sem("sFM1")]
                sST = [sem(f"sST{i}") for i in range(4)]
                sVT = [sem("sVT0"), sem("sVT1")]
                pa = Ph(nc)
                pa.wait("scalar", sVE, eps_done)
                pa.do("scalar", lambda e: e.preload_act_table(AF.Sqrt))
                pa.wait("scalar", sQ, const_ld)
                for dc in range(NDC):
                    pa.do("gpsimd", lambda e, o=wqkv_sb[:, dc, :], i=wqkv_h[dc]:
                          e.dma_start(out=o, in_=i), sWQ, 16)
                pa.wait("tensor", sWQ, sWQ.n)
                free = {}
                su = st_sb[:, 0, :]
                ssq = st_sb[:, 1, :]
                tm = st_sb[:, 2, :]
                vu = st_sb[:, 3, :]
                sd = st_sb[:, 4, :]
                rstd = st_sb[:, 5, :]
                nmr = st_sb[:, 6, :]

                def emit_ln(tt, stage):
                    xb = tt % 2
                    if stage == 0:
                        if ("x", xb) in free:
                            pa.wait("gpsimd", *free[("x", xb)])
                        xin = x_h[tt * 512:(tt + 1) * 512, :].rearrange(
                            "(p t) d -> t p d", t=128)
                        lv = pa.do("gpsimd", lambda e, o=x_sb[xb], i=xin:
                                   e.dma_start(out=o[:, :, :], in_=i),
                                   sX[xb], 16)
                        free[("xl", tt)] = (sX[xb], lv)
                        return
                    xb = tt % 2
                    pa.wait("scalar", *free[("xl", tt)])
                    if "stz" in free:
                        pa.wait("scalar", *free["stz"])
                    z1 = pa.do("scalar", lambda e: e.memzero(ssq), sSC, 1)
                    pa.wait("scalar", sSC, z1)
                    sq_done = None
                    for p in range(4):
                        sq_done = pa.do(
                            "scalar", lambda e, o=xn_tm[:, p, :],
                            i=x_sb[xb][:, p, :], ac=ssq[:, p:p + 1]:
                            e.activation(o, i, AF.Square, accum_out=ac),
                            sSC, 1)
                    pa.wait("vector", *free[("xl", tt)])
                    if "stzv" in free:
                        pa.wait("vector", *free["stzv"])
                    pa.do("vector", lambda e, o=su, i=x_sb[xb][:, :, :]:
                          e.reduce_sum(o, i, AX.X), sVE, 1)
                    pa.wait("vector", sSC, sq_done)
                    t3 = pa.do("vector", lambda e, o=tm, a=su:
                               e.scalar_tensor_tensor(
                                   o, a, 1.0 / D, a, op0=ALU.mult,
                                   op1=ALU.mult), sVE, 1)
                    pa.wait("vector", sVE, t3)
                    t4 = pa.do("vector", lambda e, o=vu, a=ssq, b=tm:
                               e.scalar_tensor_tensor(
                                   o, a, 1.0, b, op0=ALU.mult,
                                   op1=ALU.subtract), sVE, 1)
                    pa.wait("scalar", sVE, t4)
                    t5 = pa.do("scalar", lambda e, o=sd, i=vu:
                               e.activation(o, i, AF.Sqrt, scale=1.0 / D,
                                            bias=eps_sb[:, 0:1]), sSC, 1)
                    pa.wait("vector", sSC, t5)
                    t6 = pa.do("vector", lambda e, o=rstd, i=sd:
                               e.reciprocal(o, i), sVE, 1)
                    pa.wait("vector", sVE, t6)
                    t7 = pa.do("vector", lambda e, o=nmr, a=su, b=rstd:
                               e.scalar_tensor_tensor(
                                   o, a, -1.0 / D, b, op0=ALU.mult,
                                   op1=ALU.mult), sVE, 1)
                    pa.wait("scalar", sVE, t7)
                    xn_done = None
                    for p in range(4):
                        xn_done = pa.do(
                            "scalar", lambda e, o=xn_tm[:, p, :],
                            i=x_sb[xb][:, p, :],
                            sc=rstd[:, p:p + 1], bi=nmr[:, p:p + 1]:
                            e.activation(o, i, AF.Identity, scale=sc,
                                         bias=bi), sSC, 1)
                    free["stz"] = (sSC, xn_done)
                    free["stzv"] = (sSC, xn_done)
                    free[("x", tt % 2)] = (sSC, xn_done)
                    free[("xn", tt)] = (sSC, xn_done)

                emit_ln(0, 0)
                emit_ln(0, 1)
                for tt in range(8):
                    fb = tt % 2
                    cp_last = None
                    for gi in range(8):
                        p, dch = gi // 2, gi % 2
                        tb = gi % 2
                        if gi == 0:
                            pa.wait("tensor", *free[("xn", tt)])
                            if ("fm", fb) in free:
                                pa.wait("tensor", *free[("fm", fb)])
                        if ("tr", tb) in free:
                            pa.wait("tensor", *free[("tr", tb)])
                        last = None
                        for s_ in range(8):
                            dc = dch * 8 + s_
                            last = pa.do(
                                "tensor", lambda e,
                                o=psTR[tb][:, s_ * 128:(s_ + 1) * 128],
                                i=xn_tm[:, p, dc * 128:(dc + 1) * 128]:
                                e.transpose(o, i, id_bf[:, :]),
                                sPE if s_ == 7 else None, 1)
                        pa.wait("vector", sPE, last)
                        ov = xn_fm[fb][:, dch * 8:dch * 8 + 8,
                                       p * 128:(p + 1) * 128]
                        iv = psTR[tb].ap().rearrange("p (s d) -> p s d", s=8)
                        cp_last = pa.do("vector", lambda e, o=ov, i=iv:
                                        e.tensor_copy(o, i), sVE, 1)
                        trfree_val = (sVE, cp_last)
                        free[("tr", tb)] = trfree_val
                    pa.wait("sync", sVE, cp_last)
                    xout = xn_s.ap().rearrange("a p t -> p a t")[
                        :, :, tt * 512:(tt + 1) * 512]
                    stv = pa.do("sync", lambda e, o=xout, i=xn_fm[fb][:, :, :]:
                                e.dma_start(out=o, in_=i), sFM[fb], 16)
                    if tt + 1 < 8:
                        emit_ln(tt + 1, 0)
                    for fc in range(24):
                        qb = fc % 4
                        eb = fc % 4
                        if ("ps", qb) in free:
                            pa.wait("tensor", *free[("ps", qb)])
                        mm = None
                        for dc in range(NDC):
                            mm = pa.do(
                                "tensor",
                                lambda e, o=ps_qkv[qb][:, :],
                                l=wqkv_sb[:, dc, fc * 128:(fc + 1) * 128],
                                rr=xn_fm[fb][:, dc, :], s0=(dc == 0),
                                s1=(dc == NDC - 1):
                                e.matmul(o, l, rr, start=s0, stop=s1),
                                sPE if dc == NDC - 1 else None, 1)
                        if fc == 8 and tt + 1 < 8:
                            emit_ln(tt + 1, 1)
                        pa.wait("scalar", sPE, mm)
                        if ("eb", eb) in free:
                            pa.wait("scalar", *free[("eb", eb)])
                        ev = pa.do("scalar", lambda e, o=qkv_bf[eb][:, :],
                                   i=ps_qkv[qb][:, :], bi=bqkv_sb[:, fc:fc + 1]:
                                   e.activation(o, i, AF.Identity, scale=1.0,
                                                bias=bi), sSC, 1)
                        free[("ps", qb)] = (sSC, ev)
                        if fc < 16:
                            dst = (q_s if fc < 8 else k_s)[
                                fc % 8, :, tt * 512:(tt + 1) * 512]
                            pa.wait("sync", sSC, ev)
                            stv2 = pa.do("sync", lambda e, o=dst,
                                         i=qkv_bf[eb][:, :]:
                                         e.dma_start(out=o, in_=i),
                                         sST[eb], 16)
                            free[("eb", eb)] = (sST[eb], stv2)
                        else:
                            hh = fc - 16
                            tb = fc % 2
                            pa.wait("tensor", sSC, ev)
                            if ("tr", tb) in free:
                                pa.wait("tensor", *free[("tr", tb)])
                            last = None
                            for s_ in range(4):
                                last = pa.do(
                                    "tensor",
                                    lambda e,
                                    o=psTR[tb][:, s_ * 128:(s_ + 1) * 128],
                                    i=qkv_bf[eb][:, s_ * 128:(s_ + 1) * 128]:
                                    e.transpose(o, i, id_bf[:, :]),
                                    sPE if s_ == 3 else None, 1)
                            free[("eb", eb)] = (sPE, last)
                            vb = fc % 2
                            pa.wait("vector", sPE, last)
                            if ("vt", vb) in free:
                                pa.wait("vector", *free[("vt", vb)])
                            cp = pa.do("vector", lambda e, o=vtok[vb][:, :],
                                       i=psTR[tb][:, 0:512]:
                                       e.tensor_copy(o, i), sVE, 1)
                            free[("tr", tb)] = (sVE, cp)
                            pa.wait("sync", sVE, cp)
                            vdst = v_s[hh, tt * 512:(tt + 1) * 512,
                                       :].rearrange("(s t) d -> t s d", t=128)
                            ivv = vtok[vb].ap().rearrange(
                                "p (s d) -> p s d", s=4)
                            stv2 = pa.do("sync", lambda e, o=vdst, i=ivv:
                                         e.dma_start(out=o, in_=i),
                                         sVT[vb], 16)
                            free[("vt", vb)] = (sVT[vb], stv2)
                    free[("fm", fb)] = (sPE, sPE.n)
                for s_e in sST + sVT + sFM:
                    pa.wait("sync", s_e, s_e.n)
                for s_e in sX + [sWQ]:
                    pa.wait("gpsimd", s_e, s_e.n)
                pa.barrier(sB)
                pa.run()

        if nphase >= 3:
            with ExitStack() as pc:
                def psb(name, shape, dtype):
                    return pc.enter_context(nc.sbuf_tensor(name, shape, dtype))

                def pps(name, shape, dtype):
                    return pc.enter_context(nc.psum_tensor(name, shape, dtype))

                xn_sb = psb("xn_sbB", [128, NDC, 2048], BF16)
                gw_sb = [psb(f"gw{i}", [128, NDC, 128], BF16) for i in range(2)]
                uw_sb = [psb(f"uw{i}", [128, NDC, 128], BF16) for i in range(2)]
                gsil = [psb(f"gsil{i}", [128, 512], F32) for i in range(2)]
                hdn_bf = [psb(f"hdnb{i}", [128, 512], BF16) for i in range(2)]
                ps_g = [pps(f"ps_g{i}", [128, 512], F32) for i in range(2)]
                ps_u = [pps(f"ps_u{i}", [128, 512], F32) for i in range(2)]

                sXB = sem("sXB")
                sW = [sem("sW0"), sem("sW1")]
                sHB = [sem("sHB0"), sem("sHB1")]
                pb = Ph(nc)
                pb.do("scalar", lambda e: e.preload_act_table(AF.Silu))
                xin = xn_s.ap().rearrange("a p t -> p a t")[
                    :, :, bass.ds(voff["v"], 2048)]
                xl = pb.do("gpsimd", lambda e, o=xn_sb, i=xin:
                           e.dma_start(out=o[:, :, :], in_=i), sXB, 16)
                pb.wait("tensor", sXB, xl)
                pb.wait("scalar", sQ, const_ld)
                free = {}
                for ic in range(NIC):
                    wb = ic % 2
                    if ("w", wb) in free:
                        pb.wait("gpsimd", *free[("w", wb)])
                    gin = gate_h[ic].rearrange("a p d -> p a d")
                    uin = up_h[ic].rearrange("a p d -> p a d")
                    pb.do("gpsimd", lambda e, o=gw_sb[wb], i=gin:
                          e.dma_start(out=o[:, :, :], in_=i), sW[wb], 16)
                    wl = pb.do("gpsimd", lambda e, o=uw_sb[wb], i=uin:
                               e.dma_start(out=o[:, :, :], in_=i), sW[wb], 16)
                    pb.wait("tensor", sW[wb], wl)
                    last_mm = None
                    for tt in range(4):
                        gb = tt % 2
                        if ("pg", gb) in free:
                            pb.wait("tensor", *free[("pg", gb)])
                        mmg = None
                        for dc in range(NDC):
                            mmg = pa_mm = pb.do(
                                "tensor", lambda e, o=ps_g[gb][:, :],
                                l=gw_sb[wb][:, dc, :],
                                rr=xn_sb[:, dc, tt * 512:(tt + 1) * 512],
                                s0=(dc == 0), s1=(dc == NDC - 1):
                                e.matmul(o, l, rr, start=s0, stop=s1),
                                sPE if dc == NDC - 1 else None, 1)
                        if ("pu", gb) in free:
                            pb.wait("tensor", *free[("pu", gb)])
                        mmu = None
                        for dc in range(NDC):
                            mmu = pb.do(
                                "tensor", lambda e, o=ps_u[gb][:, :],
                                l=uw_sb[wb][:, dc, :],
                                rr=xn_sb[:, dc, tt * 512:(tt + 1) * 512],
                                s0=(dc == 0), s1=(dc == NDC - 1):
                                e.matmul(o, l, rr, start=s0, stop=s1),
                                sPE if dc == NDC - 1 else None, 1)
                        last_mm = mmu
                        sb2 = tt % 2
                        pb.wait("scalar", sPE, mmg)
                        if ("gs", sb2) in free:
                            pb.wait("scalar", *free[("gs", sb2)])
                        sl = pb.do("scalar", lambda e, o=gsil[sb2][:, :],
                                   i=ps_g[gb][:, :], sc=gs_sb[:, ic:ic + 1]:
                                   e.activation(o, i, AF.Silu, scale=sc),
                                   sSC, 1)
                        free[("pg", gb)] = (sSC, sl)
                        hb2 = tt % 2
                        pb.wait("vector", sSC, sl)
                        pb.wait("vector", sPE, mmu)
                        if ("hb", hb2) in free:
                            pb.wait("vector", *free[("hb", hb2)])
                        hv = pb.do("vector", lambda e, o=hdn_bf[hb2][:, :],
                                   i=ps_u[gb][:, :], sc=us_sb[:, ic:ic + 1],
                                   g2=gsil[sb2][:, :]:
                                   e.scalar_tensor_tensor(
                                       o, i, sc, g2, op0=ALU.mult,
                                       op1=ALU.mult), sVE, 1)
                        free[("pu", gb)] = (sVE, hv)
                        free[("gs", sb2)] = (sVE, hv)
                        pb.wait("sync", sVE, hv)
                        stv = pb.do("sync", lambda e,
                                    o=hdn_s[ic, :, tt * 512:(tt + 1) * 512],
                                    i=hdn_bf[hb2][:, :]:
                                    e.dma_start(out=o, in_=i), sHB[hb2], 16)
                        free[("hb", hb2)] = (sHB[hb2], stv)
                    free[("w", wb)] = (sPE, last_mm)
                for s_e in sHB:
                    pb.wait("sync", s_e, s_e.n)
                pb.wait("gpsimd", sXB, sXB.n)
                for s_e in sW:
                    pb.wait("gpsimd", s_e, s_e.n)
                pb.barrier(sB)
                pb.run()

        if nphase >= 4:
            with ExitStack() as pc:
                def psb(name, shape, dtype):
                    return pc.enter_context(nc.sbuf_tensor(name, shape, dtype))

                def pps(name, shape, dtype):
                    return pc.enter_context(nc.psum_tensor(name, shape, dtype))

                k_sb = [psb(f"k_sb{i}", [128, S], BF16) for i in range(2)]
                q_sb = [psb(f"q_sb{i}", [128, S], BF16) for i in range(2)]
                v_sb = [psb(f"v_sb{i}", [128, 32, 128], BF16) for i in range(2)]
                rhsRel = psb("rhsRel", [4, 512], BF16)
                lhsRel = psb("lhsRel", [4, 33, 128], BF16)
                slb = psb("slb", [128, 8], F32)
                rsl = psb("rsl", [128, 8], F32)
                slTSC = psb("slTSC", [128, 8], F32)
                ones1 = psb("ones1", [1, 128], F32)
                wspA = psb("wspA", [1, 512], F32)
                wspH = psb("wspH", [1, 512], F32)
                wspB = psb("wspB", [1, 512], F32)
                wspL = psb("wspL", [1, 33, 128], F32)
                tn = [psb(f"tn{i}", [128, 1024], F32) for i in range(2)]
                pr = [psb(f"pr{i}", [128, 1024], BF16) for i in range(4)]
                rc = [psb(f"rc{i}", [128, 512], F32) for i in range(2)]
                cbf = [psb(f"cbf{i}", [128, 512], BF16) for i in range(2)]
                dw_q = psb("dw_q", [128, 2, 22, 512], BF16)
                hdn_fl = psb("hdn_fl", [128, 2, 22, 512], BF16)
                fD = [psb(f"fD{i}", [128, 512], F32) for i in range(2)]
                psS = [pps(f"psS{i}", [128, 1024], F32) for i in range(2)]
                psCx = pps("psCx", [128, 512], F32)
                psDn = pps("psDn", [128, 512], F32)
                psD = [pps(f"psD{i}", [128, 512], F32) for i in range(2)]

                sHL = [sem("sHL0"), sem("sHL1")]
                sCS = [sem("sCS0"), sem("sCS1")]
                sRELs = sem("sRELs")
                sRELl = sem("sRELl")
                pcx = Ph(nc)
                pcx.do("scalar", lambda e: e.preload_act_table(AF.Tanh))
                pcx.wait("gpsimd", sGM, consts_done)
                pcx.wait("vector", sQ, const_ld)
                pcx.do("scalar", lambda e: e.dma_start(
                    out=k_sb[0][:, :], in_=k_s[0]), sHL[0], 16)
                pcx.do("scalar", lambda e: e.dma_start(
                    out=q_sb[0][:, :], in_=q_s[0]), sHL[0], 16)
                h0l = pcx.do("scalar", lambda e: e.dma_start(
                    out=v_sb[0][:, :, :],
                    in_=v_s[0].rearrange("(s t) d -> t s d", t=128)),
                    sHL[0], 16)
                g1 = pcx.do("gpsimd", lambda e: e.memset(ones1[:, :], 1.0),
                            sGM, 1)
                pcx.wait("tensor", sGM, g1)
                pcx.wait("tensor", sQ, const_ld)
                bmm = pcx.do("tensor", lambda e: e.matmul(
                    psCx[:, 0:8], ones1[0:1, :], sl_sb[0:1, 0:8],
                    start=True, stop=True, skip_group_check=True), sPE, 1)
                pcx.wait("vector", sPE, bmm)
                v1 = pcx.do("vector", lambda e: e.tensor_copy(
                    slb[:, :], psCx[:, 0:8]), sVE, 1)
                pcx.wait("vector", sVE, v1)
                v2 = pcx.do("vector", lambda e: e.reciprocal(
                    rsl[:, :], slb[:, :]), sVE, 1)
                pcx.wait("vector", sVE, v2)
                v3 = pcx.do("vector", lambda e: e.tensor_scalar_mul(
                    slTSC[:, :], slb[:, :], TSC), sVE, 1)
                setup_norm = v1
                gi1 = pcx.do("gpsimd", lambda e: e.iota(
                    wspA[:, :], [[1, 512]], channel_multiplier=0,
                    allow_small_or_imprecise_dtypes=True))
                pcx.do("gpsimd", lambda e: e.memset(wspH[:, :], 1.0))
                gi2 = pcx.do("gpsimd", lambda e: e.affine_select(
                    out=wspH[:, :], in_=wspH[:, :], compare_op=ALU.is_ge,
                    fill=0.0, base=-256, pattern=[[1, 512]],
                    channel_multiplier=0), sGM, 1)
                pcx.wait("vector", sGM, gi2)
                v4 = pcx.do("vector", lambda e: e.scalar_tensor_tensor(
                    wspB[:, :], wspH[:, :], 256.0, wspA[:, :],
                    op0=ALU.mult, op1=ALU.subtract), sVE, 1)
                pcx.wait("vector", sVE, v4)
                v5 = pcx.do("vector", lambda e: e.tensor_scalar_mul(
                    wspA[:, :], wspH[:, :], -256.0), sVE, 1)
                pcx.wait("vector", sVE, v5)
                v6 = pcx.do("vector", lambda e: e.memset(wspH[:, :], 1.0),
                            sVE, 1)
                pcx.wait("sync", sVE, v6)
                pcx.do("sync", lambda e: e.dma_start(
                    out=relr_d[0:1, :], in_=wspH[0:1, :]), sRELs, 16)
                pcx.do("sync", lambda e: e.dma_start(
                    out=relr_d[1:2, :], in_=wspB[0:1, :]), sRELs, 16)
                pcx.do("sync", lambda e: e.dma_start(
                    out=relr_d[2:3, :], in_=wspA[0:1, :]), sRELs, 16)
                pcx.do("sync", lambda e: e.dma_start(
                    out=relr_d[3:4, :], in_=wspH[0:1, :]), sRELs, 16)
                gi3 = pcx.do("gpsimd", lambda e: e.iota(
                    wspL[:, :, :], [[0, 33], [1, 128]], channel_multiplier=0,
                    allow_small_or_imprecise_dtypes=True), sGM, 1)
                pcx.wait("sync", sGM, gi3)
                pcx.do("sync", lambda e: e.dma_start(
                    out=rell_d[0:1, :, :], in_=wspL[0:1, :, :]), sRELs, 16)
                pcx.wait("gpsimd", sRELs, sRELs.n)
                gi4 = pcx.do("gpsimd", lambda e: e.iota(
                    wspL[:, :, :], [[128, 33], [0, 128]],
                    channel_multiplier=0,
                    allow_small_or_imprecise_dtypes=True), sGM, 1)
                pcx.wait("vector", sGM, gi4)
                v7 = pcx.do("vector", lambda e: e.tensor_scalar_add(
                    wspL[:, :, :], wspL[:, :, :], -3584.0), sVE, 1)
                pcx.wait("sync", sVE, v7)
                pcx.do("sync", lambda e: e.dma_start(
                    out=rell_d[3:4, :, :], in_=wspL[0:1, :, :]), sRELs, 16)
                pcx.wait("vector", sRELs, sRELs.n)
                v8 = pcx.do("vector", lambda e: e.memset(wspL[:, :, :], 1.0),
                            sVE, 1)
                pcx.wait("sync", sVE, v8)
                pcx.do("sync", lambda e: e.dma_start(
                    out=rell_d[1:2, :, :], in_=wspL[0:1, :, :]), sRELs, 16)
                pcx.do("sync", lambda e: e.dma_start(
                    out=rell_d[2:3, :, :], in_=wspL[0:1, :, :]), sRELs, 16)
                pcx.wait("gpsimd", sRELs, sRELs.n)
                pcx.do("gpsimd", lambda e: e.dma_start(
                    out=rhsRel[:, :], in_=relr_d[:, :]), sRELl, 16)
                rel_ld = pcx.do("gpsimd", lambda e: e.dma_start(
                    out=lhsRel[:, :, :], in_=rell_d[:, :, :]), sRELl, 16)
                pcx.wait("tensor", sRELl, rel_ld)
                grp_free = {}
                pr_free = {}
                head_free = {}
                norm_done_val = setup_norm
                gi_ctr = 0
                sDWh = [sem("sDWh0"), sem("sDWh1")]
                sHFh = [sem("sHFh0"), sem("sHFh1")]
                sFS = [sem("sFS0"), sem("sFS1")]
                dstate = {"dw_war": {}, "hdn_war": {}, "dw_ld": {},
                          "hdn_ld": {}, "evac": {}, "fwar": {},
                          "last_mm": None}

                def mk_dw_load(qr, hf):
                    def go():
                        if hf in dstate["dw_war"]:
                            pcx.wait("gpsimd", sPE, dstate["dw_war"][hf])
                        lv = None
                        for icl in range(22):
                            ic = hf * 22 + icl
                            din = down_h[ic, :, qr * 512:(qr + 1) * 512]
                            lv = pcx.do("gpsimd",
                                        lambda e, o=dw_q[:, hf, icl, :],
                                        i=din: e.dma_start(out=o, in_=i),
                                        sDWh[hf], 16)
                        dstate["dw_ld"][hf] = lv
                    return go

                def mk_hdn_load(qr, tt, hf):
                    def go():
                        if hf in dstate["hdn_war"]:
                            pcx.wait("sync", sPE, dstate["hdn_war"][hf])
                        hin = hdn_s[hf * 22:hf * 22 + 22, :,
                                    tt * 512:(tt + 1) * 512].rearrange(
                                        "a p d -> p a d")
                        lv = pcx.do("sync", lambda e,
                                    o=hdn_fl[:, hf, :, :], i=hin:
                                    e.dma_start(out=o, in_=i), sHFh[hf], 16)
                        dstate["hdn_ld"][hf] = lv
                    return go

                def mk_mms(qr, tt, pas, hf, ic0, icn):
                    def go():
                        if ic0 == 0:
                            pcx.wait("tensor", sDWh[hf], dstate["dw_ld"][hf])
                            pcx.wait("tensor", sHFh[hf], dstate["hdn_ld"][hf])
                            if hf == 0 and pas == 0:
                                pass
                            if hf == 0:
                                for m in range(2):
                                    if m in dstate["evac"]:
                                        pcx.wait("tensor", sVE,
                                                 dstate["evac"][m])
                        lm = None
                        for icl in range(ic0, ic0 + icn):
                            for m in range(2):
                                mc_loc = 2 * pas + m
                                st0 = (hf == 0 and icl == 0)
                                st1 = (hf == 1 and icl == 21)
                                lm = pcx.do(
                                    "tensor", lambda e, o=psD[m][:, :],
                                    l=dw_q[:, hf, icl,
                                           mc_loc * 128:(mc_loc + 1) * 128],
                                    rr=hdn_fl[:, hf, icl, :],
                                    s0=st0, s1=st1:
                                    e.matmul(o, l, rr, start=s0, stop=s1,
                                             skip_group_check=True),
                                    sPE, 1)
                        dstate["last_mm"] = lm
                        dstate["dw_war"][hf] = lm
                        dstate["hdn_war"][hf] = lm
                    return go

                def mk_evac(qr, tt, pas):
                    def go():
                        pcx.wait("vector", sPE, dstate["last_mm"])
                        for m in range(2):
                            mc_g = qr * 4 + 2 * pas + m
                            if m in dstate["fwar"]:
                                pcx.wait("vector", sFS[m], dstate["fwar"][m])
                            ev = pcx.do(
                                "vector", lambda e, o=fD[m][:, :],
                                i=psD[m][:, :],
                                sc=ds_sb[:, mc_g:mc_g + 1]:
                                e.tensor_single_scalar(o, i, sc, ALU.mult),
                                sVE, 1)
                            dstate["evac"][m] = ev
                            pcx.wait("sync", sVE, ev)
                            stv = pcx.do(
                                "sync", lambda e,
                                o=f_s[mc_g, :, tt * 512:(tt + 1) * 512],
                                i=fD[m][:, :]:
                                e.dma_start(out=o, in_=i), sFS[m], 16)
                            dstate["fwar"][m] = stv
                    return go

                down_steps = []
                for qr in range(4):
                    down_steps.append(mk_dw_load(qr, 0))
                    down_steps.append(mk_dw_load(qr, 1))
                    for tt in range(4):
                        down_steps.append(mk_hdn_load(qr, tt, 0))
                        down_steps.append(mk_hdn_load(qr, tt, 1))
                        for pas in range(2):
                            for hf in range(2):
                                down_steps.append(mk_mms(qr, tt, pas, hf, 0, 11))
                                down_steps.append(mk_mms(qr, tt, pas, hf, 11, 11))
                            down_steps.append(mk_evac(qr, tt, pas))
                dtot = len(down_steps)
                ddone = [0]
                batch_ctr = [0]

                def pace_down(final=False):
                    if final:
                        tgt = dtot
                    else:
                        tgt = min(dtot, int(dtot * batch_ctr[0] / 576.0))
                    while ddone[0] < tgt:
                        down_steps[ddone[0]]()
                        ddone[0] += 1
                for h in range(8):
                    hb = h % 2
                    if h == 0:
                        hl = h0l
                    if h > 0 and hb in head_free:
                        pcx.wait("scalar", *head_free[hb])
                    if h > 0:
                        pcx.do("scalar", lambda e, o=k_sb[hb][:, :], i=k_s[h]:
                           e.dma_start(out=o, in_=i), sHL[hb], 16)
                        pcx.do("scalar", lambda e, o=q_sb[hb][:, :], i=q_s[h]:
                               e.dma_start(out=o, in_=i), sHL[hb], 16)
                        vin = v_s[h].rearrange("(s t) d -> t s d", t=128)
                        hl = pcx.do("scalar", lambda e, o=v_sb[hb][:, :, :],
                                    i=vin:
                                    e.dma_start(out=o, in_=i), sHL[hb], 16)
                    pcx.wait("vector", sHL[hb], hl)
                    pcx.wait("vector", sVE, v3)
                    qpv = pcx.do("vector", lambda e, o=q_sb[hb][:, :],
                                 sc=rsl[:, h:h + 1]:
                                 e.tensor_single_scalar(o, o, sc, ALU.mult),
                                 sVE, 1)
                    pcx.wait("tensor", sHL[hb], hl)
                    pcx.wait("tensor", sVE, qpv)
                    h_last_pe = None
                    for qt in range(8):
                        nb = 2 * (qt + 1)
                        pend = []
                        for j in range(nb):
                            g = gi_ctr % 2
                            pb_ = gi_ctr % 4
                            gi_ctr += 1
                            if g in grp_free:
                                pcx.wait("tensor", sSC, grp_free[g])
                            sc_done = [None, None]
                            for u in range(2):
                                kt = 2 * j + u
                                pcx.do("tensor", lambda e,
                                       o=psS[g][:, u * 512:(u + 1) * 512],
                                       l=k_sb[hb][:, kt * 128:(kt + 1) * 128],
                                       rr=q_sb[hb][:, qt * 512:(qt + 1) * 512]:
                                       e.matmul(o, l, rr, start=True,
                                                stop=False,
                                                skip_group_check=True))
                                didx = kt - 4 * qt + 28
                                sc_done[u] = pcx.do(
                                    "tensor", lambda e,
                                    o=psS[g][:, u * 512:(u + 1) * 512],
                                    l=lhsRel[0:4, didx, :],
                                    rr=rhsRel[0:4, :]:
                                    e.matmul(o, l, rr, start=False, stop=True,
                                             skip_group_check=True),
                                    sPE, 1)
                            diag = j >= nb - 2
                            off = 0 if j == nb - 2 else 2
                            pv = [None, None]
                            mv = [None, None]
                            for u in range(2):
                                pcx.wait("scalar", sPE, sc_done[u])
                                tv = pcx.do(
                                    "scalar", lambda e,
                                    o=tn[g][:, u * 512:(u + 1) * 512],
                                    i=psS[g][:, u * 512:(u + 1) * 512],
                                    sc=slTSC[:, h:h + 1]:
                                    e.activation(o, i, AF.Tanh, scale=sc),
                                    sSC, 1)
                                if u == 1:
                                    grp_free[g] = tv
                                if u == 0 and pb_ in pr_free:
                                    pcx.wait("scalar", sPE, pr_free[pb_])
                                pv[u] = pcx.do(
                                    "scalar", lambda e,
                                    o=pr[pb_][:, u * 512:(u + 1) * 512],
                                    i=tn[g][:, u * 512:(u + 1) * 512]:
                                    e.activation(o, i, AF.Exp, scale=30.0),
                                    sSC, 1)
                                if diag:
                                    pcx.wait("vector", sSC, pv[u])
                                    mv[u] = pcx.do(
                                        "vector", lambda e,
                                        o=pr[pb_][:, u * 512:(u + 1) * 512],
                                        m=mask_bf[:, off + u, :]:
                                        e.tensor_tensor(o, o, m, ALU.mult),
                                        sVE, 1)
                            pend.append((j, g, pb_, pv, mv))
                            while pend and (len(pend) > 2 or j == nb - 1):
                                jj, gg, pp2, pvv, mvv = pend.pop(0)
                                if jj == 0 and norm_done_val is not None:
                                    pcx.wait("tensor", sVE, norm_done_val)
                                for u in range(2):
                                    if mvv[u] is not None:
                                        pcx.wait("tensor", sVE, mvv[u])
                                    else:
                                        pcx.wait("tensor", sSC, pvv[u])
                                    kt = 2 * jj + u
                                    st0 = (jj == 0 and u == 0)
                                    st1 = (jj == nb - 1 and u == 1)
                                    pcx.do("tensor", lambda e, o=psCx[:, :],
                                           l=v_sb[hb][:, kt, :],
                                           rr=pr[pp2][:, u * 512:(u + 1) * 512],
                                           s0=st0, s1=st1:
                                           e.matmul(o, l, rr, start=s0,
                                                    stop=s1,
                                                    skip_group_check=True))
                                cd = None
                                for u in range(2):
                                    st0 = (jj == 0 and u == 0)
                                    st1 = (jj == nb - 1 and u == 1)
                                    cd = pcx.do(
                                        "tensor", lambda e, o=psDn[:, :],
                                        l=ones_bf[:, :],
                                        rr=pr[pp2][:, u * 512:(u + 1) * 512],
                                        s0=st0, s1=st1:
                                        e.matmul(o, l, rr, start=s0, stop=s1,
                                                 skip_group_check=True),
                                        sPE if u == 1 else None, 1)
                                pr_free[pp2] = cd
                                h_last_pe = cd
                                batch_ctr[0] += 1
                                pace_down()
                        qb2 = (h * 8 + qt) % 2
                        pcx.wait("vector", sPE, h_last_pe)
                        if ("cbf", qb2) in pr_free:
                            pcx.wait("vector", sCS[qb2], pr_free[("cbf", qb2)])
                        pcx.do("vector", lambda e, o=rc[qb2][:, :],
                               i=psDn[:, :]: e.reciprocal(o, i))
                        nv = pcx.do("vector", lambda e, o=cbf[qb2][:, :],
                                    a=psCx[:, :], b=rc[qb2][:, :]:
                                    e.tensor_mul(o, a, b), sVE, 1)
                        norm_done_val = nv
                        pcx.wait("sync", sVE, nv)
                        stc = pcx.do("sync", lambda e,
                               o=ctxm[h * 128:(h + 1) * 128,
                                      qt * 512:(qt + 1) * 512],
                               i=cbf[qb2][:, :]:
                               e.dma_start(out=o, in_=i), sCS[qb2], 16)
                        pr_free[("cbf", qb2)] = stc
                    head_free[hb] = (sPE, h_last_pe)
                pace_down(final=True)
                pcx.wait("gpsimd", sCS[0], sCS[0].n)
                pcx.wait("gpsimd", sCS[1], sCS[1].n)
                if dump != "ctxm":
                    pcx.do("gpsimd", lambda e: e.collective_compute(
                    "AllGather", ALU.bypass,
                    replica_groups=[[0, 1], [2, 3], [4, 5], [6, 7]],
                        ins=[ctxm.ap().opt()],
                        outs=[ctxg.ap().opt()]), sC, 1)
                for s_e in sHL + sCS + [sHFh[0], sHFh[1], sFS[0], sFS[1]]:
                    pcx.wait("sync", s_e, s_e.n)
                for s_e in sDWh:
                    pcx.wait("gpsimd", s_e, s_e.n)
                pcx.barrier(sB)
                pcx.run()

        if False:
            with ExitStack() as pc:
                def psb(name, shape, dtype):
                    return pc.enter_context(nc.sbuf_tensor(name, shape, dtype))

                def pps(name, shape, dtype):
                    return pc.enter_context(nc.psum_tensor(name, shape, dtype))

                dw_sb = psb("dw_sb", [128, NIC, 1024], BF16)
                hdn_fl = psb("hdn_fl", [128, NIC, 512], BF16)
                f_buf = [psb(f"f_buf{i}", [128, 512], F32) for i in range(2)]
                ps_dn = [pps(f"ps_dn{i}", [128, 512], F32) for i in range(8)]

                sDW = sem("sDW")
                sHF = sem("sHF")
                sFS = [sem("sFS0"), sem("sFS1")]
                pc2 = Ph(nc)
                free = {}
                for half in range(2):
                    if "dw" in free:
                        pc2.wait("gpsimd", *free["dw"])
                    dl = None
                    for ic in range(NIC):
                        din = down_h[ic, :, half * 1024:(half + 1) * 1024]
                        dl = pc2.do("gpsimd", lambda e, o=dw_sb[:, ic, :],
                                    i=din: e.dma_start(out=o, in_=i), sDW, 16)
                    pc2.wait("tensor", sDW, dl)
                    last_mm = None
                    for tt in range(4):
                        if "hdn" in free:
                            pc2.wait("sync", *free["hdn"])
                        hin = hdn_s[:, :, tt * 512:(tt + 1) * 512].rearrange(
                            "a p d -> p a d")
                        hl = pc2.do("sync", lambda e, o=hdn_fl[:, :, :], i=hin:
                                    e.dma_start(out=o, in_=i), sHF, 16)
                        pc2.wait("tensor", sHF, hl)
                        if "ev" in free:
                            pc2.wait("tensor", *free["ev"])
                        for ic in range(NIC):
                            for mc in range(8):
                                is_l = ic == NIC - 1 and mc == 7
                                last_mm = pc2.do(
                                    "tensor", lambda e, o=ps_dn[mc][:, :],
                                    l=dw_sb[:, ic, mc * 128:(mc + 1) * 128],
                                    rr=hdn_fl[:, ic, :], s0=(ic == 0),
                                    s1=(ic == NIC - 1):
                                    e.matmul(o, l, rr, start=s0, stop=s1,
                                             skip_group_check=True),
                                    sPE if is_l else None, 1)
                        free["hdn"] = (sPE, last_mm)
                        pc2.wait("vector", sPE, last_mm)
                        evs = []
                        for mc in range(8):
                            fb2 = mc % 2
                            if ("f", fb2) in free:
                                pc2.wait("vector", *free[("f", fb2)])
                            ev = pc2.do(
                                "vector", lambda e, o=f_buf[fb2][:, :],
                                i=ps_dn[mc][:, :],
                                sc=ds_sb[:, half * 8 + mc:half * 8 + mc + 1]:
                                e.tensor_single_scalar(o, i, sc, ALU.mult),
                                sVE, 1)
                            evs.append(ev)
                            pc2.wait("sync", sVE, ev)
                            stv = pc2.do(
                                "sync", lambda e,
                                o=f_s[half * 8 + mc, :,
                                      tt * 512:(tt + 1) * 512],
                                i=f_buf[fb2][:, :]:
                                e.dma_start(out=o, in_=i), sFS[fb2], 16)
                            free[("f", fb2)] = (sFS[fb2], stv)
                        free["ev"] = (sVE, evs[-1])
                    free["dw"] = (sPE, last_mm)
                for s_e in [sHF] + sFS:
                    pc2.wait("sync", s_e, s_e.n)
                pc2.wait("gpsimd", sDW, sDW.n)
                pc2.barrier(sB)
                pc2.run()

        if nphase >= 5:
            with ExitStack() as pc:
                def psb(name, shape, dtype):
                    return pc.enter_context(nc.sbuf_tensor(name, shape, dtype))

                def pps(name, shape, dtype):
                    return pc.enter_context(nc.psum_tensor(name, shape, dtype))

                wo_sb = psb("wo_sb", [128, NDC, 2048], BF16)
                ctx_tl = [psb(f"ctx_tl{i}", [128, NDC, 512], BF16)
                          for i in range(2)]
                x_tl = psb("x_tl", [128, 4, D], F32)
                out_sb = psb("out_sb", [128, 4, D], F32)
                f_tl = [psb(f"f_tl{i}", [128, 512], F32) for i in range(4)]
                gsil = [psb(f"gsilD{i}", [128, 512], F32) for i in range(2)]
                sg = [psb(f"sg{i}", [128, 512], F32) for i in range(2)]
                ps_s = [pps(f"ps_s{i}", [128, 512], F32) for i in range(2)]
                psTRf = [pps(f"psTRf{i}", [128, 512], F32) for i in range(2)]

                sWO = sem("sWO")
                sCT = sem("sCT")
                sXD = sem("sXD")
                sFL = [sem(f"sFL{i}") for i in range(4)]
                sOS = sem("sOS")
                pd = Ph(nc)
                pd.do("scalar", lambda e: e.preload_act_table(AF.Silu))
                for dc in range(NDC):
                    pd.do("gpsimd", lambda e, o=wo_sb[:, dc, :], i=wo_h[dc]:
                          e.dma_start(out=o, in_=i), sWO, 16)
                pd.wait("gpsimd", sC, 1)
                pd.wait("tensor", sWO, sWO.n)
                free = {}
                for tt in range(4):
                    cb_ = tt % 2
                    if ("ctx", cb_) in free:
                        pd.wait("gpsimd", *free[("ctx", cb_)])
                    cin = ctxg.ap().rearrange("(a p) t -> p a t", p=128)[
                        :, :, bass.ds(voff["v"], 2048)][
                        :, :, tt * 512:(tt + 1) * 512]
                    cl = pd.do("gpsimd", lambda e, o=ctx_tl[cb_][:, :, :],
                               i=cin: e.dma_start(out=o, in_=i), sCT, 16)
                    if "x" in free:
                        pd.wait("gpsimd", *free["x"])
                    xin = x_h.ap()[bass.ds(voff["v"], 2048), :].rearrange(
                        "(a p t) d -> t a p d", p=4, t=128)
                    xl = pd.do("gpsimd", lambda e, o=x_tl[:, :, :],
                               i=xin[:, tt, :, :]:
                               e.dma_start(out=o, in_=i), sXD, 16)
                    pd.wait("tensor", sCT, cl)
                    ocp = []
                    tail = []
                    for mc in range(NDC + 1):
                        if mc < NDC:
                            pb2 = mc % 2
                            fb4 = mc % 4
                            if ("f", fb4) in free:
                                pd.wait("sync", *free[("f", fb4)])
                            fl = pd.do("sync", lambda e, o=f_tl[fb4][:, :],
                                       i=f_s[mc, :, tt * 512:(tt + 1) * 512]:
                                       e.dma_start(out=o, in_=i), sFL[fb4], 16)
                            if ("ps", pb2) in free:
                                pd.wait("tensor", *free[("ps", pb2)])
                            mm = None
                            for kc in range(NDC):
                                mm = pd.do(
                                    "tensor", lambda e, o=ps_s[pb2][:, :],
                                    l=wo_sb[:, kc, mc * 128:(mc + 1) * 128],
                                    rr=ctx_tl[cb_][:, kc, :], s0=(kc == 0),
                                    s1=(kc == NDC - 1):
                                    e.matmul(o, l, rr, start=s0, stop=s1),
                                    sPE if kc == NDC - 1 else None, 1)
                            pd.wait("scalar", sFL[fb4], fl)
                            if ("gs", pb2) in free:
                                pd.wait("scalar", *free[("gs", pb2)])
                            sl = pd.do("scalar", lambda e, o=gsil[pb2][:, :],
                                       i=f_tl[fb4][:, :]:
                                       e.activation(o, i, AF.Silu), sSC, 1)
                            free[("f", fb4)] = (sSC, sl)
                            pd.wait("vector", sPE, mm)
                            pd.wait("vector", sSC, sl)
                            if ("sg", pb2) in free:
                                pd.wait("vector", *free[("sg", pb2)])
                            sv = pd.do("vector", lambda e, o=sg[pb2][:, :],
                                       i=ps_s[pb2][:, :],
                                       bi=bo_sb[:, mc:mc + 1],
                                       g2=gsil[pb2][:, :]:
                                       e.scalar_tensor_tensor(
                                           o, i, bi, g2, op0=ALU.add,
                                           op1=ALU.mult), sVE, 1)
                            free[("ps", pb2)] = (sVE, sv)
                            free[("gs", pb2)] = (sVE, sv)
                            tail.append((mc, pb2, sv))
                        if tail and (len(tail) > 1 or mc == NDC):
                            mcp, pbp, svp = tail.pop(0)
                            pd.wait("tensor", sVE, svp)
                            if ("tr", pbp) in free:
                                pd.wait("tensor", *free[("tr", pbp)])
                            last = None
                            for s_ in range(4):
                                last = pd.do(
                                    "tensor", lambda e,
                                    o=psTRf[pbp][:, s_ * 128:(s_ + 1) * 128],
                                    i=sg[pbp][:, s_ * 128:(s_ + 1) * 128]:
                                    e.transpose(o, i, id_f32[:, :]),
                                    sPE if s_ == 3 else None, 1)
                            free[("sg", pbp)] = (sPE, last)
                            pd.wait("vector", sPE, last)
                            if mcp < 2 and "out" in free:
                                pd.wait("vector", *free["out"])
                            ov = out_sb[:, :, mcp * 128:(mcp + 1) * 128]
                            iv = psTRf[pbp].ap().rearrange(
                                "p (s d) -> p s d", s=4)
                            cp = pd.do("vector", lambda e, o=ov, i=iv:
                                       e.tensor_copy(o, i), sVE, 1)
                            free[("tr", pbp)] = (sVE, cp)
                            ocp.append(cp)
                    pd.wait("vector", sXD, xl)
                    av = pd.do("vector", lambda e, o=out_sb[:, :, :],
                               a=out_sb[:, :, :], b=x_tl[:, :, :]:
                               e.tensor_add(o, a, b), sVE, 1)
                    free["x"] = (sVE, av)
                    pd.wait("sync", sVE, av)
                    odst = out_h[tt * 512:(tt + 1) * 512, :].rearrange(
                        "(p t) d -> t p d", t=128)
                    ost = pd.do("sync", lambda e, o=odst, i=out_sb[:, :, :]:
                                e.dma_start(out=o, in_=i), sOS, 16)
                    free["out"] = (sOS, ost)
                    free[("ctx", cb_)] = (sPE, sPE.n)
                pd.wait("sync", sOS, sOS.n)
                for s_e in sFL:
                    pd.wait("sync", s_e, s_e.n)
                for s_e in (sWO, sCT, sXD):
                    pd.wait("gpsimd", s_e, s_e.n)
                pd.barrier(sB)
                pd.run()
        else:
            pz = Ph(nc)
            pz.wait("sync", sQ, sQ.n)
            pz.barrier(sB)
            pz.run()

    return nc


def prep_inputs(c, inputs):
    b, r = c // 2, c % 2
    hs = slice(1024 * r, 1024 * r + 1024)
    Wq = inputs["L0_attn_Wq"]
    Wk = inputs["L0_attn_Wk"]
    Wv = inputs["L0_attn_Wv"]
    w_cat = np.concatenate([Wq[hs], Wk[hs], Wv[hs]], 0)
    b_cat = np.concatenate([inputs["L0_attn_bq"][hs], inputs["L0_attn_bk"][hs],
                            inputs["L0_attn_bv"][hs]])
    gate = inputs["L0_ffn_gate_w"]
    up = inputs["L0_ffn_up_w"]
    down = inputs["L0_ffn_down_w"]
    slopes = inputs["L0_attn_alibi_slopes"][8 * r:8 * r + 8]
    return {
        "x": np.ascontiguousarray(inputs["L0_attn_hidden_states"][b]),
        "wqkvT": np.ascontiguousarray(w_cat.T.reshape(NDC, 128, 3072)),
        "bqkv": np.ascontiguousarray(b_cat.reshape(24, 128).T),
        "woT": np.ascontiguousarray(
            inputs["L0_attn_Wo"].T.reshape(NDC, 128, D)),
        "bo": np.ascontiguousarray(inputs["L0_attn_bo"].reshape(NDC, 128).T),
        "gateT": np.ascontiguousarray(
            gate.T.reshape(NDC, 128, NIC, 128).transpose(2, 0, 1, 3)),
        "upT": np.ascontiguousarray(
            up.T.reshape(NDC, 128, NIC, 128).transpose(2, 0, 1, 3)),
        "downT": np.ascontiguousarray(down.T.reshape(NIC, 128, D)),
        "gsc": np.ascontiguousarray(
            inputs["L0_ffn_gate_s"].reshape(NIC, 128).T),
        "usc": np.ascontiguousarray(
            inputs["L0_ffn_up_s"].reshape(NIC, 128).T),
        "dsc": np.ascontiguousarray(
            inputs["L0_ffn_down_s"].reshape(NDC, 128).T),
        "slopes2": np.ascontiguousarray(
            np.stack([slopes, slopes], 0)),
    }


def kernel(**inputs):
    nc = build_nc()
    in_maps = [prep_inputs(c, inputs) for c in range(8)]
    res = run_bass_kernel_spmd(nc, in_maps, core_ids=list(range(8)))
    out = np.empty((B, S, D), dtype=np.float32)
    for c in range(8):
        b, r = c // 2, c % 2
        out[b, 2048 * r:2048 * r + 2048, :] = res.results[c]["out"]
    return out


# version 08
# speedup vs baseline: 1.2177x; 1.0265x over previous
import math
from contextlib import ExitStack

import numpy as np

import concourse.bass as bass
import concourse.mybir as mybir
from concourse.bass_utils import run_bass_kernel_spmd

F32 = mybir.dt.float32
BF16 = mybir.dt.bfloat16
I32 = mybir.dt.int32
AF = mybir.ActivationFunctionType
ALU = mybir.AluOpType
AX = mybir.AxisListType

B, S, D, H, HD = 4, 4096, 2048, 16, 128
NI = 5632
NIC = NI // 128
NDC = D // 128
EPS = 1e-5
TSC = 1.0 / (30.0 * math.sqrt(HD))

ENGS = ("sync", "gpsimd", "scalar", "vector", "tensor")


class Sem:
    __slots__ = ("h", "n")

    def __init__(self, h):
        self.h = h
        self.n = 0


class Ph:

    def __init__(self, nc):
        self.nc = nc
        self.st = {e: [] for e in ENGS}

    def wait(self, eng, sem, val=None):
        v = sem.n if val is None else val
        if v and v > 0:
            self.st[eng].append(lambda e, h=sem.h, vv=v: e.wait_ge(h, vv))

    def do(self, eng, build, sem=None, k=1):
        if sem is None:
            self.st[eng].append(lambda e, b=build: b(e))
            return None
        sem.n += k
        self.st[eng].append(
            lambda e, b=build, h=sem.h, kk=k: b(e).then_inc(h, kk))
        return sem.n

    def barrier(self, sB):
        for e in ENGS:
            self.do(e, lambda en: en.nop(), sB, 1)
        tgt = sB.n
        for e in ENGS:
            self.wait(e, sB, tgt)

    def run(self):
        with self.nc.Block() as blk:
            for e in ENGS:
                fns = self.st[e]
                if fns:
                    getattr(blk, e)(lambda eng, fns=fns: [f(eng) for f in fns])


def build_nc(dump=None, nphase=99):
    nc = bass.Bass("TRN2", target_bir_lowering=False)

    def kind_of(name):
        return "ExternalOutput" if dump == name else "Internal"

    x_h = nc.dram_tensor("x", [S, D], F32, kind="ExternalInput")
    wqkv_h = nc.dram_tensor("wqkvT", [NDC, 128, 3072], F32, kind="ExternalInput")
    bqkv_h = nc.dram_tensor("bqkv", [128, 24], F32, kind="ExternalInput")
    wo_h = nc.dram_tensor("woT", [NDC, 128, D], F32, kind="ExternalInput")
    bo_h = nc.dram_tensor("bo", [128, NDC], F32, kind="ExternalInput")
    gate_h = nc.dram_tensor("gateT", [NIC, NDC, 128, 128], F32, kind="ExternalInput")
    up_h = nc.dram_tensor("upT", [NIC, NDC, 128, 128], F32, kind="ExternalInput")
    down_h = nc.dram_tensor("downT", [NIC, 128, D], F32, kind="ExternalInput")
    gs_h = nc.dram_tensor("gsc", [128, NIC], F32, kind="ExternalInput")
    us_h = nc.dram_tensor("usc", [128, NIC], F32, kind="ExternalInput")
    ds_h = nc.dram_tensor("dsc", [128, NDC], F32, kind="ExternalInput")
    sl_h = nc.dram_tensor("slopes2", [2, 8], F32, kind="ExternalInput")
    out_h = nc.dram_tensor("out", [2048, D], F32, kind="ExternalOutput")

    xn_s = nc.dram_tensor("xn_s", [NDC, 128, S], BF16, kind=kind_of("xn_s"))
    q_s = nc.dram_tensor("q_s", [8, 128, S], BF16, kind=kind_of("q_s"))
    k_s = nc.dram_tensor("k_s", [8, 128, S], BF16, kind=kind_of("k_s"))
    v_s = nc.dram_tensor("v_s", [8, S, 128], BF16, kind=kind_of("v_s"))
    hdn_s = nc.dram_tensor("hdn_s", [NIC, 128, 2048], BF16, kind=kind_of("hdn_s"))
    f_s = nc.dram_tensor("f_s", [NDC, 128, 2048], F32, kind=kind_of("f_s"))
    ctxm = nc.dram_tensor("ctxm", [1024, S], BF16, kind=kind_of("ctxm"))
    relr_d = nc.dram_tensor("relr_d", [4, 512], F32, kind="Internal")
    rell_d = nc.dram_tensor("rell_d", [4, 33, 128], F32, kind="Internal")
    ctxg = nc.dram_tensor("ctxg", [2048, S], BF16, kind=kind_of("ctxg"))

    ctx = ExitStack()
    with ctx:
        def sb(name, shape, dtype):
            return ctx.enter_context(nc.sbuf_tensor(name, shape, dtype))

        def sem(name):
            return Sem(ctx.enter_context(nc.semaphore(name)))

        sQ = sem("sQ")
        sG = sem("sG")
        sC = sem("sC")
        sPE = sem("sPE")
        sSC = sem("sSC")
        sVE = sem("sVE")
        sGM = sem("sGM")
        sB = sem("sB")

        bqkv_sb = sb("bqkv_sb", [128, 24], F32)
        gs_sb = sb("gs_sb", [128, NIC], F32)
        us_sb = sb("us_sb", [128, NIC], F32)
        ds_sb = sb("ds_sb", [128, NDC], F32)
        bo_sb = sb("bo_sb", [128, NDC], F32)
        sl_sb = sb("sl_sb", [2, 8], F32)
        eps_sb = sb("eps_sb", [128, 1], F32)
        ones_bf = sb("ones_bf", [128, 128], BF16)
        id_bf = sb("id_bf", [128, 128], BF16)
        id_f32 = sb("id_f32", [128, 128], F32)
        mask_bf = sb("mask_bf", [128, 4, 512], BF16)

        own_reg = ctx.enter_context(nc.gpsimd.register("own_off_r"))

        p0 = Ph(nc)
        voff = {}

        def setup_reg(g):
            pid = g.partition_id()
            g.reg_mov(own_reg, pid)
            g.reg_mod(own_reg, own_reg, 2)
            g.reg_mul(own_reg, own_reg, 2048)
            voff["v"] = g.snap(own_reg, min_val=0, max_val=2048)
            return g.nop()
        p0.do("gpsimd", setup_reg)

        for h_, t_ in [(bqkv_h, bqkv_sb), (gs_h, gs_sb), (us_h, us_sb),
                       (ds_h, ds_sb), (bo_h, bo_sb), (sl_h, sl_sb)]:
            p0.do("sync", lambda e, i=h_, o=t_: e.dma_start(out=o[...], in_=i[...]),
                  sQ, 16)
        const_ld = sQ.n

        p0.do("vector", lambda e: e.memset(eps_sb[:, :], EPS), sVE, 1)
        eps_done = sVE.n

        p0.do("gpsimd", lambda e: e.memset(ones_bf[:, :], 1.0))
        p0.do("gpsimd", lambda e: e.memset(id_bf[:, :], 0.0))
        p0.do("gpsimd", lambda e: e.affine_select(
            out=id_bf[:, :], in_=id_bf[:, :], compare_op=ALU.not_equal,
            fill=1.0, base=0, pattern=[[-1, 128]], channel_multiplier=1))
        p0.do("gpsimd", lambda e: e.memset(id_f32[:, :], 0.0))
        p0.do("gpsimd", lambda e: e.affine_select(
            out=id_f32[:, :], in_=id_f32[:, :], compare_op=ALU.not_equal,
            fill=1.0, base=0, pattern=[[-1, 128]], channel_multiplier=1))
        p0.do("gpsimd", lambda e: e.memset(mask_bf[:, :, :], 1.0))
        p0.do("gpsimd", lambda e: e.affine_select(
            out=mask_bf[:, :, :], in_=mask_bf[:, :, :], compare_op=ALU.is_ge,
            fill=0.0, base=0, pattern=[[-128, 4], [1, 512]],
            channel_multiplier=-1), sGM, 1)
        consts_done = sGM.n
        p0.barrier(sB)
        p0.run()

        if nphase >= 1:
            with ExitStack() as pc:
                def psb(name, shape, dtype):
                    return pc.enter_context(nc.sbuf_tensor(name, shape, dtype))

                def pps(name, shape, dtype):
                    return pc.enter_context(nc.psum_tensor(name, shape, dtype))

                wqkv_sb = psb("wqkv_sb", [128, NDC, 3072], BF16)
                x_sb = [psb(f"x_sb{i}", [128, 4, D], BF16) for i in range(2)]
                xn_tm = psb("xn_tm", [128, 4, D], BF16)
                xn_fm = [psb(f"xn_fm{i}", [128, NDC, 512], BF16) for i in range(2)]
                st_sb = psb("st_sb", [128, 8, 4], F32)
                qkv_bf = [psb(f"qkv_bf{i}", [128, 512], BF16) for i in range(4)]
                vtok = [psb(f"vtok{i}", [128, 512], BF16) for i in range(2)]
                ps_qkv = [pps(f"ps_qkv{i}", [128, 512], F32) for i in range(4)]
                psTR = [pps(f"psTRa{i}", [128, 1024], BF16) for i in range(2)]

                sWQ = sem("sWQ")
                sX = [sem("sX0"), sem("sX1")]
                sFM = [sem("sFM0"), sem("sFM1")]
                sST = [sem(f"sST{i}") for i in range(4)]
                sVT = [sem("sVT0"), sem("sVT1")]
                pa = Ph(nc)
                pa.wait("scalar", sVE, eps_done)
                pa.do("scalar", lambda e: e.preload_act_table(AF.Sqrt))
                pa.wait("scalar", sQ, const_ld)
                pa.wait("tensor", sGM, consts_done)
                pa.wait("vector", sGM, consts_done)
                for dc in range(NDC):
                    pa.do("gpsimd", lambda e, o=wqkv_sb[:, dc, :], i=wqkv_h[dc]:
                          e.dma_start(out=o, in_=i), sWQ, 16)
                pa.wait("tensor", sWQ, sWQ.n)
                free = {}
                su = st_sb[:, 0, :]
                ssq = st_sb[:, 1, :]
                tm = st_sb[:, 2, :]
                vu = st_sb[:, 3, :]
                sd = st_sb[:, 4, :]
                rstd = st_sb[:, 5, :]
                nmr = st_sb[:, 6, :]

                def emit_ln(tt, stage):
                    xb = tt % 2
                    if stage == 0:
                        if ("x", xb) in free:
                            pa.wait("gpsimd", *free[("x", xb)])
                        xin = x_h[tt * 512:(tt + 1) * 512, :].rearrange(
                            "(p t) d -> t p d", t=128)
                        lv = pa.do("gpsimd", lambda e, o=x_sb[xb], i=xin:
                                   e.dma_start(out=o[:, :, :], in_=i),
                                   sX[xb], 16)
                        free[("xl", tt)] = (sX[xb], lv)
                        return
                    xb = tt % 2
                    pa.wait("scalar", *free[("xl", tt)])
                    if "stz" in free:
                        pa.wait("scalar", *free["stz"])
                    z1 = pa.do("scalar", lambda e: e.memzero(ssq), sSC, 1)
                    pa.wait("scalar", sSC, z1)
                    sq_done = None
                    for p in range(4):
                        sq_done = pa.do(
                            "scalar", lambda e, o=xn_tm[:, p, :],
                            i=x_sb[xb][:, p, :], ac=ssq[:, p:p + 1]:
                            e.activation(o, i, AF.Square, accum_out=ac),
                            sSC, 1)
                    pa.wait("vector", *free[("xl", tt)])
                    if "stzv" in free:
                        pa.wait("vector", *free["stzv"])
                    pa.do("vector", lambda e, o=su, i=x_sb[xb][:, :, :]:
                          e.reduce_sum(o, i, AX.X), sVE, 1)
                    pa.wait("vector", sSC, sq_done)
                    t3 = pa.do("vector", lambda e, o=tm, a=su:
                               e.scalar_tensor_tensor(
                                   o, a, 1.0 / D, a, op0=ALU.mult,
                                   op1=ALU.mult), sVE, 1)
                    pa.wait("vector", sVE, t3)
                    t4 = pa.do("vector", lambda e, o=vu, a=ssq, b=tm:
                               e.scalar_tensor_tensor(
                                   o, a, 1.0, b, op0=ALU.mult,
                                   op1=ALU.subtract), sVE, 1)
                    pa.wait("scalar", sVE, t4)
                    t5 = pa.do("scalar", lambda e, o=sd, i=vu:
                               e.activation(o, i, AF.Sqrt, scale=1.0 / D,
                                            bias=eps_sb[:, 0:1]), sSC, 1)
                    pa.wait("vector", sSC, t5)
                    t6 = pa.do("vector", lambda e, o=rstd, i=sd:
                               e.reciprocal(o, i), sVE, 1)
                    pa.wait("vector", sVE, t6)
                    t7 = pa.do("vector", lambda e, o=nmr, a=su, b=rstd:
                               e.scalar_tensor_tensor(
                                   o, a, -1.0 / D, b, op0=ALU.mult,
                                   op1=ALU.mult), sVE, 1)
                    pa.wait("scalar", sVE, t7)
                    xn_done = None
                    for p in range(4):
                        xn_done = pa.do(
                            "scalar", lambda e, o=xn_tm[:, p, :],
                            i=x_sb[xb][:, p, :],
                            sc=rstd[:, p:p + 1], bi=nmr[:, p:p + 1]:
                            e.activation(o, i, AF.Identity, scale=sc,
                                         bias=bi), sSC, 1)
                    free["stz"] = (sSC, xn_done)
                    free["stzv"] = (sSC, xn_done)
                    free[("x", tt % 2)] = (sSC, xn_done)
                    free[("xn", tt)] = (sSC, xn_done)

                emit_ln(0, 0)
                emit_ln(0, 1)
                for tt in range(8):
                    fb = tt % 2
                    cp_last = None
                    for gi in range(8):
                        p, dch = gi // 2, gi % 2
                        tb = gi % 2
                        if gi == 0:
                            pa.wait("tensor", *free[("xn", tt)])
                            if ("fm", fb) in free:
                                pa.wait("tensor", *free[("fm", fb)])
                        if ("tr", tb) in free:
                            pa.wait("tensor", *free[("tr", tb)])
                        last = None
                        for s_ in range(8):
                            dc = dch * 8 + s_
                            last = pa.do(
                                "tensor", lambda e,
                                o=psTR[tb][:, s_ * 128:(s_ + 1) * 128],
                                i=xn_tm[:, p, dc * 128:(dc + 1) * 128]:
                                e.transpose(o, i, id_bf[:, :]),
                                sPE if s_ == 7 else None, 1)
                        pa.wait("vector", sPE, last)
                        ov = xn_fm[fb][:, dch * 8:dch * 8 + 8,
                                       p * 128:(p + 1) * 128]
                        iv = psTR[tb].ap().rearrange("p (s d) -> p s d", s=8)
                        cp_last = pa.do("vector", lambda e, o=ov, i=iv:
                                        e.tensor_copy(o, i), sVE, 1)
                        trfree_val = (sVE, cp_last)
                        free[("tr", tb)] = trfree_val
                    pa.wait("sync", sVE, cp_last)
                    xout = xn_s.ap().rearrange("a p t -> p a t")[
                        :, :, tt * 512:(tt + 1) * 512]
                    stv = pa.do("sync", lambda e, o=xout, i=xn_fm[fb][:, :, :]:
                                e.dma_start(out=o, in_=i), sFM[fb], 16)
                    if tt + 1 < 8:
                        emit_ln(tt + 1, 0)
                    for fc in range(24):
                        qb = fc % 4
                        eb = fc % 4
                        if ("ps", qb) in free:
                            pa.wait("tensor", *free[("ps", qb)])
                        mm = None
                        for dc in range(NDC):
                            mm = pa.do(
                                "tensor",
                                lambda e, o=ps_qkv[qb][:, :],
                                l=wqkv_sb[:, dc, fc * 128:(fc + 1) * 128],
                                rr=xn_fm[fb][:, dc, :], s0=(dc == 0),
                                s1=(dc == NDC - 1):
                                e.matmul(o, l, rr, start=s0, stop=s1),
                                sPE if dc == NDC - 1 else None, 1)
                        if fc == 8 and tt + 1 < 8:
                            emit_ln(tt + 1, 1)
                        pa.wait("scalar", sPE, mm)
                        if ("eb", eb) in free:
                            pa.wait("scalar", *free[("eb", eb)])
                        ev = pa.do("scalar", lambda e, o=qkv_bf[eb][:, :],
                                   i=ps_qkv[qb][:, :], bi=bqkv_sb[:, fc:fc + 1]:
                                   e.activation(o, i, AF.Identity, scale=1.0,
                                                bias=bi), sSC, 1)
                        free[("ps", qb)] = (sSC, ev)
                        if fc < 16:
                            dst = (q_s if fc < 8 else k_s)[
                                fc % 8, :, tt * 512:(tt + 1) * 512]
                            pa.wait("sync", sSC, ev)
                            stv2 = pa.do("sync", lambda e, o=dst,
                                         i=qkv_bf[eb][:, :]:
                                         e.dma_start(out=o, in_=i),
                                         sST[eb], 16)
                            free[("eb", eb)] = (sST[eb], stv2)
                        else:
                            hh = fc - 16
                            tb = fc % 2
                            pa.wait("tensor", sSC, ev)
                            if ("tr", tb) in free:
                                pa.wait("tensor", *free[("tr", tb)])
                            last = None
                            for s_ in range(4):
                                last = pa.do(
                                    "tensor",
                                    lambda e,
                                    o=psTR[tb][:, s_ * 128:(s_ + 1) * 128],
                                    i=qkv_bf[eb][:, s_ * 128:(s_ + 1) * 128]:
                                    e.transpose(o, i, id_bf[:, :]),
                                    sPE if s_ == 3 else None, 1)
                            free[("eb", eb)] = (sPE, last)
                            vb = fc % 2
                            pa.wait("vector", sPE, last)
                            if ("vt", vb) in free:
                                pa.wait("vector", *free[("vt", vb)])
                            cp = pa.do("vector", lambda e, o=vtok[vb][:, :],
                                       i=psTR[tb][:, 0:512]:
                                       e.tensor_copy(o, i), sVE, 1)
                            free[("tr", tb)] = (sVE, cp)
                            pa.wait("sync", sVE, cp)
                            vdst = v_s[hh, tt * 512:(tt + 1) * 512,
                                       :].rearrange("(s t) d -> t s d", t=128)
                            ivv = vtok[vb].ap().rearrange(
                                "p (s d) -> p s d", s=4)
                            stv2 = pa.do("sync", lambda e, o=vdst, i=ivv:
                                         e.dma_start(out=o, in_=i),
                                         sVT[vb], 16)
                            free[("vt", vb)] = (sVT[vb], stv2)
                    free[("fm", fb)] = (sPE, sPE.n)
                for s_e in sST + sVT + sFM:
                    pa.wait("sync", s_e, s_e.n)
                for s_e in sX + [sWQ]:
                    pa.wait("gpsimd", s_e, s_e.n)
                pa.barrier(sB)
                pa.run()

        if nphase >= 3:
            with ExitStack() as pc:
                def psb(name, shape, dtype):
                    return pc.enter_context(nc.sbuf_tensor(name, shape, dtype))

                def pps(name, shape, dtype):
                    return pc.enter_context(nc.psum_tensor(name, shape, dtype))

                xn_sb = psb("xn_sbB", [128, NDC, 2048], BF16)
                gw_sb = [psb(f"gw{i}", [128, NDC, 128], BF16) for i in range(2)]
                uw_sb = [psb(f"uw{i}", [128, NDC, 128], BF16) for i in range(2)]
                gsil = [psb(f"gsil{i}", [128, 512], F32) for i in range(2)]
                hdn_bf = [psb(f"hdnb{i}", [128, 512], BF16) for i in range(2)]
                ps_g = [pps(f"ps_g{i}", [128, 512], F32) for i in range(2)]
                ps_u = [pps(f"ps_u{i}", [128, 512], F32) for i in range(2)]

                sXB = sem("sXB")
                sW = [sem("sW0"), sem("sW1")]
                sHB = [sem("sHB0"), sem("sHB1")]
                pb = Ph(nc)
                pb.do("scalar", lambda e: e.preload_act_table(AF.Silu))
                xin = xn_s.ap().rearrange("a p t -> p a t")[
                    :, :, bass.ds(voff["v"], 2048)]
                xl = pb.do("gpsimd", lambda e, o=xn_sb, i=xin:
                           e.dma_start(out=o[:, :, :], in_=i), sXB, 16)
                pb.wait("tensor", sXB, xl)
                pb.wait("scalar", sQ, const_ld)
                free = {}
                for ic in range(NIC):
                    wb = ic % 2
                    if ("w", wb) in free:
                        pb.wait("gpsimd", *free[("w", wb)])
                    gin = gate_h[ic].rearrange("a p d -> p a d")
                    uin = up_h[ic].rearrange("a p d -> p a d")
                    pb.do("gpsimd", lambda e, o=gw_sb[wb], i=gin:
                          e.dma_start(out=o[:, :, :], in_=i), sW[wb], 16)
                    wl = pb.do("gpsimd", lambda e, o=uw_sb[wb], i=uin:
                               e.dma_start(out=o[:, :, :], in_=i), sW[wb], 16)
                    pb.wait("tensor", sW[wb], wl)
                    last_mm = None
                    for tt in range(4):
                        gb = tt % 2
                        if ("pg", gb) in free:
                            pb.wait("tensor", *free[("pg", gb)])
                        mmg = None
                        for dc in range(NDC):
                            mmg = pa_mm = pb.do(
                                "tensor", lambda e, o=ps_g[gb][:, :],
                                l=gw_sb[wb][:, dc, :],
                                rr=xn_sb[:, dc, tt * 512:(tt + 1) * 512],
                                s0=(dc == 0), s1=(dc == NDC - 1):
                                e.matmul(o, l, rr, start=s0, stop=s1),
                                sPE if dc == NDC - 1 else None, 1)
                        if ("pu", gb) in free:
                            pb.wait("tensor", *free[("pu", gb)])
                        mmu = None
                        for dc in range(NDC):
                            mmu = pb.do(
                                "tensor", lambda e, o=ps_u[gb][:, :],
                                l=uw_sb[wb][:, dc, :],
                                rr=xn_sb[:, dc, tt * 512:(tt + 1) * 512],
                                s0=(dc == 0), s1=(dc == NDC - 1):
                                e.matmul(o, l, rr, start=s0, stop=s1),
                                sPE if dc == NDC - 1 else None, 1)
                        last_mm = mmu
                        sb2 = tt % 2
                        pb.wait("scalar", sPE, mmg)
                        if ("gs", sb2) in free:
                            pb.wait("scalar", *free[("gs", sb2)])
                        sl = pb.do("scalar", lambda e, o=gsil[sb2][:, :],
                                   i=ps_g[gb][:, :], sc=gs_sb[:, ic:ic + 1]:
                                   e.activation(o, i, AF.Silu, scale=sc),
                                   sSC, 1)
                        free[("pg", gb)] = (sSC, sl)
                        hb2 = tt % 2
                        pb.wait("vector", sSC, sl)
                        pb.wait("vector", sPE, mmu)
                        if ("hb", hb2) in free:
                            pb.wait("vector", *free[("hb", hb2)])
                        hv = pb.do("vector", lambda e, o=hdn_bf[hb2][:, :],
                                   i=ps_u[gb][:, :], sc=us_sb[:, ic:ic + 1],
                                   g2=gsil[sb2][:, :]:
                                   e.scalar_tensor_tensor(
                                       o, i, sc, g2, op0=ALU.mult,
                                       op1=ALU.mult), sVE, 1)
                        free[("pu", gb)] = (sVE, hv)
                        free[("gs", sb2)] = (sVE, hv)
                        pb.wait("sync", sVE, hv)
                        stv = pb.do("sync", lambda e,
                                    o=hdn_s[ic, :, tt * 512:(tt + 1) * 512],
                                    i=hdn_bf[hb2][:, :]:
                                    e.dma_start(out=o, in_=i), sHB[hb2], 16)
                        free[("hb", hb2)] = (sHB[hb2], stv)
                    free[("w", wb)] = (sPE, last_mm)
                for s_e in sHB:
                    pb.wait("sync", s_e, s_e.n)
                pb.wait("gpsimd", sXB, sXB.n)
                for s_e in sW:
                    pb.wait("gpsimd", s_e, s_e.n)
                pb.barrier(sB)
                pb.run()

        if nphase >= 4:
            with ExitStack() as pc:
                def psb(name, shape, dtype):
                    return pc.enter_context(nc.sbuf_tensor(name, shape, dtype))

                def pps(name, shape, dtype):
                    return pc.enter_context(nc.psum_tensor(name, shape, dtype))

                k_sb = [psb(f"k_sb{i}", [128, S], BF16) for i in range(2)]
                q_sb = [psb(f"q_sb{i}", [128, S], BF16) for i in range(2)]
                v_sb = [psb(f"v_sb{i}", [128, 32, 128], BF16) for i in range(2)]
                rhsRel = psb("rhsRel", [4, 512], BF16)
                lhsRel = psb("lhsRel", [4, 33, 128], BF16)
                slb = psb("slb", [128, 8], F32)
                rsl = psb("rsl", [128, 8], F32)
                slTSC = psb("slTSC", [128, 8], F32)
                ones1 = psb("ones1", [1, 128], F32)
                wspA = psb("wspA", [1, 512], F32)
                wspH = psb("wspH", [1, 512], F32)
                wspB = psb("wspB", [1, 512], F32)
                wspL = psb("wspL", [1, 33, 128], F32)
                tn = [psb(f"tn{i}", [128, 1024], F32) for i in range(2)]
                pr = [psb(f"pr{i}", [128, 1024], BF16) for i in range(4)]
                rc = [psb(f"rc{i}", [128, 512], F32) for i in range(2)]
                cbf = [psb(f"cbf{i}", [128, 512], BF16) for i in range(2)]
                dw_q = psb("dw_q", [128, 2, 22, 512], BF16)
                hdn_fl = psb("hdn_fl", [128, 2, 22, 512], BF16)
                fD = [psb(f"fD{i}", [128, 512], F32) for i in range(2)]
                psS = [pps(f"psS{i}", [128, 1024], F32) for i in range(2)]
                psCx = pps("psCx", [128, 512], F32)
                psDn = pps("psDn", [128, 512], F32)
                psD = [pps(f"psD{i}", [128, 512], F32) for i in range(2)]

                sHL = [sem("sHL0"), sem("sHL1")]
                sCS = [sem("sCS0"), sem("sCS1")]
                sRELs = sem("sRELs")
                sRELl = sem("sRELl")
                pcx = Ph(nc)
                pcx.do("scalar", lambda e: e.preload_act_table(AF.Tanh))
                pcx.wait("gpsimd", sGM, consts_done)
                pcx.wait("tensor", sGM, consts_done)
                pcx.wait("vector", sGM, consts_done)
                pcx.wait("vector", sQ, const_ld)
                pcx.do("scalar", lambda e: e.dma_start(
                    out=k_sb[0][:, :], in_=k_s[0]), sHL[0], 16)
                pcx.do("scalar", lambda e: e.dma_start(
                    out=q_sb[0][:, :], in_=q_s[0]), sHL[0], 16)
                h0l = pcx.do("scalar", lambda e: e.dma_start(
                    out=v_sb[0][:, :, :],
                    in_=v_s[0].rearrange("(s t) d -> t s d", t=128)),
                    sHL[0], 16)
                g1 = pcx.do("gpsimd", lambda e: e.memset(ones1[:, :], 1.0),
                            sGM, 1)
                pcx.wait("tensor", sGM, g1)
                pcx.wait("tensor", sQ, const_ld)
                bmm = pcx.do("tensor", lambda e: e.matmul(
                    psCx[:, 0:8], ones1[0:1, :], sl_sb[0:1, 0:8],
                    start=True, stop=True, skip_group_check=True), sPE, 1)
                pcx.wait("vector", sPE, bmm)
                v1 = pcx.do("vector", lambda e: e.tensor_copy(
                    slb[:, :], psCx[:, 0:8]), sVE, 1)
                pcx.wait("vector", sVE, v1)
                v2 = pcx.do("vector", lambda e: e.reciprocal(
                    rsl[:, :], slb[:, :]), sVE, 1)
                pcx.wait("vector", sVE, v2)
                v3 = pcx.do("vector", lambda e: e.tensor_scalar_mul(
                    slTSC[:, :], slb[:, :], TSC), sVE, 1)
                setup_norm = v1
                gi1 = pcx.do("gpsimd", lambda e: e.iota(
                    wspA[:, :], [[1, 512]], channel_multiplier=0,
                    allow_small_or_imprecise_dtypes=True))
                pcx.do("gpsimd", lambda e: e.memset(wspH[:, :], 1.0))
                gi2 = pcx.do("gpsimd", lambda e: e.affine_select(
                    out=wspH[:, :], in_=wspH[:, :], compare_op=ALU.is_ge,
                    fill=0.0, base=-256, pattern=[[1, 512]],
                    channel_multiplier=0), sGM, 1)
                pcx.wait("vector", sGM, gi2)
                v4 = pcx.do("vector", lambda e: e.scalar_tensor_tensor(
                    wspB[:, :], wspH[:, :], 256.0, wspA[:, :],
                    op0=ALU.mult, op1=ALU.subtract), sVE, 1)
                pcx.wait("vector", sVE, v4)
                v5 = pcx.do("vector", lambda e: e.tensor_scalar_mul(
                    wspA[:, :], wspH[:, :], -256.0), sVE, 1)
                pcx.wait("vector", sVE, v5)
                v6 = pcx.do("vector", lambda e: e.memset(wspH[:, :], 1.0),
                            sVE, 1)
                pcx.wait("sync", sVE, v6)
                pcx.do("sync", lambda e: e.dma_start(
                    out=relr_d[0:1, :], in_=wspH[0:1, :]), sRELs, 16)
                pcx.do("sync", lambda e: e.dma_start(
                    out=relr_d[1:2, :], in_=wspB[0:1, :]), sRELs, 16)
                pcx.do("sync", lambda e: e.dma_start(
                    out=relr_d[2:3, :], in_=wspA[0:1, :]), sRELs, 16)
                pcx.do("sync", lambda e: e.dma_start(
                    out=relr_d[3:4, :], in_=wspH[0:1, :]), sRELs, 16)
                gi3 = pcx.do("gpsimd", lambda e: e.iota(
                    wspL[:, :, :], [[0, 33], [1, 128]], channel_multiplier=0,
                    allow_small_or_imprecise_dtypes=True), sGM, 1)
                pcx.wait("sync", sGM, gi3)
                pcx.do("sync", lambda e: e.dma_start(
                    out=rell_d[0:1, :, :], in_=wspL[0:1, :, :]), sRELs, 16)
                pcx.wait("gpsimd", sRELs, sRELs.n)
                gi4 = pcx.do("gpsimd", lambda e: e.iota(
                    wspL[:, :, :], [[128, 33], [0, 128]],
                    channel_multiplier=0,
                    allow_small_or_imprecise_dtypes=True), sGM, 1)
                pcx.wait("vector", sGM, gi4)
                v7 = pcx.do("vector", lambda e: e.tensor_scalar_add(
                    wspL[:, :, :], wspL[:, :, :], -3584.0), sVE, 1)
                pcx.wait("sync", sVE, v7)
                pcx.do("sync", lambda e: e.dma_start(
                    out=rell_d[3:4, :, :], in_=wspL[0:1, :, :]), sRELs, 16)
                pcx.wait("vector", sRELs, sRELs.n)
                v8 = pcx.do("vector", lambda e: e.memset(wspL[:, :, :], 1.0),
                            sVE, 1)
                pcx.wait("sync", sVE, v8)
                pcx.do("sync", lambda e: e.dma_start(
                    out=rell_d[1:2, :, :], in_=wspL[0:1, :, :]), sRELs, 16)
                pcx.do("sync", lambda e: e.dma_start(
                    out=rell_d[2:3, :, :], in_=wspL[0:1, :, :]), sRELs, 16)
                pcx.wait("gpsimd", sRELs, sRELs.n)
                pcx.do("gpsimd", lambda e: e.dma_start(
                    out=rhsRel[:, :], in_=relr_d[:, :]), sRELl, 16)
                rel_ld = pcx.do("gpsimd", lambda e: e.dma_start(
                    out=lhsRel[:, :, :], in_=rell_d[:, :, :]), sRELl, 16)
                pcx.wait("tensor", sRELl, rel_ld)
                grp_free = {}
                pr_free = {}
                head_free = {}
                norm_done_val = setup_norm
                gi_ctr = 0
                sDWh = [sem("sDWh0"), sem("sDWh1")]
                sHFh = [sem("sHFh0"), sem("sHFh1")]
                sFS = [sem("sFS0"), sem("sFS1")]
                dstate = {"dw_war": {}, "hdn_war": {}, "dw_ld": {},
                          "hdn_ld": {}, "evac": {}, "fwar": {},
                          "last_mm": None}

                def mk_dw_load(qr, hf):
                    def go():
                        if hf in dstate["dw_war"]:
                            pcx.wait("gpsimd", sPE, dstate["dw_war"][hf])
                        lv = None
                        for icl in range(22):
                            ic = hf * 22 + icl
                            din = down_h[ic, :, qr * 512:(qr + 1) * 512]
                            lv = pcx.do("gpsimd",
                                        lambda e, o=dw_q[:, hf, icl, :],
                                        i=din: e.dma_start(out=o, in_=i),
                                        sDWh[hf], 16)
                        dstate["dw_ld"][hf] = lv
                    return go

                def mk_hdn_load(qr, tt, hf):
                    def go():
                        if hf in dstate["hdn_war"]:
                            pcx.wait("sync", sPE, dstate["hdn_war"][hf])
                        hin = hdn_s[hf * 22:hf * 22 + 22, :,
                                    tt * 512:(tt + 1) * 512].rearrange(
                                        "a p d -> p a d")
                        lv = pcx.do("sync", lambda e,
                                    o=hdn_fl[:, hf, :, :], i=hin:
                                    e.dma_start(out=o, in_=i), sHFh[hf], 16)
                        dstate["hdn_ld"][hf] = lv
                    return go

                def mk_mms(qr, tt, pas, hf, ic0, icn):
                    def go():
                        if ic0 == 0:
                            pcx.wait("tensor", sDWh[hf], dstate["dw_ld"][hf])
                            pcx.wait("tensor", sHFh[hf], dstate["hdn_ld"][hf])
                            if hf == 0 and pas == 0:
                                pass
                            if hf == 0:
                                for m in range(2):
                                    if m in dstate["evac"]:
                                        pcx.wait("tensor", sVE,
                                                 dstate["evac"][m])
                        lm = None
                        for icl in range(ic0, ic0 + icn):
                            for m in range(2):
                                mc_loc = 2 * pas + m
                                st0 = (hf == 0 and icl == 0)
                                st1 = (hf == 1 and icl == 21)
                                lm = pcx.do(
                                    "tensor", lambda e, o=psD[m][:, :],
                                    l=dw_q[:, hf, icl,
                                           mc_loc * 128:(mc_loc + 1) * 128],
                                    rr=hdn_fl[:, hf, icl, :],
                                    s0=st0, s1=st1:
                                    e.matmul(o, l, rr, start=s0, stop=s1,
                                             skip_group_check=True),
                                    sPE, 1)
                        dstate["last_mm"] = lm
                        dstate["dw_war"][hf] = lm
                        dstate["hdn_war"][hf] = lm
                    return go

                def mk_evac(qr, tt, pas):
                    def go():
                        pcx.wait("vector", sPE, dstate["last_mm"])
                        for m in range(2):
                            mc_g = qr * 4 + 2 * pas + m
                            if m in dstate["fwar"]:
                                pcx.wait("vector", sFS[m], dstate["fwar"][m])
                            ev = pcx.do(
                                "vector", lambda e, o=fD[m][:, :],
                                i=psD[m][:, :],
                                sc=ds_sb[:, mc_g:mc_g + 1]:
                                e.tensor_single_scalar(o, i, sc, ALU.mult),
                                sVE, 1)
                            dstate["evac"][m] = ev
                            pcx.wait("sync", sVE, ev)
                            stv = pcx.do(
                                "sync", lambda e,
                                o=f_s[mc_g, :, tt * 512:(tt + 1) * 512],
                                i=fD[m][:, :]:
                                e.dma_start(out=o, in_=i), sFS[m], 16)
                            dstate["fwar"][m] = stv
                    return go

                down_steps = []
                for qr in range(4):
                    down_steps.append(mk_dw_load(qr, 0))
                    down_steps.append(mk_dw_load(qr, 1))
                    for tt in range(4):
                        down_steps.append(mk_hdn_load(qr, tt, 0))
                        down_steps.append(mk_hdn_load(qr, tt, 1))
                        for pas in range(2):
                            for hf in range(2):
                                down_steps.append(mk_mms(qr, tt, pas, hf, 0, 11))
                                down_steps.append(mk_mms(qr, tt, pas, hf, 11, 11))
                            down_steps.append(mk_evac(qr, tt, pas))
                dtot = len(down_steps)
                ddone = [0]
                batch_ctr = [0]

                def pace_down(final=False):
                    if final:
                        tgt = dtot
                    else:
                        tgt = min(dtot, int(dtot * batch_ctr[0] / 576.0))
                    while ddone[0] < tgt:
                        down_steps[ddone[0]]()
                        ddone[0] += 1
                for h in range(8):
                    hb = h % 2
                    if h == 0:
                        hl = h0l
                    if h > 0 and hb in head_free:
                        pcx.wait("scalar", *head_free[hb])
                    if h > 0:
                        pcx.do("scalar", lambda e, o=k_sb[hb][:, :], i=k_s[h]:
                           e.dma_start(out=o, in_=i), sHL[hb], 16)
                        pcx.do("scalar", lambda e, o=q_sb[hb][:, :], i=q_s[h]:
                               e.dma_start(out=o, in_=i), sHL[hb], 16)
                        vin = v_s[h].rearrange("(s t) d -> t s d", t=128)
                        hl = pcx.do("scalar", lambda e, o=v_sb[hb][:, :, :],
                                    i=vin:
                                    e.dma_start(out=o, in_=i), sHL[hb], 16)
                    pcx.wait("vector", sHL[hb], hl)
                    pcx.wait("vector", sVE, v3)
                    qpv = pcx.do("vector", lambda e, o=q_sb[hb][:, :],
                                 sc=rsl[:, h:h + 1]:
                                 e.tensor_single_scalar(o, o, sc, ALU.mult),
                                 sVE, 1)
                    pcx.wait("tensor", sHL[hb], hl)
                    pcx.wait("tensor", sVE, qpv)
                    h_last_pe = None
                    for qt in range(8):
                        nb = 2 * (qt + 1)
                        pend = []
                        for j in range(nb):
                            g = gi_ctr % 2
                            pb_ = gi_ctr % 4
                            gi_ctr += 1
                            if g in grp_free:
                                pcx.wait("tensor", sSC, grp_free[g])
                            sc_done = [None, None]
                            for u in range(2):
                                kt = 2 * j + u
                                pcx.do("tensor", lambda e,
                                       o=psS[g][:, u * 512:(u + 1) * 512],
                                       l=k_sb[hb][:, kt * 128:(kt + 1) * 128],
                                       rr=q_sb[hb][:, qt * 512:(qt + 1) * 512]:
                                       e.matmul(o, l, rr, start=True,
                                                stop=False,
                                                skip_group_check=True))
                                didx = kt - 4 * qt + 28
                                sc_done[u] = pcx.do(
                                    "tensor", lambda e,
                                    o=psS[g][:, u * 512:(u + 1) * 512],
                                    l=lhsRel[0:4, didx, :],
                                    rr=rhsRel[0:4, :]:
                                    e.matmul(o, l, rr, start=False, stop=True,
                                             skip_group_check=True),
                                    sPE, 1)
                            diag = j >= nb - 2
                            off = 0 if j == nb - 2 else 2
                            pv = [None, None]
                            mv = [None, None]
                            for u in range(2):
                                pcx.wait("scalar", sPE, sc_done[u])
                                tv = pcx.do(
                                    "scalar", lambda e,
                                    o=tn[g][:, u * 512:(u + 1) * 512],
                                    i=psS[g][:, u * 512:(u + 1) * 512],
                                    sc=slTSC[:, h:h + 1]:
                                    e.activation(o, i, AF.Tanh, scale=sc),
                                    sSC, 1)
                                if u == 1:
                                    grp_free[g] = tv
                                if u == 0 and pb_ in pr_free:
                                    pcx.wait("scalar", sPE, pr_free[pb_])
                                pv[u] = pcx.do(
                                    "scalar", lambda e,
                                    o=pr[pb_][:, u * 512:(u + 1) * 512],
                                    i=tn[g][:, u * 512:(u + 1) * 512]:
                                    e.activation(o, i, AF.Exp, scale=30.0),
                                    sSC, 1)
                                if diag:
                                    pcx.wait("vector", sSC, pv[u])
                                    mv[u] = pcx.do(
                                        "vector", lambda e,
                                        o=pr[pb_][:, u * 512:(u + 1) * 512],
                                        m=mask_bf[:, off + u, :]:
                                        e.tensor_tensor(o, o, m, ALU.mult),
                                        sVE, 1)
                            pend.append((j, g, pb_, pv, mv))
                            while pend and (len(pend) > 2 or j == nb - 1):
                                jj, gg, pp2, pvv, mvv = pend.pop(0)
                                if jj == 0 and norm_done_val is not None:
                                    pcx.wait("tensor", sVE, norm_done_val)
                                for u in range(2):
                                    if mvv[u] is not None:
                                        pcx.wait("tensor", sVE, mvv[u])
                                    else:
                                        pcx.wait("tensor", sSC, pvv[u])
                                    kt = 2 * jj + u
                                    st0 = (jj == 0 and u == 0)
                                    st1 = (jj == nb - 1 and u == 1)
                                    pcx.do("tensor", lambda e, o=psCx[:, :],
                                           l=v_sb[hb][:, kt, :],
                                           rr=pr[pp2][:, u * 512:(u + 1) * 512],
                                           s0=st0, s1=st1:
                                           e.matmul(o, l, rr, start=s0,
                                                    stop=s1,
                                                    skip_group_check=True))
                                cd = None
                                for u in range(2):
                                    st0 = (jj == 0 and u == 0)
                                    st1 = (jj == nb - 1 and u == 1)
                                    cd = pcx.do(
                                        "tensor", lambda e, o=psDn[:, :],
                                        l=ones_bf[:, :],
                                        rr=pr[pp2][:, u * 512:(u + 1) * 512],
                                        s0=st0, s1=st1:
                                        e.matmul(o, l, rr, start=s0, stop=s1,
                                                 skip_group_check=True),
                                        sPE if u == 1 else None, 1)
                                pr_free[pp2] = cd
                                h_last_pe = cd
                                batch_ctr[0] += 1
                                pace_down()
                        qb2 = (h * 8 + qt) % 2
                        pcx.wait("vector", sPE, h_last_pe)
                        if ("cbf", qb2) in pr_free:
                            pcx.wait("vector", sCS[qb2], pr_free[("cbf", qb2)])
                        pcx.do("vector", lambda e, o=rc[qb2][:, :],
                               i=psDn[:, :]: e.reciprocal(o, i))
                        nv = pcx.do("vector", lambda e, o=cbf[qb2][:, :],
                                    a=psCx[:, :], b=rc[qb2][:, :]:
                                    e.tensor_mul(o, a, b), sVE, 1)
                        norm_done_val = nv
                        pcx.wait("sync", sVE, nv)
                        stc = pcx.do("sync", lambda e,
                               o=ctxm[h * 128:(h + 1) * 128,
                                      qt * 512:(qt + 1) * 512],
                               i=cbf[qb2][:, :]:
                               e.dma_start(out=o, in_=i), sCS[qb2], 16)
                        pr_free[("cbf", qb2)] = stc
                    head_free[hb] = (sPE, h_last_pe)
                pace_down(final=True)
                pcx.wait("gpsimd", sCS[0], sCS[0].n)
                pcx.wait("gpsimd", sCS[1], sCS[1].n)
                if dump != "ctxm":
                    pcx.do("gpsimd", lambda e: e.collective_compute(
                    "AllGather", ALU.bypass,
                    replica_groups=[[0, 1], [2, 3], [4, 5], [6, 7]],
                        ins=[ctxm.ap().opt()],
                        outs=[ctxg.ap().opt()]), sC, 1)
                for s_e in sHL + sCS + [sHFh[0], sHFh[1], sFS[0], sFS[1]]:
                    pcx.wait("sync", s_e, s_e.n)
                for s_e in sDWh:
                    pcx.wait("gpsimd", s_e, s_e.n)
                pcx.barrier(sB)
                pcx.run()

        if False:
            with ExitStack() as pc:
                def psb(name, shape, dtype):
                    return pc.enter_context(nc.sbuf_tensor(name, shape, dtype))

                def pps(name, shape, dtype):
                    return pc.enter_context(nc.psum_tensor(name, shape, dtype))

                dw_sb = psb("dw_sb", [128, NIC, 1024], BF16)
                hdn_fl = psb("hdn_fl", [128, NIC, 512], BF16)
                f_buf = [psb(f"f_buf{i}", [128, 512], F32) for i in range(2)]
                ps_dn = [pps(f"ps_dn{i}", [128, 512], F32) for i in range(8)]

                sDW = sem("sDW")
                sHF = sem("sHF")
                sFS = [sem("sFS0"), sem("sFS1")]
                pc2 = Ph(nc)
                free = {}
                for half in range(2):
                    if "dw" in free:
                        pc2.wait("gpsimd", *free["dw"])
                    dl = None
                    for ic in range(NIC):
                        din = down_h[ic, :, half * 1024:(half + 1) * 1024]
                        dl = pc2.do("gpsimd", lambda e, o=dw_sb[:, ic, :],
                                    i=din: e.dma_start(out=o, in_=i), sDW, 16)
                    pc2.wait("tensor", sDW, dl)
                    last_mm = None
                    for tt in range(4):
                        if "hdn" in free:
                            pc2.wait("sync", *free["hdn"])
                        hin = hdn_s[:, :, tt * 512:(tt + 1) * 512].rearrange(
                            "a p d -> p a d")
                        hl = pc2.do("sync", lambda e, o=hdn_fl[:, :, :], i=hin:
                                    e.dma_start(out=o, in_=i), sHF, 16)
                        pc2.wait("tensor", sHF, hl)
                        if "ev" in free:
                            pc2.wait("tensor", *free["ev"])
                        for ic in range(NIC):
                            for mc in range(8):
                                is_l = ic == NIC - 1 and mc == 7
                                last_mm = pc2.do(
                                    "tensor", lambda e, o=ps_dn[mc][:, :],
                                    l=dw_sb[:, ic, mc * 128:(mc + 1) * 128],
                                    rr=hdn_fl[:, ic, :], s0=(ic == 0),
                                    s1=(ic == NIC - 1):
                                    e.matmul(o, l, rr, start=s0, stop=s1,
                                             skip_group_check=True),
                                    sPE if is_l else None, 1)
                        free["hdn"] = (sPE, last_mm)
                        pc2.wait("vector", sPE, last_mm)
                        evs = []
                        for mc in range(8):
                            fb2 = mc % 2
                            if ("f", fb2) in free:
                                pc2.wait("vector", *free[("f", fb2)])
                            ev = pc2.do(
                                "vector", lambda e, o=f_buf[fb2][:, :],
                                i=ps_dn[mc][:, :],
                                sc=ds_sb[:, half * 8 + mc:half * 8 + mc + 1]:
                                e.tensor_single_scalar(o, i, sc, ALU.mult),
                                sVE, 1)
                            evs.append(ev)
                            pc2.wait("sync", sVE, ev)
                            stv = pc2.do(
                                "sync", lambda e,
                                o=f_s[half * 8 + mc, :,
                                      tt * 512:(tt + 1) * 512],
                                i=f_buf[fb2][:, :]:
                                e.dma_start(out=o, in_=i), sFS[fb2], 16)
                            free[("f", fb2)] = (sFS[fb2], stv)
                        free["ev"] = (sVE, evs[-1])
                    free["dw"] = (sPE, last_mm)
                for s_e in [sHF] + sFS:
                    pc2.wait("sync", s_e, s_e.n)
                pc2.wait("gpsimd", sDW, sDW.n)
                pc2.barrier(sB)
                pc2.run()

        if nphase >= 5:
            with ExitStack() as pc:
                def psb(name, shape, dtype):
                    return pc.enter_context(nc.sbuf_tensor(name, shape, dtype))

                def pps(name, shape, dtype):
                    return pc.enter_context(nc.psum_tensor(name, shape, dtype))

                wo_sb = psb("wo_sb", [128, NDC, 2048], BF16)
                ctx_tl = [psb(f"ctx_tl{i}", [128, NDC, 512], BF16)
                          for i in range(2)]
                x_tl = psb("x_tl", [128, 4, D], F32)
                out_sb = psb("out_sb", [128, 4, D], F32)
                f_tl = [psb(f"f_tl{i}", [128, 512], F32) for i in range(4)]
                gsil = [psb(f"gsilD{i}", [128, 512], F32) for i in range(2)]
                sg = [psb(f"sg{i}", [128, 512], F32) for i in range(2)]
                ps_s = [pps(f"ps_s{i}", [128, 512], F32) for i in range(2)]
                psTRf = [pps(f"psTRf{i}", [128, 512], F32) for i in range(2)]

                sWO = sem("sWO")
                sCT = sem("sCT")
                sXD = sem("sXD")
                sFL = [sem(f"sFL{i}") for i in range(4)]
                sOS = sem("sOS")
                pd = Ph(nc)
                pd.do("scalar", lambda e: e.preload_act_table(AF.Silu))
                pd.wait("tensor", sGM, consts_done)
                for dc in range(NDC):
                    pd.do("gpsimd", lambda e, o=wo_sb[:, dc, :], i=wo_h[dc]:
                          e.dma_start(out=o, in_=i), sWO, 16)
                pd.wait("gpsimd", sC, 1)
                pd.wait("tensor", sWO, sWO.n)
                free = {}
                for tt in range(4):
                    cb_ = tt % 2
                    if ("ctx", cb_) in free:
                        pd.wait("gpsimd", *free[("ctx", cb_)])
                    cin = ctxg.ap().rearrange("(a p) t -> p a t", p=128)[
                        :, :, bass.ds(voff["v"], 2048)][
                        :, :, tt * 512:(tt + 1) * 512]
                    cl = pd.do("gpsimd", lambda e, o=ctx_tl[cb_][:, :, :],
                               i=cin: e.dma_start(out=o, in_=i), sCT, 16)
                    if "x" in free:
                        pd.wait("gpsimd", *free["x"])
                    xin = x_h.ap()[bass.ds(voff["v"], 2048), :].rearrange(
                        "(a p t) d -> t a p d", p=4, t=128)
                    xl = pd.do("gpsimd", lambda e, o=x_tl[:, :, :],
                               i=xin[:, tt, :, :]:
                               e.dma_start(out=o, in_=i), sXD, 16)
                    pd.wait("tensor", sCT, cl)
                    ocp = []
                    tail = []
                    for mc in range(NDC + 1):
                        if mc < NDC:
                            pb2 = mc % 2
                            fb4 = mc % 4
                            if ("f", fb4) in free:
                                pd.wait("sync", *free[("f", fb4)])
                            fl = pd.do("sync", lambda e, o=f_tl[fb4][:, :],
                                       i=f_s[mc, :, tt * 512:(tt + 1) * 512]:
                                       e.dma_start(out=o, in_=i), sFL[fb4], 16)
                            if ("ps", pb2) in free:
                                pd.wait("tensor", *free[("ps", pb2)])
                            mm = None
                            for kc in range(NDC):
                                mm = pd.do(
                                    "tensor", lambda e, o=ps_s[pb2][:, :],
                                    l=wo_sb[:, kc, mc * 128:(mc + 1) * 128],
                                    rr=ctx_tl[cb_][:, kc, :], s0=(kc == 0),
                                    s1=(kc == NDC - 1):
                                    e.matmul(o, l, rr, start=s0, stop=s1),
                                    sPE if kc == NDC - 1 else None, 1)
                            pd.wait("scalar", sFL[fb4], fl)
                            if ("gs", pb2) in free:
                                pd.wait("scalar", *free[("gs", pb2)])
                            sl = pd.do("scalar", lambda e, o=gsil[pb2][:, :],
                                       i=f_tl[fb4][:, :]:
                                       e.activation(o, i, AF.Silu), sSC, 1)
                            free[("f", fb4)] = (sSC, sl)
                            pd.wait("vector", sPE, mm)
                            pd.wait("vector", sSC, sl)
                            if ("sg", pb2) in free:
                                pd.wait("vector", *free[("sg", pb2)])
                            sv = pd.do("vector", lambda e, o=sg[pb2][:, :],
                                       i=ps_s[pb2][:, :],
                                       bi=bo_sb[:, mc:mc + 1],
                                       g2=gsil[pb2][:, :]:
                                       e.scalar_tensor_tensor(
                                           o, i, bi, g2, op0=ALU.add,
                                           op1=ALU.mult), sVE, 1)
                            free[("ps", pb2)] = (sVE, sv)
                            free[("gs", pb2)] = (sVE, sv)
                            tail.append((mc, pb2, sv))
                        if tail and (len(tail) > 1 or mc == NDC):
                            mcp, pbp, svp = tail.pop(0)
                            pd.wait("tensor", sVE, svp)
                            if ("tr", pbp) in free:
                                pd.wait("tensor", *free[("tr", pbp)])
                            last = None
                            for s_ in range(4):
                                last = pd.do(
                                    "tensor", lambda e,
                                    o=psTRf[pbp][:, s_ * 128:(s_ + 1) * 128],
                                    i=sg[pbp][:, s_ * 128:(s_ + 1) * 128]:
                                    e.transpose(o, i, id_f32[:, :]),
                                    sPE if s_ == 3 else None, 1)
                            free[("sg", pbp)] = (sPE, last)
                            pd.wait("vector", sPE, last)
                            if mcp < 2 and "out" in free:
                                pd.wait("vector", *free["out"])
                            ov = out_sb[:, :, mcp * 128:(mcp + 1) * 128]
                            iv = psTRf[pbp].ap().rearrange(
                                "p (s d) -> p s d", s=4)
                            cp = pd.do("vector", lambda e, o=ov, i=iv:
                                       e.tensor_copy(o, i), sVE, 1)
                            free[("tr", pbp)] = (sVE, cp)
                            ocp.append(cp)
                    pd.wait("vector", sXD, xl)
                    av = pd.do("vector", lambda e, o=out_sb[:, :, :],
                               a=out_sb[:, :, :], b=x_tl[:, :, :]:
                               e.tensor_add(o, a, b), sVE, 1)
                    free["x"] = (sVE, av)
                    pd.wait("sync", sVE, av)
                    odst = out_h[tt * 512:(tt + 1) * 512, :].rearrange(
                        "(p t) d -> t p d", t=128)
                    ost = pd.do("sync", lambda e, o=odst, i=out_sb[:, :, :]:
                                e.dma_start(out=o, in_=i), sOS, 16)
                    free["out"] = (sOS, ost)
                    free[("ctx", cb_)] = (sPE, sPE.n)
                pd.wait("sync", sOS, sOS.n)
                for s_e in sFL:
                    pd.wait("sync", s_e, s_e.n)
                for s_e in (sWO, sCT, sXD):
                    pd.wait("gpsimd", s_e, s_e.n)
                pd.barrier(sB)
                pd.run()
        else:
            pz = Ph(nc)
            pz.wait("sync", sQ, sQ.n)
            pz.barrier(sB)
            pz.run()

    return nc


def prep_inputs(c, inputs):
    b, r = c // 2, c % 2
    hs = slice(1024 * r, 1024 * r + 1024)
    Wq = inputs["L0_attn_Wq"]
    Wk = inputs["L0_attn_Wk"]
    Wv = inputs["L0_attn_Wv"]
    w_cat = np.concatenate([Wq[hs], Wk[hs], Wv[hs]], 0)
    b_cat = np.concatenate([inputs["L0_attn_bq"][hs], inputs["L0_attn_bk"][hs],
                            inputs["L0_attn_bv"][hs]])
    gate = inputs["L0_ffn_gate_w"]
    up = inputs["L0_ffn_up_w"]
    down = inputs["L0_ffn_down_w"]
    slopes = inputs["L0_attn_alibi_slopes"][8 * r:8 * r + 8]
    return {
        "x": np.ascontiguousarray(inputs["L0_attn_hidden_states"][b]),
        "wqkvT": np.ascontiguousarray(w_cat.T.reshape(NDC, 128, 3072)),
        "bqkv": np.ascontiguousarray(b_cat.reshape(24, 128).T),
        "woT": np.ascontiguousarray(
            inputs["L0_attn_Wo"].T.reshape(NDC, 128, D)),
        "bo": np.ascontiguousarray(inputs["L0_attn_bo"].reshape(NDC, 128).T),
        "gateT": np.ascontiguousarray(
            gate.T.reshape(NDC, 128, NIC, 128).transpose(2, 0, 1, 3)),
        "upT": np.ascontiguousarray(
            up.T.reshape(NDC, 128, NIC, 128).transpose(2, 0, 1, 3)),
        "downT": np.ascontiguousarray(down.T.reshape(NIC, 128, D)),
        "gsc": np.ascontiguousarray(
            inputs["L0_ffn_gate_s"].reshape(NIC, 128).T),
        "usc": np.ascontiguousarray(
            inputs["L0_ffn_up_s"].reshape(NIC, 128).T),
        "dsc": np.ascontiguousarray(
            inputs["L0_ffn_down_s"].reshape(NDC, 128).T),
        "slopes2": np.ascontiguousarray(
            np.stack([slopes, slopes], 0)),
    }


def kernel(**inputs):
    nc = build_nc()
    in_maps = [prep_inputs(c, inputs) for c in range(8)]
    res = run_bass_kernel_spmd(nc, in_maps, core_ids=list(range(8)))
    out = np.empty((B, S, D), dtype=np.float32)
    for c in range(8):
        b, r = c // 2, c % 2
        out[b, 2048 * r:2048 * r + 2048, :] = res.results[c]["out"]
    return out
